# version 03
# baseline (speedup 1.0000x reference)
from contextlib import ExitStack

import numpy as np
import ml_dtypes
import concourse.bass as bass
import concourse.mybir as mybir
from concourse.bass_utils import run_bass_kernel_spmd

F32 = mybir.dt.float32
BF16 = mybir.dt.bfloat16
AF = mybir.ActivationFunctionType
ALU = mybir.AluOpType
AX = mybir.AxisListType.X

D = 4096
T = 1024
NK = D // 128
FF = 16384
NFC = FF // 128
EPS = 1e-5
P0_NGU = 2 * NFC // 8
P0_ND = 4 * NFC // 8
BF = ml_dtypes.bfloat16
ENGS = ("sync", "tensor", "vector", "scalar", "gpsimd")



def _tile_big(w):
    wT = np.ascontiguousarray(w.T)
    t = wT.reshape(NK, 128, NFC, 128)
    return np.ascontiguousarray(t.transpose(2, 1, 0, 3).reshape(NFC, 128, NK * 128))


def _tile_down(w):
    wT = np.ascontiguousarray(w.T)
    t = wT.reshape(NFC, 128, 4, 8, 128)
    return np.ascontiguousarray(t.transpose(2, 0, 1, 3, 4).reshape(4, NFC, 128, 1024))


def _pcol(v, ncol):
    return np.ascontiguousarray(np.asarray(v, np.float32).reshape(ncol, 128).T)


def _pcol2(v):
    v = np.asarray(v, np.float32)
    L = v.shape[0]
    return np.ascontiguousarray(v.reshape(L, 4, 128).transpose(2, 0, 1).reshape(128, L * 4))


def prep_host(inputs):
    f = np.float32
    sh = {}
    sh["wg"] = _tile_big(np.asarray(inputs["L0_ffn_gate_w"], f))
    sh["wu"] = _tile_big(np.asarray(inputs["L0_ffn_up_w"], f))
    sh["wd"] = _tile_down(np.asarray(inputs["L0_ffn_down_w"], f))
    sh["gsv"] = _pcol(inputs["L0_ffn_gate_scale"].ravel(), NFC)
    sh["usv"] = _pcol(inputs["L0_ffn_up_scale"].ravel(), NFC)
    sh["dsv"] = _pcol(inputs["L0_ffn_down_scale"].ravel(), NK)
    W1 = np.asarray(inputs["L0_attn_W1"], f)
    W2 = np.asarray(inputs["L0_attn_W2"], f)
    sh["winT"] = np.ascontiguousarray(np.asarray(inputs["L0_attn_Win"], f).T)
    sh["w1T"] = np.ascontiguousarray(
        W1.transpose(0, 2, 1).reshape(4, 4, 128, 2048).transpose(0, 2, 1, 3).reshape(4, 128, 8192))
    sh["w2T"] = np.ascontiguousarray(
        W2.transpose(0, 2, 1).reshape(4, 16, 128, 512).transpose(0, 2, 1, 3).reshape(4, 128, 8192))
    sh["woutT"] = np.ascontiguousarray(
        np.asarray(inputs["L0_attn_Wout"], f).T.reshape(4, 128, 4096).transpose(1, 0, 2).reshape(128, 16384))
    sh["bin2"] = _pcol(inputs["L0_attn_bin"], 4)
    for nm, key in [("ln1w", "L0_attn_ln1_w"), ("ln1b", "L0_attn_ln1_b"),
                    ("pzre", "L0_attn_phz_re"), ("pzim", "L0_attn_phz_im"),
                    ("lcre", "L0_attn_lc_re"), ("lcim", "L0_attn_lc_im"),
                    ("sclv", "L0_attn_sc_lin"), ("ln2w", "L0_attn_ln2_w"),
                    ("ln2b", "L0_attn_ln2_b"), ("b2v", "L0_attn_b2")]:
        sh[nm] = _pcol2(inputs[key])
    sh["b1v"] = np.ascontiguousarray(
        np.asarray(inputs["L0_attn_b1"], f).reshape(4, 16, 128).transpose(2, 0, 1).reshape(128, 64))
    sh["lnlw"] = _pcol(inputs["L0_attn_lnl_w"], 4)
    sh["lnlb"] = _pcol(inputs["L0_attn_lnl_b"], 4)
    sh["boutv"] = _pcol(inputs["L0_attn_bout"], NK)
    sh["rwT"] = np.ascontiguousarray(np.asarray(inputs["L1_ffn_router_weight"], f).T)
    sh["ebw"] = np.asarray(inputs["L1_ffn_expert_bias"], f)
    sh["rbv"] = np.ascontiguousarray(np.asarray(inputs["L1_ffn_router_bias"], f).reshape(64, 1))
    sh["bvq"] = np.asarray(inputs["L1_attn_bv_q"], f).reshape(1, 4096).copy()
    sh["bvqp"] = np.ascontiguousarray(
        np.asarray(inputs["L1_attn_bv_q"], f).reshape(4, 8, 128).transpose(2, 0, 1).reshape(128, 32))
    sh["bvkp"] = np.ascontiguousarray(
        np.asarray(inputs["L1_attn_bv_k"], f).reshape(4, 8, 128).transpose(2, 0, 1).reshape(128, 32))
    sh["bvk"] = np.asarray(inputs["L1_attn_bv_k"], f).reshape(1, 4096).copy()
    sh["bvv"] = np.asarray(inputs["L1_attn_bv_v"], f).reshape(1, 4096).copy()
    sh["ident"] = np.eye(128, dtype=BF)
    wgu = np.concatenate([sh["wg"], sh["wu"]], axis=0)
    wdf = sh["wd"].reshape(512, 128, 1024)
    x_full = np.asarray(inputs["L0_attn_x"], f)
    per_core = []
    for c in range(8):
        b, j = c // 2, c % 2
        d = {"x": np.ascontiguousarray(x_full[b, j * T:(j + 1) * T, :].T)}
        hgu = P0_NGU // 2
        d["wslgu"] = np.ascontiguousarray(np.concatenate(
            [sh["wg"][c * hgu:(c + 1) * hgu], sh["wu"][c * hgu:(c + 1) * hgu]], axis=0))
        d["wsld"] = np.ascontiguousarray(wdf[c * P0_ND:(c + 1) * P0_ND])
        s_idx = np.arange(2048)[:, None]
        t_idx = np.arange(T)[None, :] + j * T
        d["mask"] = (s_idx <= t_idx).astype(BF)
        sel = np.zeros((128, 3), f)
        sel[:, 0] = 1.0 if j == 0 else 0.0
        sel[:, 1] = 0.0 if j == 0 else 1.0
        d["selv"] = sel
        per_core.append(d)
    return sh, per_core



class Eng:
    pass


class Ctx:
    def __init__(self, nc, stack):
        self.nc = nc
        self.stack = stack
        self.t = {}
        self.sems = {}
        self.fence = []
        self.e = Eng()
        self.ecnt = {}

    def sem(self, name):
        if name not in self.sems:
            self.sems[name] = self.stack.enter_context(self.nc.semaphore(name))
        return self.sems[name]

    def sb(self, stack, name, shape, dtype):
        h = stack.enter_context(self.nc.sbuf_tensor(name, shape, dtype))
        self.t[name] = h
        return h

    def ps(self, stack, name, shape, dtype=F32):
        h = stack.enter_context(self.nc.psum_tensor(name, shape, dtype))
        self.t[name] = h
        return h

    def emit_fence(self, eng_obj):
        pass

    def edge(self, eng_obj, inst, name="veg"):
        sem = self.sem(name)
        key = (name, id(eng_obj.__class__))
        self.ecnt[name] = self.ecnt.get(name, 0) + 1
        inst.then_inc(sem, 1)
        eng_obj.wait_ge(sem, self.ecnt[name])



def emit_ln(ctx, q, eng, x_src, xn):
    e = ctx.e
    lx = [ctx.sem(q + f"lx{i}") for i in range(2)]
    lx2 = [ctx.sem(q + f"ly{i}") for i in range(2)]
    cv, cs, sm = ctx.sem(q + "cv"), ctx.sem(q + "cs"), ctx.sem(q + "sm")
    vf, sf, bm, bc = ctx.sem(q + "vf"), ctx.sem(q + "sf"), ctx.sem(q + "bm"), ctx.sem(q + "bc")
    xnv = ctx.sem(q + "xnv")
    xld = ctx.t["lnxld"]; xb = ctx.t["lnxb"]; sq = ctx.t["lnsqb"]
    tmpf = ctx.t["lntmpf"]; rows = ctx.t["lnrows"]
    rbc = ctx.t["lnrbc"]; mrbc = ctx.t["lnmrbc"]
    psS, psQ = ctx.t["lnpsS"], ctx.t["lnpsQ"]
    psB1, psB2 = ctx.t["lnpsB1"], ctx.t["lnpsB2"]
    onesb, onesf = ctx.t["onesb"], ctx.t["onesf"]
    inv_n = 1.0 / D
    R = lambda i: rows[0:1, i * T:(i + 1) * T]

    if eng == "sync":
        sy = e.sync
        for k in range(NK):
            sl = k % 2
            if k >= 2:
                sy.wait_ge(sm, k - 1)
            sy.dma_start(out=xld[:, sl * T:(sl + 1) * T],
                         in_=x_src[k * 128:(k + 1) * 128, :]).then_inc(lx[sl], 16)
        for k in range(NK):
            sl = k % 2
            if k >= 2:
                sy.wait_ge(xnv, k - 1)
            else:
                sy.wait_ge(bc, 2)
            sy.dma_start(out=xld[:, sl * T:(sl + 1) * T],
                         in_=x_src[k * 128:(k + 1) * 128, :]).then_inc(lx2[sl], 16)
    elif eng == "vector":
        v = e.vector
        for k in range(NK):
            sl = k % 2
            v.wait_ge(lx[sl], 16 * (k // 2 + 1))
            if k >= 2:
                v.wait_ge(sm, k - 1)
            v.tensor_copy(xb[:, sl * T:(sl + 1) * T],
                          xld[:, sl * T:(sl + 1) * T]).then_inc(cv, 1)
        v.wait_ge(sm, NK)
        v.tensor_scalar_mul(R(0), psS[0:1, :], inv_n)
        v.tensor_scalar_mul(R(1), psQ[0:1, :], inv_n)
        ctx.edge(v, v.tensor_tensor(R(2), R(0), R(0), ALU.mult), q + "ve")
        v.tensor_tensor(R(1), R(1), R(2), ALU.subtract).then_inc(vf, 1)
        v.wait_ge(sf, 1)
        ctx.edge(v, v.reciprocal(R(2), R(1)), q + "ve")
        v.tensor_tensor(R(3), R(0), R(2), ALU.mult).then_inc(vf, 1)
        v.wait_ge(bm, 1)
        v.tensor_copy(rbc[:], psB1[:])
        v.tensor_copy(mrbc[:], psB2[:]).then_inc(bc, 2)
        for k in range(NK):
            sl = k % 2
            v.wait_ge(lx2[sl], 16 * (k // 2 + 1))
            v.tensor_tensor(tmpf[:, sl * T:(sl + 1) * T],
                            xld[:, sl * T:(sl + 1) * T], rbc[:], ALU.mult)
            v.tensor_tensor(xn[:, k * T:(k + 1) * T],
                            tmpf[:, sl * T:(sl + 1) * T], mrbc[:],
                            ALU.subtract).then_inc(xnv, 1)
    elif eng == "scalar":
        sc = e.scalar
        for k in range(NK):
            sl = k % 2
            sc.wait_ge(lx[sl], 16 * (k // 2 + 1))
            if k >= 2:
                sc.wait_ge(sm, k - 1)
            sc.activation(sq[:, sl * T:(sl + 1) * T],
                          xld[:, sl * T:(sl + 1) * T], AF.Square).then_inc(cs, 1)
        sc.wait_ge(vf, 1)
        sc.activation(R(1), R(1), AF.Sqrt, bias=ctx.t["epsv"][0:1, 0:1]).then_inc(sf, 1)
    elif eng == "tensor":
        te = e.tensor
        for k in range(NK):
            sl = k % 2
            te.wait_ge(cv, k + 1)
            te.wait_ge(cs, k + 1)
            for h in range(2):
                te.matmul(psS[0:1, h * 512:(h + 1) * 512], onesb[:, 0:1],
                          xb[:, sl * T + h * 512:sl * T + (h + 1) * 512],
                          start=(k == 0), stop=(k == NK - 1))
            for h in range(2):
                last = te.matmul(psQ[0:1, h * 512:(h + 1) * 512], onesb[:, 0:1],
                                 sq[:, sl * T + h * 512:sl * T + (h + 1) * 512],
                                 start=(k == 0), stop=(k == NK - 1))
            last.then_inc(sm, 1)
        te.wait_ge(vf, 2)
        for h in range(2):
            te.matmul(psB1[:, h * 512:(h + 1) * 512], onesf[0:1, :],
                      rows[0:1, 2 * T + h * 512:2 * T + (h + 1) * 512], start=True, stop=True)
        for h in range(2):
            last = te.matmul(psB2[:, h * 512:(h + 1) * 512], onesf[0:1, :],
                             rows[0:1, 3 * T + h * 512:3 * T + (h + 1) * 512], start=True, stop=True)
        last.then_inc(bm, 1)
    return None


def emit_p0(ctx, eng):
    e = ctx.e
    lw = [ctx.sem(f"p0lw{i}") for i in range(2)]
    rd = ctx.sem("p0rd")
    pr = ctx.sem("p0pr")
    mmd = ctx.sem("p0mm")
    ccd = ctx.sem("p0cc")
    thv = ctx.sem("p0thv")
    wslgu = ctx.t["d_wslgu"]; wsld = ctx.t["d_wsld"]
    wbuf = ctx.t["p0wbuf"]; acc = ctx.t["p0acc"]; accw = ctx.t["p0accw"]
    row = ctx.t["p0row"]
    ps = ctx.t["p0ps"]; thr = ctx.t["thr"]
    onesf = ctx.t["onesf"]
    cin, cout = ctx.t["d_thr_in"], ctx.t["d_thr_out"]
    NW = float(FF * D)

    chunks = [(0 if i < P0_NGU // 2 else 1, wslgu[i][:, hf * 2048:(hf + 1) * 2048], 2048)
              for i in range(P0_NGU) for hf in range(2)] + \
             [(2, wsld[i], 1024) for i in range(P0_ND)]

    if eng == "sync":
        sy = e.sync
        for i, (_, ap, w) in enumerate(chunks):
            s = i % 2
            if i >= 2:
                sy.wait_ge(rd, i - 1)
            sy.dma_start(out=wbuf[:, s * 2048:s * 2048 + w], in_=ap).then_inc(lw[s], 16)
    elif eng == "vector":
        v = e.vector
        rz = ctx.sem("p0rz")
        v.memset(row[:], 0.0).then_inc(rz, 1)
        bounds = [0]
        last = None
        for i, (col, _, w) in enumerate(chunks):
            s = i % 2
            if col >= len(bounds):
                bounds.append(i)
            v.wait_ge(lw[s], 16 * (i // 2 + 1))
            last = v.reduce_sum(accw[:, i:i + 1], wbuf[:, s * 2048:s * 2048 + w],
                                axis=AX, apply_absolute_value=True)
            if i < len(chunks) - 1:
                last.then_inc(rd, 1)
        bounds.append(len(chunks))
        ctx.edge(v, last)
        fin = ctx.sem("p0fin")
        for col in range(3):
            v.reduce_sum(acc[:, col:col + 1],
                         accw[:, bounds[col]:bounds[col + 1]], axis=AX).then_inc(fin, 1)
        v.wait_ge(mmd, 1)
        v.wait_ge(rz, 1)
        v.tensor_scalar_mul(row[0:1, 0:3], ps[0:1, 0:3], 0.7 / NW).then_inc(pr, 1)
        v.wait_ge(mmd, 2)
        v.tensor_copy(thr[:, 0:4], ps[:, 0:4]).then_inc(thv, 1)
    elif eng == "tensor":
        te = e.tensor
        te.wait_ge(ctx.sem("p0fin"), 3)
        te.matmul(ps[0:1, 0:3], onesf[:, 0:1], acc[:, 0:3], start=True, stop=True).then_inc(mmd, 1)
        te.wait_ge(ccd, 32)
        te.matmul(ps[:, 0:8], onesf[0:1, :], row[0:1, 0:8],
                  start=True, stop=True).then_inc(mmd, 1)
    elif eng == "gpsimd":
        g = e.gpsimd
        ccv = ctx.sem("p0ccv")
        g.wait_ge(pr, 1)
        g.dma_start(out=cin, in_=row[0:1, :]).then_inc(ccd, 16)
        g.wait_ge(ccd, 16)
        g.collective_compute("AllReduce", ALU.add,
                             replica_groups=[list(range(8))],
                             ins=[cin], outs=[cout]).then_inc(ccv, 1)
        g.wait_ge(ccv, 1)
        g.dma_start(out=row[0:1, :], in_=cout).then_inc(ccd, 16)
    return {"done": [(thv, 1)]}



def emit_ffn_s1(ctx, ph, eng, quantize, save):
    e = ctx.e
    q = "s1"
    lg = [ctx.sem(q + f"lg{i}") for i in range(4)]
    lu = [ctx.sem(q + f"lu{i}") for i in range(4)]
    qd = ctx.sem(q + "qd")
    sq1 = ctx.sem(q + "sq1")
    mm = ctx.sem(q + "mm")
    ev = ctx.sem(q + "ev")
    hv = ctx.sem(q + "hv")
    sgq = [ctx.sem(q + f"sgq{i}") for i in range(2)]
    squ = [ctx.sem(q + f"squ{i}") for i in range(2)]
    sh = [ctx.sem(q + f"sh{i}") for i in range(2)]
    xnv = ctx.sem("lnxnv")
    thv = ctx.sem("p0thv")

    xn = ctx.t["xn"]
    wg, wu = ctx.t.get("d_wg"), ctx.t.get("d_wu")
    wqg, wqu = ctx.t["d_wqg"], ctx.t["d_wqu"]
    h_state = ctx.t["d_h"]
    gq, uq = ctx.t["gq"], ctx.t["uq"]
    sgs, ups, hbf = ctx.t["sgs"], ctx.t["ups"], ctx.t["hbf"]
    psG = [ctx.t["psG0"], ctx.t["psG1"]]
    psU = [ctx.t["psU0"], ctx.t["psU1"]]
    gsv, usv = ctx.t["gsv_t"], ctx.t["usv_t"]
    thr = ctx.t["thr"]
    if quantize:
        wst = ctx.t["wst"]; absf = ctx.t["absf"]
        sgb = ctx.t["sgb"]; mbb = ctx.t["mbb"]

    def unit(fc, u):
        return 4 * fc + u

    if eng == "sync":
        sy = e.sync
        sy.wait_ge(xnv, NK)
        if quantize:
            sy.wait_ge(thv, 1)
        for fc in range(NFC):
            for u in range(4):
                mat = 0 if u < 2 else 1
                hf = u % 2
                if quantize:
                    slot = unit(fc, u) % 4
                    if fc >= 1:
                        sy.wait_ge(qd, unit(fc, u) - 3)
                    src = (wg if mat == 0 else wu)[fc][:, hf * 2048:(hf + 1) * 2048]
                    sem = lg[slot]
                    sy.dma_start(out=wst[:, slot * 2048:(slot + 1) * 2048],
                                 in_=src).then_inc(sem, 16)
                else:
                    if u % 2 == 1:
                        continue
                    s2 = fc % 2
                    if fc >= 2:
                        sy.wait_ge(mm, fc - 1)
                    if mat == 0:
                        sy.dma_start(out=gq[:, s2 * 4096:(s2 + 1) * 4096],
                                     in_=wqg[fc]).then_inc(lg[s2], 16)
                    else:
                        sy.dma_start(out=uq[:, s2 * 4096:(s2 + 1) * 4096],
                                     in_=wqu[fc]).then_inc(lu[s2], 16)
    elif eng == "scalar":
        sc = e.scalar
        ctx.emit_fence(sc)
        for fc in range(NFC):
            if quantize:
                for u in range(4):
                    slot = unit(fc, u) % 4
                    sc.wait_ge(lg[slot], 16 * (unit(fc, u) // 4 + 1))
                    sc.wait_ge(qd, unit(fc, u))
                    sc.activation(absf[:], wst[:, slot * 2048:(slot + 1) * 2048], AF.Abs)
                    sc.activation(sgb[:], wst[:, slot * 2048:(slot + 1) * 2048],
                                  AF.Sign).then_inc(sq1, 1)
            sc.wait_ge(mm, fc + 1)
            if fc >= 2:
                sc.wait_ge(hv, fc - 1)
            s = fc % 2
            sc.activation(sgs[:, s * T:(s + 1) * T], psG[s][:], AF.Silu,
                          scale=gsv[:, fc:fc + 1])
            sc.activation(ups[:, s * T:(s + 1) * T], psU[s][:], AF.Identity,
                          scale=usv[:, fc:fc + 1]).then_inc(ev, 2)
    elif eng == "vector":
        v = e.vector
        ctx.emit_fence(v)
        if quantize:
            v.wait_ge(thv, 1)
        for fc in range(NFC):
            s = fc % 2
            if quantize:
                for u in range(4):
                    mat = 0 if u < 2 else 1
                    hf = u % 2
                    v.wait_ge(sq1, unit(fc, u) + 1)
                    if fc >= 2:
                        v.wait_ge(mm, fc - 1)
                        v.wait_ge((sgq if mat == 0 else squ)[s], 16 * (fc // 2) if save else 0)
                    v.tensor_single_scalar(mbb[:], absf[:], thr[:, mat:mat + 1], ALU.is_gt)
                    dst = (gq if mat == 0 else uq)
                    v.tensor_tensor(dst[:, s * 4096 + hf * 2048:s * 4096 + (hf + 1) * 2048],
                                    mbb[:], sgb[:], ALU.mult).then_inc(qd, 1)
            v.wait_ge(ev, 2 * fc + 2)
            if fc >= 2:
                v.wait_ge(sh[s], 16 * (fc // 2))
            v.tensor_tensor(hbf[:, s * T:(s + 1) * T], sgs[:, s * T:(s + 1) * T],
                            ups[:, s * T:(s + 1) * T], ALU.mult).then_inc(hv, 1)
    elif eng == "tensor":
        te = e.tensor
        ctx.emit_fence(te)
        te.wait_ge(xnv, NK)
        for fc in range(NFC):
            s = fc % 2
            if quantize:
                te.wait_ge(qd, unit(fc, 2))
            else:
                te.wait_ge(lg[s], 16 * (fc // 2 + 1))
            if fc >= 2:
                te.wait_ge(ev, 2 * (fc - 2) + 2)
            for k in range(NK):
                lhs = gq[:, s * 4096 + k * 128:s * 4096 + (k + 1) * 128]
                for h in range(2):
                    te.matmul(psG[s][:, h * 512:(h + 1) * 512], lhs,
                              xn[:, k * T + h * 512:k * T + (h + 1) * 512],
                              start=(k == 0), stop=(k == NK - 1))
            if quantize:
                te.wait_ge(qd, unit(fc, 4 - 1) + 1)
            else:
                te.wait_ge(lu[s], 16 * (fc // 2 + 1))
            for k in range(NK):
                lhs = uq[:, s * 4096 + k * 128:s * 4096 + (k + 1) * 128]
                for h in range(2):
                    last = te.matmul(psU[s][:, h * 512:(h + 1) * 512], lhs,
                                     xn[:, k * T + h * 512:k * T + (h + 1) * 512],
                                     start=(k == 0), stop=(k == NK - 1))
            last.then_inc(mm, 1)
    elif eng == "gpsimd":
        g = e.gpsimd
        ctx.emit_fence(g)
        for fc in range(NFC):
            s = fc % 2
            if quantize and save:
                g.wait_ge(qd, unit(fc, 1) + 1)
                g.dma_start(out=wqg[fc], in_=gq[:, s * 4096:(s + 1) * 4096]).then_inc(sgq[s], 16)
                g.wait_ge(qd, unit(fc, 3) + 1)
                g.dma_start(out=wqu[fc], in_=uq[:, s * 4096:(s + 1) * 4096]).then_inc(squ[s], 16)
            g.wait_ge(hv, fc + 1)
            g.dma_start(out=h_state[fc], in_=hbf[:, s * T:(s + 1) * T]).then_inc(sh[s], 16)
    return {}


def emit_ffn_s2(ctx, ph, eng, quantize, save, x_in, x_out):
    e = ctx.e
    q = "s2"
    lh = [ctx.sem(q + f"lh{i}") for i in range(2)]
    ldw = [ctx.sem(q + f"ld{i}") for i in range(2)]
    qd2 = ctx.sem(q + "qd")
    sq2 = ctx.sem(q + "sq2")
    m2 = ctx.sem(q + "m2")
    e2 = ctx.sem(q + "e2")
    xov = ctx.sem(q + "xov")
    lx2 = ctx.sem(q + "lx2")
    so = [ctx.sem(q + f"so{i}") for i in range(2)]
    sdq = [ctx.sem(q + f"sdq{i}") for i in range(2)]
    sh1 = [ctx.sem(f"s1sh{i}") for i in range(2)]

    h_state = ctx.t["d_h"]
    wd, wqd = ctx.t.get("d_wd"), ctx.t["d_wqd"]
    hb2 = ctx.t["hb2"]; dq = ctx.t["dq"]
    xin2 = ctx.t["xin2"]; tmp2 = ctx.t["tmp2"]; xo = ctx.t["xo"]
    psD = [ctx.t[f"psD{kg}"] for kg in range(8)]
    dsh = ctx.t["dsh"]
    if quantize:
        dwf = ctx.t["dwf"]; absd = ctx.t["absd"]
        sgd = ctx.t["sgd"]; mbd = ctx.t["mbd"]
    thr = ctx.t["thr"]

    def quant_here(qi):
        return quantize and qi % 2 == 0

    if eng == "sync":
        sy = e.sync
        sy.wait_ge(ctx.sem("s1mm"), NFC)
        sy.wait_ge(ctx.sem("s1ev"), 2 * NFC)
        sy.wait_ge(ctx.sem("s1hv"), NFC)
        for i_ in range(2):
            sy.wait_ge(sh1[i_], 16 * (NFC // 2))
            if quantize and save:
                sy.wait_ge(ctx.sem(f"s1sgq{i_}"), 16 * (NFC // 2))
                sy.wait_ge(ctx.sem(f"s1squ{i_}"), 16 * (NFC // 2))
        for qi in range(8):
            g_, th = qi // 2, qi % 2
            sy.wait_ge(xov, 8 * qi)
            for kg in range(8):
                dk = g_ * 8 + kg
                sy.dma_start(out=xin2[:, kg * 512:(kg + 1) * 512],
                             in_=x_in[dk * 128:(dk + 1) * 128, th * 512:(th + 1) * 512]
                             ).then_inc(lx2, 16)
            for fc in range(NFC):
                it = qi * NFC + fc
                s = fc % 2
                if it >= 2:
                    sy.wait_ge(m2, it - 1)
                if qi == 0:
                    sy.wait_ge(sh1[fc % 2], 16 * (fc // 2 + 1))
                sy.dma_start(out=hb2[:, s * 512:(s + 1) * 512],
                             in_=h_state[fc][:, th * 512:(th + 1) * 512]).then_inc(lh[s], 16)
                if quant_here(qi):
                    nq = g_ * NFC + fc
                    if nq >= 2:
                        sy.wait_ge(qd2, nq - 1)
                    sy.dma_start(out=dwf[:, s * 1024:(s + 1) * 1024],
                                 in_=wd[g_][fc]).then_inc(ldw[s], 16)
                else:
                    if quantize:
                        sy.wait_ge(sdq[s], 16 * ((g_ * NFC + fc) // 2 + 1))
                    sy.dma_start(out=dq[:, s * 1024:(s + 1) * 1024],
                                 in_=wqd[g_][fc]).then_inc(ldw[s], 16)
    elif eng == "scalar":
        sc = e.scalar
        for qi in range(8):
            g_, th = qi // 2, qi % 2
            if quant_here(qi):
                for fc in range(NFC):
                    nq = g_ * NFC + fc
                    s = fc % 2
                    it = qi * NFC + fc
                    sc.wait_ge(ldw[s], 16 * (it // 2 + 1))
                    sc.wait_ge(qd2, nq)
                    sc.activation(absd[:], dwf[:, s * 1024:(s + 1) * 1024], AF.Abs)
                    sc.activation(sgd[:], dwf[:, s * 1024:(s + 1) * 1024],
                                  AF.Sign).then_inc(sq2, 1)
            for kg in range(8):
                u = qi * 8 + kg
                sc.wait_ge(m2, NFC * (qi + 1))
                if u >= 2:
                    sc.wait_ge(xov, u - 1)
                sc.activation(tmp2[:, (u % 2) * 512:(u % 2 + 1) * 512], psD[kg][:],
                              AF.Identity, scale=dsh[:, g_ * 8 + kg:g_ * 8 + kg + 1]).then_inc(e2, 1)
    elif eng == "vector":
        v = e.vector
        for qi in range(8):
            g_, th = qi // 2, qi % 2
            if quant_here(qi):
                for fc in range(NFC):
                    nq = g_ * NFC + fc
                    s = fc % 2
                    it = qi * NFC + fc
                    v.wait_ge(sq2, nq + 1)
                    if it >= 2:
                        v.wait_ge(m2, it - 1)
                    if save and nq >= 2:
                        v.wait_ge(sdq[s], 16 * (nq // 2))
                    v.tensor_single_scalar(mbd[:], absd[:], thr[:, 2:3], ALU.is_gt)
                    v.tensor_tensor(dq[:, s * 1024:(s + 1) * 1024], mbd[:], sgd[:],
                                    ALU.mult).then_inc(qd2, 1)
            for kg in range(8):
                u = qi * 8 + kg
                v.wait_ge(e2, u + 1)
                v.wait_ge(lx2, 16 * 8 * (qi + 1))
                if u >= 2:
                    v.wait_ge(so[u % 2], 16 * (u // 2))
                v.tensor_tensor(xo[:, (u % 2) * 512:(u % 2 + 1) * 512],
                                xin2[:, kg * 512:(kg + 1) * 512],
                                tmp2[:, (u % 2) * 512:(u % 2 + 1) * 512],
                                ALU.add).then_inc(xov, 1)
    elif eng == "tensor":
        te = e.tensor
        for qi in range(8):
            g_, th = qi // 2, qi % 2
            for fc in range(NFC):
                it = qi * NFC + fc
                s = fc % 2
                if quant_here(qi):
                    te.wait_ge(qd2, g_ * NFC + fc + 1)
                else:
                    te.wait_ge(ldw[s], 16 * (it // 2 + 1))
                te.wait_ge(lh[s], 16 * (it // 2 + 1))
                if fc == 0:
                    te.wait_ge(e2, 8 * qi)
                for kg in range(8):
                    lhs = dq[:, s * 1024 + kg * 128:s * 1024 + (kg + 1) * 128]
                    last = te.matmul(psD[kg][:], lhs, hb2[:, s * 512:(s + 1) * 512],
                                     start=(fc == 0), stop=(fc == NFC - 1))
                last.then_inc(m2, 1)
    elif eng == "gpsimd":
        g = e.gpsimd
        for qi in range(8):
            g_, th = qi // 2, qi % 2
            if quant_here(qi) and save:
                for fc in range(NFC):
                    nq = g_ * NFC + fc
                    s = fc % 2
                    g.wait_ge(qd2, nq + 1)
                    g.dma_start(out=wqd[g_][fc],
                                in_=dq[:, s * 1024:(s + 1) * 1024]).then_inc(sdq[s], 16)
            for kg in range(8):
                u = qi * 8 + kg
                dk = g_ * 8 + kg
                g.wait_ge(xov, u + 1)
                g.dma_start(out=x_out[dk * 128:(dk + 1) * 128, th * 512:(th + 1) * 512],
                            in_=xo[:, (u % 2) * 512:(u % 2 + 1) * 512]).then_inc(so[u % 2], 16)
    return {"done": [(so[0], 16 * 32), (so[1], 16 * 32), (m2, 8 * NFC),
                     (ctx.sem(q + "e2"), 64), (ctx.sem(q + "xov"), 64)]}



def emit_p2(ctx, eng, x_in, x_out):
    e = ctx.e
    E = eng
    xn = ctx.t["xn"]
    win = ctx.t["p2win"]; hI = ctx.t["p2h"]
    psA, psB = ctx.t["p2psA"], ctx.t["p2psB"]
    PS = [psA, psB]
    d_winT = ctx.t["d_winT"]; d_w1T = ctx.t["d_w1T"]; d_w2T = ctx.t["d_w2T"]
    d_woutT = ctx.t["d_woutT"]

    emit_ln(ctx, "ln", eng, x_in, xn)

    wlwin = ctx.sem("p2wlwin")
    winm = ctx.sem("p2winm")
    wine = ctx.sem("p2wine")
    xnv = ctx.sem("lnxnv")

    PS4 = [ctx.t["poolA"], ctx.t["poolB"], ctx.t["poolC"], ctx.t["poolD"]]
    if E == "gpsimd":
        g = e.gpsimd
        for kvh in range(2):
            if kvh == 1:
                g.wait_ge(winm, 1)
            g.dma_start(out=win.rearrange("p (kv di) -> p kv di", kv=16),
                        in_=d_winT.rearrange("(kv p) di -> p kv di", p=128)
                        [:, kvh * 16:(kvh + 1) * 16, :]).then_inc(wlwin, 16)
    if E == "tensor":
        te = e.tensor
        te.wait_ge(xnv, NK)
        for kvh in range(2):
            te.wait_ge(wlwin, 16 * (kvh + 1))
            for kv in range(16):
                kvg = kvh * 16 + kv
                for mdi in range(4):
                    lhs = win[:, kv * 512 + mdi * 128:kv * 512 + (mdi + 1) * 128]
                    for h in range(2):
                        last = te.matmul(PS4[mdi][:, h * 512:(h + 1) * 512], lhs,
                                         xn[:, kvg * T + h * 512:kvg * T + (h + 1) * 512],
                                         start=(kvg == 0), stop=(kvg == NK - 1))
            last.then_inc(winm, 1)
    if E == "scalar":
        sc = e.scalar
        for mdi in range(4):
            sc.wait_ge(winm, 2)
            sc.activation(hI[:, mdi * T:(mdi + 1) * T], PS4[mdi][:], AF.Identity,
                          bias=ctx.t["bin2_t"][:, mdi:mdi + 1]).then_inc(wine, 1)

    hb = ctx.t["p2hb"]; sqh = ctx.t["p2sqh"]; rows = ctx.t["p2rows"]
    rbc = ctx.t["p2rbc"]; mrbc = ctx.t["p2mrbc"]
    yB = ctx.t["p2y"]; cA = ctx.t["p2cA"]; iA = ctx.t["p2iA"]; iB = ctx.t["p2iB"]
    povr = ctx.t["p2povr"]; povi = ctx.t["p2povi"]
    w1 = ctx.t["p2w1"]; w2 = ctx.t["p2w2"]; wout = ctx.t["p2wout"]
    zb = ctx.t["p2z"]; y2 = ctx.t["p2y2"]; tm2 = ctx.t["p2tm"]
    xin3 = ctx.t["p2xin"]; xo3 = ctx.t["p2xo"]
    pw = ctx.t["p2pw"]; scb = ctx.t["p2sc"]
    onesb, onesf = ctx.t["onesb"], ctx.t["onesf"]
    psS2 = ctx.t["p2s_psS"]; psQ2 = ctx.t["p2s_psQ"]
    psB1, psB2 = psA, psB
    d_sc_in, d_sc_out = ctx.t["d_sc_in"], ctx.t["d_sc_out"]
    ln1w, ln1b = ctx.t["ln1w_t"], ctx.t["ln1b_t"]
    ln2w, ln2b = ctx.t["ln2w_t"], ctx.t["ln2b_t"]
    pzre, pzim = ctx.t["pzre_t"], ctx.t["pzim_t"]
    lcre, lcim = ctx.t["lcre_t"], ctx.t["lcim_t"]
    sclv = ctx.t["sclv_t"]; b1v = ctx.t["b1v_t"]; b2v = ctx.t["b2v_t"]
    lnlw, lnlb = ctx.t["lnlw_t"], ctx.t["lnlb_t"]
    boutv = ctx.t["boutv_t"]; selv = ctx.t["selv_t"]

    hbv = ctx.sem("p2hbv"); hbs = ctx.sem("p2hbs"); ism = ctx.sem("p2ism")
    ivf = ctx.sem("p2ivf"); isf = ctx.sem("p2isf"); ibm = ctx.sem("p2ibm")
    itv = ctx.sem("p2itv"); ys = ctx.sem("p2ys")
    rq = ctx.sem("p2rq"); rs = ctx.sem("p2rs")
    scs = ctx.sem("p2scs"); gcc = ctx.sem("p2gcc"); gcv = ctx.sem("p2gcv")
    wl1 = ctx.sem("p2wl1"); wl2 = ctx.sem("p2wl2"); wlo = ctx.sem("p2wlo")
    ifm = ctx.sem("p2ifm"); ife = ctx.sem("p2ife"); ihv = ctx.sem("p2ihv")
    xl3 = [ctx.sem(f"p2xl3{i}") for i in range(2)]
    xo3v = ctx.sem("p2xo3v")
    so3 = [ctx.sem(f"p2so3{i}") for i in range(2)]

    n_ln = [0]
    n_ys = [0]
    n_ifm = [0]
    n_ife = [0]
    n_ihv = [0]

    def inner_ln(wvec, bvec, wcol, out_t, silu, tmb=None):
        li = n_ln[0]
        tm_ = cA if tmb is None else tmb
        R = lambda i_: rows[0:1, i_ * T:(i_ + 1) * T]
        if E == "vector":
            v = e.vector
            if li == 0:
                v.wait_ge(wine, 4)
            v.tensor_copy(hb[:], hI[:]).then_inc(hbv, 1)
            v.wait_ge(ism, li + 1)
            v.tensor_scalar_mul(R(0), psS2[0:1, :], 1.0 / 512)
            v.tensor_scalar_mul(R(1), psQ2[0:1, :], 1.0 / 512)
            ctx.edge(v, v.tensor_tensor(R(2), R(0), R(0), ALU.mult), "p2ve")
            v.tensor_tensor(R(1), R(1), R(2), ALU.subtract).then_inc(ivf, 1)
            v.wait_ge(isf, li + 1)
            ctx.edge(v, v.reciprocal(R(2), R(1)), "p2ve")
            v.tensor_tensor(R(3), R(0), R(2), ALU.mult).then_inc(ivf, 1)
            v.wait_ge(ibm, li + 1)
            v.tensor_copy(rbc[:], psB1[:])
            v.tensor_copy(mrbc[:], psB2[:])
            for g_ in range(4):
                v.tensor_tensor(tm_[:, g_ * T:(g_ + 1) * T], hI[:, g_ * T:(g_ + 1) * T],
                                rbc[:], ALU.mult)
                v.tensor_tensor(tm_[:, g_ * T:(g_ + 1) * T], tm_[:, g_ * T:(g_ + 1) * T],
                                mrbc[:], ALU.subtract).then_inc(itv, 1)
        elif E == "scalar":
            sc = e.scalar
            sc.wait_ge(hbv, li + 1)
            sc.activation(sqh[:], hI[:], AF.Square).then_inc(hbs, 1)
            sc.wait_ge(ivf, 2 * li + 1)
            sc.activation(rows[0:1, 1 * T:2 * T], rows[0:1, 1 * T:2 * T],
                          AF.Sqrt, bias=ctx.t["epsv"][0:1, 0:1]).then_inc(isf, 1)
            for g_ in range(4):
                sc.wait_ge(itv, 4 * li + g_ + 1)
                fn = AF.Silu if silu else AF.Identity
                sc.activation(out_t[:, g_ * T:(g_ + 1) * T], tm_[:, g_ * T:(g_ + 1) * T],
                              fn, scale=wvec[:, wcol + g_:wcol + g_ + 1],
                              bias=bvec[:, wcol + g_:wcol + g_ + 1]).then_inc(ys, 1)
        elif E == "tensor":
            te = e.tensor
            te.wait_ge(hbv, li + 1)
            te.wait_ge(hbs, li + 1)
            for g_ in range(4):
                for h in range(2):
                    te.matmul(psS2[0:1, h * 512:(h + 1) * 512], onesb[:, 0:1],
                              hb[:, g_ * T + h * 512:g_ * T + (h + 1) * 512],
                              start=(g_ == 0), stop=(g_ == 3))
            for g_ in range(4):
                for h in range(2):
                    last = te.matmul(psQ2[0:1, h * 512:(h + 1) * 512], onesb[:, 0:1],
                                     sqh[:, g_ * T + h * 512:g_ * T + (h + 1) * 512],
                                     start=(g_ == 0), stop=(g_ == 3))
            last.then_inc(ism, 1)
            te.wait_ge(ivf, 2 * li + 2)
            for h in range(2):
                te.matmul(psB1[:, h * 512:(h + 1) * 512], onesf[0:1, :],
                          rows[0:1, 2 * T + h * 512:2 * T + (h + 1) * 512],
                          start=True, stop=True)
            for h in range(2):
                last = te.matmul(psB2[:, h * 512:(h + 1) * 512], onesf[0:1, :],
                                 rows[0:1, 3 * T + h * 512:3 * T + (h + 1) * 512],
                                 start=True, stop=True)
            last.then_inc(ibm, 1)
        n_ln[0] += 1
        n_ys[0] += 4

    def cmul_sc(v, dre, dim_, sre, sim, scre, scim, tmp):
        v.tensor_single_scalar(dre, sre, scre, ALU.mult)
        ctx.edge(v, v.tensor_single_scalar(tmp, sim, scim, ALU.mult), "p2ve")
        v.tensor_tensor(dre, dre, tmp, ALU.subtract)
        v.tensor_single_scalar(dim_, sim, scre, ALU.mult)
        ctx.edge(v, v.tensor_single_scalar(tmp, sre, scim, ALU.mult), "p2ve")
        ctx.edge(v, v.tensor_tensor(dim_, dim_, tmp, ALU.add), "p2ve")

    V3 = lambda t_: t_.rearrange("p (g t) -> p g t", g=4)

    for i in range(4):
        ycol = 4 * i
        inner_ln(ln1w, ln1b, 4 * i, yB, True)
        ys_after_ln1 = n_ys[0]

        if E == "vector":
            v = e.vector
            EG = lambda inst: ctx.edge(v, inst, "p2ve")
            zre = pzre[:, 4 * i:4 * i + 4]; zim = pzim[:, 4 * i:4 * i + 4]
            P_RE = scb[:, 48:52]; P_IM = scb[:, 52:56]
            TM = scb[:, 44:48]
            v.tensor_tensor(scb[:, 36:40], zre, zre, ALU.mult)
            EG(v.tensor_tensor(TM, zim, zim, ALU.mult))
            v.tensor_tensor(scb[:, 36:40], scb[:, 36:40], TM, ALU.add).then_inc(rq, 1)
            v.wait_ge(rs, i + 1)
            EG(v.reciprocal(TM, scb[:, 36:40]))
            EG(v.tensor_tensor(TM, TM, scb[:, 40:44], ALU.mult))
            v.tensor_tensor(P_RE, zre, TM, ALU.mult)
            EG(v.tensor_tensor(P_IM, zim, TM, ALU.mult))
            v.tensor_copy(pw[:, 0:4], P_RE)
            EG(v.tensor_copy(pw[:, 48:52], P_IM))
            for k in range(1, 11):
                pr0 = pw[:, (k - 1) * 4:k * 4]; pi0 = pw[:, 48 + (k - 1) * 4:48 + k * 4]
                prk = pw[:, k * 4:(k + 1) * 4]; pik = pw[:, 48 + k * 4:48 + (k + 1) * 4]
                v.tensor_tensor(prk, pr0, pr0, ALU.mult)
                EG(v.tensor_tensor(TM, pi0, pi0, ALU.mult))
                v.tensor_tensor(prk, prk, TM, ALU.subtract)
                EG(v.tensor_tensor(pik, pr0, pi0, ALU.mult))
                EG(v.tensor_scalar_mul(pik, pik, 2.0))
            EG(v.tensor_scalar_mul(pw[:, 96:140], pw[:, 48:92], -1.0))
            pvr = V3(povr); pvi = V3(povi)
            for g_ in range(4):
                v.tensor_copy(pvr[:, g_, 0:1], P_RE[:, g_:g_ + 1])
                v.tensor_copy(pvi[:, g_, 0:1], P_IM[:, g_:g_ + 1])
                n = 1
                k = 0
                while n < T:
                    take = min(n, T - n)
                    cmul_sc(v, pvr[:, g_, n:n + take], pvi[:, g_, n:n + take],
                            pvr[:, g_, 0:take], pvi[:, g_, 0:take],
                            pw[:, k * 4 + g_:k * 4 + g_ + 1],
                            pw[:, 48 + k * 4 + g_:48 + k * 4 + g_ + 1],
                            iB[:, 0:take])
                    n *= 2
                    k += 1
        if E == "scalar":
            sc = e.scalar
            sc.wait_ge(rq, i + 1)
            sc.activation(scb[:, 36:40], scb[:, 36:40], AF.Sqrt)
            sc.activation(scb[:, 40:44], scb[:, 36:40], AF.Tanh).then_inc(rs, 1)

        if E == "vector":
            v = e.vector
            v.wait_ge(ys, ys_after_ln1)
            yv = V3(yB); cAv = V3(cA); iAv = V3(iA); iBv = V3(iB)
            ms2 = ctx.sem("p2ms")
            last = None
            for g_ in range(4):
                last = v.memset(iAv[:, g_, 0:1], 0.0)
            last.then_inc(ms2, 1)
            for g_ in range(4):
                pr = pw[:, g_:g_ + 1]; pi = pw[:, 48 + g_:48 + g_ + 1]
                v.tensor_copy(cAv[:, g_, 0:1], yv[:, g_, 0:1])
                v.scalar_tensor_tensor(cAv[:, g_, 1:T], yv[:, g_, 0:T - 1], pr,
                                       yv[:, g_, 1:T], ALU.mult, ALU.add)
                v.tensor_single_scalar(iAv[:, g_, 1:T], yv[:, g_, 0:T - 1], pi, ALU.mult)
            v.wait_ge(ms2, i + 1)
            src_re, src_im, dst_re, dst_im = cA, iA, yB, iB
            for k in range(1, 10):
                s_ = 1 << k
                sr, si_ = V3(src_re), V3(src_im)
                dr, di_ = V3(dst_re), V3(dst_im)
                for g_ in range(4):
                    pr = pw[:, k * 4 + g_:k * 4 + g_ + 1]
                    pi = pw[:, 48 + k * 4 + g_:48 + k * 4 + g_ + 1]
                    pin = pw[:, 96 + k * 4 + g_:96 + k * 4 + g_ + 1]
                    v.tensor_copy(dr[:, g_, 0:s_], sr[:, g_, 0:s_])
                    v.tensor_copy(di_[:, g_, 0:s_], si_[:, g_, 0:s_])
                    v.scalar_tensor_tensor(dr[:, g_, s_:T], sr[:, g_, 0:T - s_], pr,
                                           sr[:, g_, s_:T], ALU.mult, ALU.add)
                    v.scalar_tensor_tensor(dr[:, g_, s_:T], si_[:, g_, 0:T - s_],
                                           pin,
                                           dr[:, g_, s_:T], ALU.mult, ALU.add)
                    v.scalar_tensor_tensor(di_[:, g_, s_:T], si_[:, g_, 0:T - s_], pr,
                                           si_[:, g_, s_:T], ALU.mult, ALU.add)
                    v.scalar_tensor_tensor(di_[:, g_, s_:T], sr[:, g_, 0:T - s_], pi,
                                           di_[:, g_, s_:T], ALU.mult, ALU.add)
                src_re, src_im, dst_re, dst_im = dst_re, dst_im, src_re, src_im
            fre, fim = V3(src_re), V3(src_im)
            lr = lcre[:, 4 * i:4 * i + 4]; li_ = lcim[:, 4 * i:4 * i + 4]
            for g_ in range(4):
                cmul_sc(v, scb[:, 0 + g_:1 + g_], scb[:, 4 + g_:5 + g_],
                        lr[:, g_:g_ + 1], li_[:, g_:g_ + 1],
                        pw[:, 40 + g_:41 + g_], pw[:, 88 + g_:89 + g_],
                        scb[:, 44:45])
                ctx.edge(v, v.tensor_tensor(scb[:, 0 + g_:1 + g_], scb[:, 0 + g_:1 + g_],
                                fre[:, g_, T - 1:T], ALU.add), "p2ve")
                last = v.tensor_tensor(scb[:, 4 + g_:5 + g_], scb[:, 4 + g_:5 + g_],
                                       fim[:, g_, T - 1:T], ALU.add)
                if g_ < 3:
                    ctx.edge(v, last, "p2ve")
            last.then_inc(scs, 1)
            v.wait_ge(gcc, 48 * (i + 1))
            for g_ in range(4):
                v.tensor_single_scalar(scb[:, 24 + g_:25 + g_], lr[:, g_:g_ + 1],
                                       selv[:, 0:1], ALU.mult)
                ctx.edge(v, v.tensor_single_scalar(scb[:, 44:45], scb[:, 8 + g_:9 + g_],
                                       selv[:, 1:2], ALU.mult), "p2ve")
                v.tensor_tensor(scb[:, 24 + g_:25 + g_], scb[:, 24 + g_:25 + g_],
                                scb[:, 44:45], ALU.add)
                v.tensor_single_scalar(scb[:, 28 + g_:29 + g_], li_[:, g_:g_ + 1],
                                       selv[:, 0:1], ALU.mult)
                ctx.edge(v, v.tensor_single_scalar(scb[:, 44:45], scb[:, 12 + g_:13 + g_],
                                       selv[:, 1:2], ALU.mult), "p2ve")
                ctx.edge(v, v.tensor_tensor(scb[:, 28 + g_:29 + g_], scb[:, 28 + g_:29 + g_],
                                scb[:, 44:45], ALU.add), "p2ve")
                ctx.edge(v, v.tensor_scalar_mul(scb[:, 32 + g_:33 + g_], scb[:, 28 + g_:29 + g_], -1.0), "p2ve")
            for g_ in range(4):
                v.scalar_tensor_tensor(fre[:, g_, :], V3(povr)[:, g_, :],
                                       scb[:, 24 + g_:25 + g_], fre[:, g_, :],
                                       ALU.mult, ALU.add)
                v.scalar_tensor_tensor(fre[:, g_, :], V3(povi)[:, g_, :],
                                       scb[:, 32 + g_:33 + g_], fre[:, g_, :],
                                       ALU.mult, ALU.add)
                v.scalar_tensor_tensor(hI[:, g_ * T:(g_ + 1) * T], src_re[:, g_ * T:(g_ + 1) * T],
                                       sclv[:, 4 * i + g_:4 * i + g_ + 1],
                                       hI[:, g_ * T:(g_ + 1) * T], ALU.mult, ALU.add)
        if E == "gpsimd":
            g = e.gpsimd
            g.wait_ge(scs, i + 1)
            g.dma_start(out=d_sc_in, in_=scb[:, 0:8]).then_inc(gcc, 16)
            g.wait_ge(gcc, 48 * i + 16)
            g.collective_compute("AllGather", ALU.bypass,
                                 replica_groups=[[0, 1], [2, 3], [4, 5], [6, 7]],
                                 ins=[d_sc_in], outs=[d_sc_out]).then_inc(gcv, 1)
            g.wait_ge(gcv, i + 1)
            g.dma_start(out=scb[:, 8:16], in_=d_sc_out[0:128, :]).then_inc(gcc, 16)
            g.dma_start(out=scb[:, 16:24], in_=d_sc_out[128:256, :]).then_inc(gcc, 16)

        inner_ln(ln2w, ln2b, 4 * i, y2, False)
        ys_after_ln2 = n_ys[0]

        if E == "gpsimd":
            g = e.gpsimd
            if True:
                g.wait_ge(ifm, n_ifm[0])
                if i == 0:
                    g.wait_ge(winm, 2)
                g.dma_start(out=w1.rearrange("p (kd f) -> p kd f", kd=4),
                            in_=d_w1T[i].rearrange("p (kd f) -> p kd f", kd=4)).then_inc(wl1, 16)
                g.dma_start(out=w2.rearrange("p (fk dd) -> p fk dd", fk=16),
                            in_=d_w2T[i].rearrange("p (fk dd) -> p fk dd", fk=16)).then_inc(wl2, 16)
        if E == "tensor":
            te = e.tensor
            te.wait_ge(wl1, 16 * (i + 1))
            te.wait_ge(ys, ys_after_ln2)
            for fk in range(16):
                ps = PS[fk % 2]
                if fk >= 2:
                    te.wait_ge(ife, n_ife[0] + fk - 1)
                else:
                    te.wait_ge(ife, n_ife[0])
                    if i == 0:
                        te.wait_ge(wine, 4)
                for kd in range(4):
                    lhs = w1[:, kd * 2048 + fk * 128:kd * 2048 + (fk + 1) * 128]
                    for h in range(2):
                        last = te.matmul(ps[:, h * 512:(h + 1) * 512], lhs,
                                         y2[:, kd * T + h * 512:kd * T + (h + 1) * 512],
                                         start=(kd == 0), stop=(kd == 3))
                last.then_inc(ifm, 1)
            te.wait_ge(wl2, 16 * (i + 1))
            for g_ in range(4):
                ps = PS[g_ % 2]
                te.wait_ge(ife, n_ife[0] + 16)
                if g_ >= 2:
                    te.wait_ge(ife, n_ife[0] + 16 + g_ - 1)
                for fk in range(16):
                    lhs = w2[:, fk * 512 + g_ * 128:fk * 512 + (g_ + 1) * 128]
                    for h in range(2):
                        last = te.matmul(ps[:, h * 512:(h + 1) * 512], lhs,
                                         zb[:, fk * T + h * 512:fk * T + (h + 1) * 512],
                                         start=(fk == 0), stop=(fk == 15))
                last.then_inc(ifm, 1)
        if E == "scalar":
            sc = e.scalar
            for fk in range(16):
                sc.wait_ge(ifm, n_ifm[0] + fk + 1)
                sc.activation(zb[:, fk * T:(fk + 1) * T], PS[fk % 2][:], AF.Silu,
                              bias=b1v[:, 16 * i + fk:16 * i + fk + 1]).then_inc(ife, 1)
            for g_ in range(4):
                sc.wait_ge(ifm, n_ifm[0] + 16 + g_ + 1)
                if g_ >= 2:
                    sc.wait_ge(ihv, n_ihv[0] + g_ - 1)
                sc.activation(tm2[:, (g_ % 2) * T:(g_ % 2 + 1) * T], PS[g_ % 2][:],
                              AF.Identity, bias=b2v[:, 4 * i + g_:4 * i + g_ + 1]).then_inc(ife, 1)
        if E == "vector":
            v = e.vector
            for g_ in range(4):
                v.wait_ge(ife, n_ife[0] + 16 + g_ + 1)
                v.tensor_tensor(hI[:, g_ * T:(g_ + 1) * T], hI[:, g_ * T:(g_ + 1) * T],
                                tm2[:, (g_ % 2) * T:(g_ % 2 + 1) * T], ALU.add).then_inc(ihv, 1)
        n_ifm[0] += 20
        n_ife[0] += 20
        n_ihv[0] += 4

    inner_ln(lnlw, lnlb, 0, y2, False, tmb=yB)
    ys_final = n_ys[0]

    if E == "gpsimd":
        g = e.gpsimd
        g.wait_ge(ifm, n_ifm[0])
        g.dma_start(out=wout.rearrange("p (kd v) -> p kd v", kd=4),
                    in_=d_woutT.rearrange("p (kd v) -> p kd v", kd=4)).then_inc(wlo, 16)
        for vk in range(NK):
            g.wait_ge(xo3v, vk + 1)
            g.dma_start(out=x_out[vk * 128:(vk + 1) * 128, :],
                        in_=xo3[:, (vk % 2) * T:(vk % 2 + 1) * T]).then_inc(so3[vk % 2], 16)
    if E == "sync":
        sy = e.sync
        for vk in range(NK):
            if vk < 2:
                sy.wait_ge(ys, ys_final)
            if vk >= 2:
                sy.wait_ge(xo3v, vk - 1)
            sy.dma_start(out=xin3[:, (vk % 2) * T:(vk % 2 + 1) * T],
                         in_=x_in[vk * 128:(vk + 1) * 128, :]).then_inc(xl3[vk % 2], 16)
    if E == "tensor":
        te = e.tensor
        te.wait_ge(wlo, 16)
        te.wait_ge(ys, ys_final)
        for vk in range(NK):
            ps = PS[vk % 2]
            if vk >= 2:
                te.wait_ge(ife, n_ife[0] + vk - 1)
            else:
                te.wait_ge(ife, n_ife[0])
            for kd in range(4):
                lhs = wout[:, kd * 4096 + vk * 128:kd * 4096 + (vk + 1) * 128]
                for h in range(2):
                    last = te.matmul(ps[:, h * 512:(h + 1) * 512], lhs,
                                     y2[:, kd * T + h * 512:kd * T + (h + 1) * 512],
                                     start=(kd == 0), stop=(kd == 3))
            last.then_inc(ifm, 1)
    if E == "scalar":
        sc = e.scalar
        for vk in range(NK):
            sc.wait_ge(ifm, n_ifm[0] + vk + 1)
            if vk < 2:
                sc.wait_ge(ihv, n_ihv[0])
            if vk >= 2:
                sc.wait_ge(xo3v, vk - 1)
            sc.activation(tm2[:, (vk % 2) * T:(vk % 2 + 1) * T], PS[vk % 2][:],
                          AF.Identity, bias=boutv[:, vk:vk + 1]).then_inc(ife, 1)
    if E == "vector":
        v = e.vector
        for vk in range(NK):
            v.wait_ge(ife, n_ife[0] + vk + 1)
            v.wait_ge(xl3[vk % 2], 16 * (vk // 2 + 1))
            if vk >= 2:
                v.wait_ge(so3[vk % 2], 16 * (vk // 2))
            v.tensor_tensor(xo3[:, (vk % 2) * T:(vk % 2 + 1) * T],
                            xin3[:, (vk % 2) * T:(vk % 2 + 1) * T],
                            tm2[:, (vk % 2) * T:(vk % 2 + 1) * T], ALU.add).then_inc(xo3v, 1)
    return {"done": [(so3[0], 16 * 16), (so3[1], 16 * 16), (ctx.sem("p2ifm"), n_ifm[0] + NK),
                     (ctx.sem("p2xo3v"), NK), (ctx.sem("p2gcc"), 48 * 4)]}



def emit_p4(ctx, eng, x_in, x_out, load_weights):
    e = ctx.e
    E = eng
    xn = ctx.t["xn"]
    emit_ln(ctx, "ln", eng, x_in, xn)
    xnv = ctx.sem("lnxnv")
    rwsb = ctx.t["rw_sb"]; ebsb = ctx.t["eb_sb"]; rbsb = ctx.t["rb_sb"]
    idt = ctx.t["idt"]
    lgb = ctx.t["p4lgb"]; smf = ctx.t["p4smf"]; wkf = ctx.t["p4wkf"]
    wdf4 = ctx.t["p4wdf4"]; wdT = ctx.t["p4wdT"]; cols = ctx.t["p4cols"]
    wrow = ctx.t["p4wrow"]; hws = ctx.t["p4hws"]
    tmp4 = ctx.t["p4tmp4"]; xin4 = ctx.t["p4xin4"]; xo4 = ctx.t["p4xo4"]
    psL = ctx.t["p4psL"]; psT = ctx.t["p4psT"]; psTb = ctx.t["p4psTb"]
    psR = ctx.t["p4psR"]; psW = ctx.t["p4psW"]
    psM = [ctx.t["p4psM0"], ctx.t["p4psM1"]]
    onesb, onesf = ctx.t["onesb"], ctx.t["onesf"]
    d_rwT, d_ebw, d_rbv = ctx.t["d_rwT"], ctx.t["d_ebw"], ctx.t["d_rbv"]

    rl = ctx.sem("p46rl")
    lgm = ctx.sem("p4lgm"); lge = ctx.sem("p4lge")
    trm = ctx.sem("p4trm"); trc = ctx.sem("p4trc")
    vnm = ctx.sem("p4vnm"); sexp = ctx.sem("p4sexp")
    wdv = ctx.sem("p4wdv")
    tbm = ctx.sem("p4tbm")
    tbc = ctx.sem("p4tbc")
    wsr = ctx.sem("p4wsr"); wbm = ctx.sem("p4wbm"); wbc = ctx.sem("p4wbc")
    mxm = ctx.sem("p4mxm"); mxe = ctx.sem("p4mxe"); mxv = ctx.sem("p4mxv")
    xl4 = [ctx.sem(f"p4xl4{i}") for i in range(2)]
    so4 = [ctx.sem(f"p4so4{i}") for i in range(2)]
    wdb = ctx.t["p4lgb"]

    if E == "gpsimd":
        g = e.gpsimd
        ctx.emit_fence(g)
        if load_weights:
            g.dma_start(out=rwsb.rearrange("p (kv e) -> p kv e", kv=NK),
                        in_=d_rwT.rearrange("(kv p) e -> p kv e", p=128)).then_inc(rl, 16)
            g.dma_start(out=ebsb[:], in_=d_ebw).then_inc(rl, 16)
            g.dma_start(out=rbsb[0:64, 0:1], in_=d_rbv).then_inc(rl, 16)
        for dk in range(NK):
            g.wait_ge(mxv, dk + 1)
            g.dma_start(out=x_out[dk * 128:(dk + 1) * 128, :],
                        in_=xo4[:, (dk % 2) * T:(dk % 2 + 1) * T]).then_inc(so4[dk % 2], 16)
    if E == "sync":
        sy = e.sync
        for dk in range(NK):
            if dk < 2:
                sy.wait_ge(xnv, NK)
            if dk >= 2:
                sy.wait_ge(mxv, dk - 1)
            sy.dma_start(out=xin4[:, (dk % 2) * T:(dk % 2 + 1) * T],
                         in_=x_in[dk * 128:(dk + 1) * 128, :]).then_inc(xl4[dk % 2], 16)
    if E == "tensor":
        te = e.tensor
        if load_weights:
            te.wait_ge(rl, 48)
        te.wait_ge(xnv, NK)
        for kv in range(NK):
            for h in range(2):
                last = te.matmul(psL[:, h * 512:(h + 1) * 512],
                                 rwsb[:, kv * 64:(kv + 1) * 64],
                                 xn[:, kv * T + h * 512:kv * T + (h + 1) * 512],
                                 start=(kv == 0), stop=(kv == NK - 1))
        last.then_inc(lgm, 1)
        te.wait_ge(lge, 1)
        for ti in range(8):
            te.transpose(psT[:, ti * 64:(ti + 1) * 64], lgb[0:64, ti * 128:(ti + 1) * 128],
                         idt[0:64, 0:64]).then_inc(trm, 1)
        for ti in range(8):
            te.wait_ge(wdv, ti + 1)
            if ti >= 1:
                te.wait_ge(tbc, ti)
            te.transpose(psTb[:], ctx.t["p4wdb"][:, ti * 64:(ti + 1) * 64],
                         idt[:]).then_inc(tbm, 1)
        te.wait_ge(tbc, 8)
        for h in range(2):
            last = te.matmul(psR[0:1, h * 512:(h + 1) * 512], onesb[0:64, 0:1],
                             wdT[:, h * 512:(h + 1) * 512], start=True, stop=True)
        last.then_inc(lgm, 1)
        te.wait_ge(wsr, 1)
        for h in range(2):
            last = te.matmul(psW[:, h * 512:(h + 1) * 512], onesf[0:1, :],
                             wrow[0:1, h * 512:(h + 1) * 512], start=True, stop=True)
        last.then_inc(wbm, 1)
        for dk in range(NK):
            if dk >= 2:
                te.wait_ge(mxe, dk - 1)
            for h in range(2):
                last = te.matmul(psM[dk % 2][:, h * 512:(h + 1) * 512],
                                 ebsb[:, dk * 128:(dk + 1) * 128],
                                 wdT[:, h * 512:(h + 1) * 512], start=True, stop=True)
            last.then_inc(mxm, 1)
    if E == "scalar":
        sc = e.scalar
        ctx.emit_fence(sc)
        sc.wait_ge(lgm, 1)
        sc.activation(lgb[:], psL[:], AF.Identity, bias=rbsb[:, 0:1]).then_inc(lge, 1)
        for ti in range(8):
            sc.wait_ge(vnm, ti + 1)
            sc.activation(smf[:, ti * 64:(ti + 1) * 64], smf[:, ti * 64:(ti + 1) * 64],
                          AF.Exp, bias=cols[:, 0:1]).then_inc(sexp, 1)
        for dk in range(NK):
            sc.wait_ge(mxm, dk + 1)
            if dk >= 2:
                sc.wait_ge(mxv, dk - 1)
            sc.activation(tmp4[:, (dk % 2) * T:(dk % 2 + 1) * T], psM[dk % 2][:],
                          AF.Identity, scale=0.5).then_inc(mxe, 1)
    if E == "vector":
        v = e.vector
        ctx.emit_fence(v)
        EG = lambda inst: ctx.edge(v, inst, "p4ve")
        for ti in range(8):
            v.wait_ge(trm, ti + 1)
            S = smf[:, ti * 64:(ti + 1) * 64]
            EG(v.tensor_copy(S, psT[:, ti * 64:(ti + 1) * 64]))
            v.reduce_max(cols[:, 0:1], S, axis=AX, negate=True).then_inc(vnm, 1)
            v.wait_ge(sexp, ti + 1)
            EG(v.reduce_sum(cols[:, 1:2], S, axis=AX))
            EG(v.reciprocal(cols[:, 2:3], cols[:, 1:2]))
            EG(v.tensor_scalar_mul(S, S, cols[:, 2:3]))
            W = wkf[:, ti * 64:(ti + 1) * 64]
            EG(v.tensor_copy(W, S))
            for r in range(8):
                EG(v.reduce_max(cols[:, 3:4], W, axis=AX))
                if r < 7:
                    EG(v.tensor_single_scalar(wdf4[:, ti * 64:(ti + 1) * 64], W,
                                              cols[:, 3:4], ALU.is_ge))
                    EG(v.scalar_tensor_tensor(W, wdf4[:, ti * 64:(ti + 1) * 64], -1e30,
                                              W, ALU.mult, ALU.add))
            EG(v.tensor_single_scalar(wdf4[:, ti * 64:(ti + 1) * 64], S, cols[:, 3:4], ALU.is_ge))
            EG(v.tensor_tensor(wdf4[:, ti * 64:(ti + 1) * 64], S,
                               wdf4[:, ti * 64:(ti + 1) * 64], ALU.mult))
            v.tensor_copy(ctx.t["p4wdb"][:, ti * 64:(ti + 1) * 64],
                          wdf4[:, ti * 64:(ti + 1) * 64]).then_inc(wdv, 1)
        for ti in range(8):
            v.wait_ge(tbm, ti + 1)
            v.tensor_copy(wdT[:, ti * 128:(ti + 1) * 128], psTb[:]).then_inc(tbc, 1)
        v.wait_ge(lgm, 2)
        v.tensor_scalar_mul(wrow[0:1, :], psR[0:1, :], 0.5).then_inc(wsr, 1)
        v.wait_ge(wbm, 1)
        v.tensor_copy(hws[:], psW[:]).then_inc(wbc, 1)
        for dk in range(NK):
            v.wait_ge(mxe, dk + 1)
            v.wait_ge(xl4[dk % 2], 16 * (dk // 2 + 1))
            if dk >= 2:
                v.wait_ge(so4[dk % 2], 16 * (dk // 2))
            o = xo4[:, (dk % 2) * T:(dk % 2 + 1) * T]
            v.tensor_tensor(o, xn[:, dk * T:(dk + 1) * T], hws[:], ALU.mult)
            v.tensor_tensor(o, o, xin4[:, (dk % 2) * T:(dk % 2 + 1) * T], ALU.add)
            v.tensor_tensor(o, o, tmp4[:, (dk % 2) * T:(dk % 2 + 1) * T],
                            ALU.add).then_inc(mxv, 1)
    return {"done": [(so4[0], 16 * 16), (so4[1], 16 * 16), (ctx.sem("p4mxv"), NK),
                     (ctx.sem("p4mxe"), NK), (ctx.sem("p4mxm"), NK)]}



def emit_p5(ctx, eng, x_in, x_out):
    e = ctx.e
    E = eng
    xn = ctx.t["xn"]
    emit_ln(ctx, "ln", eng, x_in, xn)
    xnv = ctx.sem("lnxnv")
    idt = ctx.t["idt"]
    onesb, onesf = ctx.t["onesb"], ctx.t["onesf"]
    stg = ctx.t["p5stg"]; row5 = ctx.t["p5row"]; sgr = ctx.t["p5sgr"]
    al5 = ctx.t["p5al"]; qkp = ctx.t["p5qkp"]; sgp = ctx.t["p5sgp"]
    abc = ctx.t["p5abc"]; qbp = ctx.t["p5qb"]; absq = ctx.t["p5absq"]
    Ksb = ctx.t["p5K"]; Vsb = ctx.t["p5V"]; att = ctx.t["p5att"]
    Qsb = ctx.t["p5Q"]
    vbc = ctx.t["p5vbc"]; msk = ctx.t["p5msk"]; tmp5 = ctx.t["p5tmp"]
    xin5 = ctx.t["p5xin"]; xo5 = ctx.t["p5xo"]
    psTT = ctx.t["p5psTT"]; psBC = ctx.t["p5psBC"]
    psS5 = [ctx.t["p5psS0"], ctx.t["p5psS1"]]
    psO5 = [ctx.t["p5psO0"], ctx.t["p5psO1"]]
    d_bvq, d_bvk, d_bvv = ctx.t["d_bvq"], ctx.t["d_bvk"], ctx.t["d_bvv"]
    d_bvqp, d_bvkp = ctx.t["d_bvqp"], ctx.t["d_bvkp"]
    d_mask = ctx.t["d_mask"]
    gin_fm, g_fm = ctx.t["d_gin_fm"], ctx.t["d_g_fm"]
    gin_tok, g_tok = ctx.t["d_gin_tok"], ctx.t["d_g_tok"]

    bvl = ctx.sem("p5bvl")
    bvs = ctx.sem("p5bvs")
    bvv_ = ctx.sem("p5bvv")
    bvm = ctx.sem("p5bvm")
    bvc = ctx.sem("p5bvc")
    trm5 = ctx.sem("p5trm")
    trc5 = ctx.sem("p5trc")
    cfm = ctx.sem("p5cfm")
    ctk = ctx.sem("p5ctk")
    gcf = ctx.sem("p5gcf")
    gct = ctx.sem("p5gct")
    qmv = ctx.sem("p5qmv")
    klc = ctx.sem("p5klc")
    kmv = ctx.sem("p5kmv")
    vlc = ctx.sem("p5vlc")
    vmv = ctx.sem("p5vmv")
    mls = [ctx.sem(f"p5mls{i}") for i in range(2)]
    smm = ctx.sem("p5smm")
    sev = ctx.sem("p5sev")
    smk = ctx.sem("p5smk")
    omm = ctx.sem("p5omm")
    oev = ctx.sem("p5oev")
    oxv = ctx.sem("p5oxv")
    xl5 = [ctx.sem(f"p5xl5{i}") for i in range(2)]
    so5 = [ctx.sem(f"p5so5{i}") for i in range(2)]

    HD2 = 1 / 32.0

    if E == "gpsimd":
        g = e.gpsimd
        g.wait_ge(xnv, NK)
        g.dma_start(out=row5[0:1, 0:4096], in_=d_bvv).then_inc(bvl, 16)
        g.dma_start(out=qkp[:, 0:32], in_=d_bvqp).then_inc(bvl, 16)
        g.dma_start(out=qkp[:, 32:64], in_=d_bvkp).then_inc(bvl, 16)
        g.wait_ge(xnv, NK)
        g.dma_start(out=gin_fm.rearrange("(k p) t -> p k t", p=128),
                    in_=xn.rearrange("p (k t) -> p k t", k=NK)).then_inc(cfm, 16)
        g.wait_ge(cfm, 16)
        g.collective_compute("AllGather", ALU.bypass,
                             replica_groups=[[0, 1], [2, 3], [4, 5], [6, 7]],
                             ins=[gin_fm], outs=[g_fm]).then_inc(gcf, 1)
        trcs = ctx.sem("p5trcs")
        for tt in range(8):
            g.wait_ge(trcs, 8 * (tt + 1))
            g.dma_start(out=gin_tok[tt * 128:(tt + 1) * 128, :], in_=stg[:]).then_inc(ctk, 16)
        g.wait_ge(ctk, 128)
        g.collective_compute("AllGather", ALU.bypass,
                             replica_groups=[[0, 1], [2, 3], [4, 5], [6, 7]],
                             ins=[gin_tok], outs=[g_tok]).then_inc(gct, 1)
        for hd in range(4):
            for hk in range(8):
                u = hd * 8 + hk
                g.wait_ge(oxv, u + 1)
                dk = hd * 8 + hk
                g.dma_start(out=x_out[dk * 128:(dk + 1) * 128, :],
                            in_=xo5[:, (u % 2) * T:(u % 2 + 1) * T]).then_inc(so5[u % 2], 16)
    if E == "sync":
        sy = e.sync
        qlc = ctx.sem("p5qlc")
        for hd in range(4):
            sy.wait_ge(cfm, 16)
            if hd > 0:
                sy.wait_ge(smm, 16 * hd)
            sy.dma_start(out=Qsb.rearrange("p (hk t) -> p hk t", hk=8),
                         in_=gin_fm[hd * 1024:(hd + 1) * 1024, :]
                         .rearrange("(hk p) t -> p hk t", p=128)).then_inc(qlc, 16)
            sy.wait_ge(gcf, 1)
            sy.wait_ge(ctk, 128)
            for blk in range(2):
                dst = Ksb.rearrange("p (hk b t) -> p hk b t", hk=8, b=2)[:, :, blk, :]
                src = g_fm[blk * 4096 + hd * 1024:blk * 4096 + (hd + 1) * 1024, :]
                sy.dma_start(out=dst,
                             in_=src.rearrange("(hk p) t -> p hk t", p=128)).then_inc(klc, 16)
            sy.wait_ge(gct, 1)
            if hd > 0:
                sy.wait_ge(omm, 8 * hd)
            for si in range(16):
                sy.dma_start(out=Vsb[:, si * T:(si + 1) * T],
                             in_=g_tok[si * 128:(si + 1) * 128,
                                       hd * 1024:(hd + 1) * 1024]).then_inc(vlc, 16)
            for si in range(16):
                u = hd * 16 + si
                if u == 0:
                    sy.wait_ge(ctx.sem("p5bvm"), 6)
                    sy.wait_ge(ctk, 128)
                if u >= 2:
                    sy.wait_ge(smk, u - 1)
                sy.dma_start(out=msk[:, (u % 2) * T:(u % 2 + 1) * T],
                             in_=d_mask[si * 128:(si + 1) * 128, :]).then_inc(mls[u % 2], 16)
            for hk in range(8):
                u = hd * 8 + hk
                if u < 2:
                    sy.wait_ge(ctx.sem("p5bvm"), 6)
                    sy.wait_ge(ctk, 128)
                if u >= 2:
                    sy.wait_ge(oxv, u - 1)
                dk = hd * 8 + hk
                sy.dma_start(out=xin5[:, (u % 2) * T:(u % 2 + 1) * T],
                             in_=x_in[dk * 128:(dk + 1) * 128, :]).then_inc(xl5[u % 2], 16)
    if E == "scalar":
        sc = e.scalar
        ctx.emit_fence(sc)
        sc.wait_ge(bvl, 48)
        sc.activation(absq[:], qkp[:, 0:64], AF.Abs).then_inc(bvs, 1)
        sc.activation(sgp[:, 0:64], qkp[:, 0:64], AF.Sign).then_inc(bvs, 1)
        sc.activation(sgr[0:1, :], row5[0:1, :], AF.Sign).then_inc(bvs, 1)
        for hd in range(4):
            for si in range(16):
                u = hd * 16 + si
                sc.wait_ge(smm, u + 1)
                if hd > 0:
                    sc.wait_ge(omm, 8 * hd)
                sc.activation(att[:, si * T:(si + 1) * T], psS5[si % 2][:],
                              AF.Sigmoid, scale=4.0 * HD2).then_inc(sev, 1)
            for hk in range(8):
                u = hd * 8 + hk
                sc.wait_ge(omm, u + 1)
                if u >= 2:
                    sc.wait_ge(oxv, u - 1)
                sc.activation(tmp5[:, (u % 2) * T:(u % 2 + 1) * T], psO5[hk % 2][:],
                              AF.Identity).then_inc(oev, 1)
    if E == "tensor":
        te = e.tensor
        te.wait_ge(xnv, NK)
        trcs = ctx.sem("p5trcs")
        for tt in range(8):
            for k in range(NK):
                if k % 4 == 0 and (k > 0 or tt > 0):
                    te.wait_ge(trcs, tt * 8 + k // 4)
                te.transpose(psTT[:, (k % 4) * 128:(k % 4 + 1) * 128],
                             xn[:, k * T + tt * 128:k * T + tt * 128 + 128],
                             idt[:]).then_inc(trm5, 1)
        te.wait_ge(bvs, 1)
        te.matmul(psBC[0:1, 0:64], onesb[:, 0:1], absq[:],
                  start=True, stop=True).then_inc(bvm, 1)
        te.wait_ge(bvv_, 1)
        te.matmul(psBC[:, 0:64], onesf[0:1, :], al5[0:1, 0:64],
                  start=True, stop=True).then_inc(bvm, 1)
        te.wait_ge(bvv_, 3)
        for hd in range(4):
            for h in range(2):
                last = te.matmul(psBC[:, h * 512:(h + 1) * 512],
                                 onesf[0:1, :],
                                 sgr[0:1, hd * 1024 + h * 512:hd * 1024 + (h + 1) * 512],
                                 start=True, stop=True)
            last.then_inc(bvm, 1)
            te.wait_ge(bvc, hd + 1)
        for hd in range(4):
            te.wait_ge(kmv, 8 * (hd + 1))
            te.wait_ge(qmv, 8 * (hd + 1))
            for si in range(16):
                u = hd * 16 + si
                blk, sil = si // 8, si % 8
                if u >= 2:
                    te.wait_ge(sev, u - 1)
                for hk in range(8):
                    lhs = Ksb[:, hk * 2048 + blk * 1024 + sil * 128:
                              hk * 2048 + blk * 1024 + (sil + 1) * 128]
                    for h in range(2):
                        last = te.matmul(psS5[si % 2][:, h * 512:(h + 1) * 512], lhs,
                                         Qsb[:, hk * T + h * 512:hk * T + (h + 1) * 512],
                                         start=(hk == 0), stop=(hk == 7))
                last.then_inc(smm, 1)
            te.wait_ge(vmv, 16 * (hd + 1))
            for hk in range(8):
                u = hd * 8 + hk
                if u >= 2:
                    te.wait_ge(oev, u - 1)
                for si in range(16):
                    lhs = Vsb[:, si * T + hk * 128:si * T + (hk + 1) * 128]
                    te.wait_ge(smk, hd * 16 + si + 1)
                    for h in range(2):
                        last = te.matmul(psO5[hk % 2][:, h * 512:(h + 1) * 512], lhs,
                                         att[:, si * T + h * 512:si * T + (h + 1) * 512],
                                         start=(si == 0), stop=(si == 15))
                last.then_inc(omm, 1)
    if E == "vector":
        v = e.vector
        ctx.emit_fence(v)
        trcs = ctx.sem("p5trcs")
        for tt in range(8):
            if tt >= 1:
                v.wait_ge(ctk, 16 * tt)
            for kq in range(8):
                v.wait_ge(trm5, tt * 32 + kq * 4 + 4)
                v.tensor_copy(stg[:, kq * 512:(kq + 1) * 512],
                              psTT[:]).then_inc(trcs, 1)
        EG5 = lambda inst: ctx.edge(v, inst, "p5ve")
        v.wait_ge(bvl, 16)
        for hd in range(4):
            last = v.reduce_sum(al5[0:1, 72 + hd:73 + hd],
                                row5[0:1, hd * 1024:(hd + 1) * 1024],
                                axis=AX, apply_absolute_value=True)
        EG5(last)
        EG5(v.tensor_scalar_mul(al5[0:1, 72:76], al5[0:1, 72:76], 1.0 / 1024))
        v.wait_ge(bvm, 1)
        for c_ in range(8):
            last = v.reduce_sum(al5[0:1, 64 + c_:65 + c_], psBC[0:1, c_ * 8:(c_ + 1) * 8], axis=AX)
        EG5(last)
        EG5(v.tensor_scalar_mul(al5[0:1, 64:72], al5[0:1, 64:72], 1.0 / 1024))
        last = None
        for c_ in range(8):
            last = v.tensor_single_scalar(al5[0:1, c_ * 8:(c_ + 1) * 8],
                                          ctx.t["onesf"][0:1, 0:8],
                                          al5[0:1, 64 + c_:65 + c_], ALU.mult)
        last.then_inc(bvv_, 1)
        v.wait_ge(bvs, 2)
        v.wait_ge(bvm, 2)
        EG5(v.tensor_copy(abc[:, 0:64], psBC[:, 0:64]))
        v.tensor_tensor(qbp[:, 0:64], sgp[:, 0:64], abc[:, 0:64], ALU.mult).then_inc(bvv_, 1)
        v.wait_ge(bvs, 3)
        for hd in range(4):
            seg = sgr[0:1, hd * 1024:(hd + 1) * 1024]
            last = v.tensor_single_scalar(seg, seg, al5[0:1, 72 + hd:73 + hd], ALU.mult)
        last.then_inc(bvv_, 1)
        for hd in range(4):
            v.wait_ge(bvm, hd + 3)
            v.tensor_copy(vbc[:, hd * 1024:(hd + 1) * 1024], psBC[:, 0:1024]).then_inc(bvc, 1)
        for hd in range(4):
            v.wait_ge(ctx.sem("p5qlc"), 16 * (hd + 1))
            for hk in range(8):
                v.tensor_single_scalar(Qsb[:, hk * T:(hk + 1) * T],
                                       Qsb[:, hk * T:(hk + 1) * T],
                                       qbp[:, hd * 8 + hk:hd * 8 + hk + 1],
                                       ALU.mult).then_inc(qmv, 1)
            v.wait_ge(klc, 32 * (hd + 1))
            for hk in range(8):
                v.tensor_single_scalar(Ksb[:, hk * 2048:(hk + 1) * 2048],
                                       Ksb[:, hk * 2048:(hk + 1) * 2048],
                                       qbp[:, 32 + hd * 8 + hk:33 + hd * 8 + hk],
                                       ALU.mult).then_inc(kmv, 1)
            v.wait_ge(vlc, 16 * 16 * (hd + 1))
            v.wait_ge(bvc, hd + 1)
            for si in range(16):
                v.tensor_tensor(Vsb[:, si * T:(si + 1) * T], Vsb[:, si * T:(si + 1) * T],
                                vbc[:, hd * 1024:(hd + 1) * 1024], ALU.mult).then_inc(vmv, 1)
            for si in range(16):
                u = hd * 16 + si
                v.wait_ge(sev, u + 1)
                v.wait_ge(mls[u % 2], 16 * (u // 2 + 1))
                v.tensor_tensor(att[:, si * T:(si + 1) * T], att[:, si * T:(si + 1) * T],
                                msk[:, (u % 2) * T:(u % 2 + 1) * T], ALU.mult).then_inc(smk, 1)
            for hk in range(8):
                u = hd * 8 + hk
                v.wait_ge(oev, u + 1)
                v.wait_ge(xl5[u % 2], 16 * (u // 2 + 1))
                if u >= 2:
                    v.wait_ge(so5[u % 2], 16 * (u // 2))
                v.tensor_tensor(xo5[:, (u % 2) * T:(u % 2 + 1) * T],
                                xin5[:, (u % 2) * T:(u % 2 + 1) * T],
                                tmp5[:, (u % 2) * T:(u % 2 + 1) * T],
                                ALU.add).then_inc(oxv, 1)
    return {"done": [(so5[0], 16 * 16), (so5[1], 16 * 16), (ctx.sem("p5oxv"), 32),
                     (ctx.sem("p5omm"), 32), (ctx.sem("p5gct"), 1)]}



class Carver:
    def __init__(self, nc):
        total = nc.sbuf_bytes_remaining - 2048
        self.nbytes = (total // 512) * 512
        self.h = nc.alloc_sbuf_tensor("arena", [128, self.nbytes], mybir.dt.uint8)
        self.f32 = self.h.ap().bitcast(F32)
        self.bf = self.h.ap().bitcast(BF16)
        self.off = 0

    def take(self, m, name, cols, dtype, rows=128):
        sz = 4 if dtype == F32 else 2
        self.off = (self.off + 63) // 64 * 64
        e0 = self.off // sz
        m[name] = (self.f32 if dtype == F32 else self.bf)[0:rows, e0:e0 + cols]
        self.off += cols * sz
        assert self.off <= self.nbytes, (name, self.off, self.nbytes)

    def row(self, m, name, cols, dtype, part):
        sz = 4 if dtype == F32 else 2
        self.off = (self.off + 63) // 64 * 64
        e0 = self.off // sz
        m[name] = (self.f32 if dtype == F32 else self.bf)[part:part + 1, e0:e0 + cols]
        self.off += cols * sz


def carve_persistent(cv, m):
    cv.take(m, "onesb", 16, BF16)
    cv.take(m, "onesf", 128, F32)
    cv.take(m, "thr", 16, F32)
    cv.take(m, "epsv", 16, F32)
    cv.take(m, "gsv_t", 128, F32)
    cv.take(m, "usv_t", 128, F32)
    cv.take(m, "dsv_t", 32, F32)
    cv.take(m, "dsh", 32, F32)
    cv.take(m, "bin2_t", 16, F32)
    for nm in ("ln1w_t", "ln1b_t", "pzre_t", "pzim_t", "lcre_t", "lcim_t",
               "sclv_t", "ln2w_t", "ln2b_t", "b2v_t"):
        cv.take(m, nm, 16, F32)
    cv.take(m, "b1v_t", 64, F32)
    cv.take(m, "lnlw_t", 16, F32)
    cv.take(m, "lnlb_t", 16, F32)
    cv.take(m, "boutv_t", 32, F32)
    cv.take(m, "selv_t", 16, F32)
    cv.take(m, "rw_sb", 2048, BF16)
    cv.take(m, "eb_sb", 4096, BF16, rows=64)
    cv.take(m, "rb_sb", 16, F32, rows=64)
    cv.take(m, "idt", 128, BF16)
    cv.take(m, "p5qkp", 64, F32)
    cv.take(m, "p5sgp", 64, F32)
    cv.take(m, "p5abc", 64, F32)
    cv.take(m, "p5qb", 64, F32)
    cv.take(m, "p5absq", 64, BF16)
    cv.row(m, "p5al", 80, F32, 0)
    cv.take(m, "p0acc", 4, F32)
    cv.take(m, "p0accw", 192, F32)
    cv.row(m, "p0row", 8, F32, 0)


def carve_ln(cv, m):
    cv.take(m, "lnxld", 2 * T, F32)
    cv.take(m, "lnxb", 2 * T, BF16)
    cv.take(m, "lnsqb", 2 * T, BF16)
    cv.take(m, "lntmpf", 2 * T, F32)
    cv.take(m, "lnrbc", T, F32)
    cv.take(m, "lnmrbc", T, F32)
    cv.take(m, "lnrows", 4 * T, F32)


def carve_phase(kind, cv, m, quantize):
    base = cv.off
    cv.take(m, "xn", NK * T, BF16)
    ln_start = cv.off
    carve_ln(cv, m)
    ln_end = cv.off
    if kind in ("p01", "p3"):
        cv.off = ln_start
        if quantize:
            cv.take(m, "wst", 4 * 2048, F32)
            cv.take(m, "absf", 2048, F32)
            cv.take(m, "sgb", 2048, BF16)
            cv.take(m, "mbb", 2048, BF16)
        cv.take(m, "gq", 2 * 4096, BF16)
        cv.take(m, "uq", 2 * 4096, BF16)
        cv.take(m, "sgs", 2 * T, BF16)
        cv.take(m, "ups", 2 * T, BF16)
        cv.take(m, "hbf", 2 * T, BF16)
        s1_end = cv.off
        cv.off = ln_start
        cv.take(m, "hb2", 2 * 512, BF16)
        if quantize:
            cv.take(m, "dwf", 2 * 1024, F32)
            cv.take(m, "absd", 1024, F32)
            cv.take(m, "sgd", 1024, BF16)
            cv.take(m, "mbd", 1024, BF16)
        cv.take(m, "dq", 2 * 1024, BF16)
        cv.take(m, "xin2", 8 * 512, F32)
        cv.take(m, "tmp2", 2 * 512, F32)
        cv.take(m, "xo", 2 * 512, F32)
        cv.off = max(cv.off, s1_end, ln_end)
        if quantize:
            cv.take(m, "p0wbuf", 2 * 2048, F32)
    elif kind == "p2":
        cv.off = base
        cv.take(m, "p2y", 4 * T, F32)
        cv.take(m, "p2cA", 4 * T, F32)
        cv.take(m, "p2iA", 4 * T, F32)
        cv.take(m, "p2iB", 4 * T, F32)
        scan_end = cv.off
        yb, ca, ia, ib = m["p2y"], m["p2cA"], m["p2iA"], m["p2iB"]
        m["p2tm"] = yb.bitcast(F32)[:, 0:2 * T] if False else m["p2y"][:, 0:2 * T]
        m["p2xin"] = m["p2y"][:, 2 * T:4 * T]
        bb = m["p2cA"].bitcast(BF16)
        m["p2z"] = None
        e0 = m["p2cA"].offset * 2
        m["p2z"] = cv.bf[:, e0:e0 + 16 * T]
        m["p2wout"] = cv.bf[:, e0:e0 + 16384]
        e0b = m["p2iB"].offset * 2
        m["p2hb"] = cv.bf[:, e0b:e0b + 4 * T]
        m["p2sqh"] = cv.bf[:, e0b + 4 * T:e0b + 8 * T]
        cv.off = ln_start
        cv.take(m, "p2povr", 4 * T, BF16)
        cv.take(m, "p2povi", 4 * T, BF16)
        assert cv.off <= ln_start + 16 * 1024 + 4096
        cv.off = ln_end
        cv.take(m, "p2y2", 4 * T, BF16)
        cv.take(m, "p2xo", 2 * T, F32)
        cv.take(m, "p2win", 16 * 512, BF16)
        m["p2w1"] = m["p2win"]
        cv.take(m, "p2w2", 16 * 512, BF16)
        cv.take(m, "p2h", 4 * T, F32)
        cv.take(m, "p2pw", 144, F32)
        cv.take(m, "p2sc", 64, F32)
        m["p2rbc"] = m["lnrbc"]
        m["p2mrbc"] = m["lnmrbc"]
        m["p2rows"] = m["lnrows"]
    elif kind in ("p4", "p6"):
        cv.off = ln_start
        cv.take(m, "p4lgb", T, BF16, rows=64)
        cv.take(m, "p4smf", 8 * 64, F32)
        cv.take(m, "p4wkf", 8 * 64, F32)
        cv.take(m, "p4wdf4", 8 * 64, F32)
        cv.take(m, "p4wdb", 8 * 64, BF16)
        cv.take(m, "p4wdT", T, BF16, rows=64)
        cv.take(m, "p4cols", 8, F32)
        cv.row(m, "p4wrow", T, F32, 0)
        cv.take(m, "p4hws", T, F32)
        cv.take(m, "p4tmp4", 2 * T, F32)
        cv.take(m, "p4xin4", 2 * T, F32)
        cv.take(m, "p4xo4", 2 * T, F32)
        cv.off = max(cv.off, ln_end)
    elif kind == "p5":
        cv.off = ln_start
        cv.take(m, "p5stg", 4096, BF16)
        cv.row(m, "p5row", 4096, F32, 0)
        cv.row(m, "p5sgr", 4096, F32, 0)
        pre_end = cv.off
        cv.off = ln_start
        cv.take(m, "p5msk", 2 * T, BF16)
        cv.take(m, "p5tmp", 2 * T, F32)
        cv.take(m, "p5xin", 2 * T, F32)
        cv.take(m, "p5xo", 2 * T, F32)
        assert cv.off <= ln_end and pre_end <= ln_end
        cv.off = base
        cv.take(m, "p5K", 8 * 2048, BF16)
        cv.take(m, "p5V", 16 * T, BF16)
        assert cv.off <= ln_start + 2048
        cv.off = ln_end
        cv.take(m, "p5att", 16 * T, BF16)
        cv.take(m, "p5Q", 8 * T, BF16)
        cv.take(m, "p5vbc", 4 * T, BF16)
    print(f"  carve {kind}: peak {cv.off} / {cv.nbytes}")


GLOBAL_SEMS = ("prm", "ini", "pbar", "bnd")


class BuildCtx(Ctx):
    def __init__(self, nc, stack, npool=64):
        super().__init__(nc, stack)
        self.pool = [stack.enter_context(nc.semaphore(f"sp{i}")) for i in range(npool)]
        self.gsem = {g: stack.enter_context(nc.semaphore(g)) for g in GLOBAL_SEMS}
        self.phase_map = {}
        self.rec = None
        self.known = None

    def begin_phase(self, known_names):
        self.ecnt = {}
        self.known = sorted(set(known_names))
        assert len(self.known) <= len(self.pool), len(self.known)
        self.phase_map = {nm: self.pool[i] for i, nm in enumerate(self.known)}

    def sem(self, name):
        if name in self.gsem:
            return self.gsem[name]
        if self.phase_map is None or name not in self.phase_map:
            if self.known is None:
                self.collect_names.add(name)
                return self.pool[len(self.collect_names) % len(self.pool)]
            raise KeyError(f"sem {name} not registered for this phase")
        return self.phase_map[name]


class RecEng:

    def __init__(self, real, rec):
        self._real = real
        self._rec = rec

    def __getattr__(self, name):
        attr = getattr(self._real, name)
        if not callable(attr):
            return attr
        rec = self._rec

        def call(*a, **k):
            r = attr(*a, **k)
            if hasattr(r, "then_inc"):
                return RecInst(r, rec)
            return r
        return call


class RecInst:
    def __init__(self, real, rec):
        self._real = real
        self._rec = rec

    def then_inc(self, sem, n=1):
        self._rec[sem.name] = self._rec.get(sem.name, 0) + n
        r = self._real.then_inc(sem, n)
        return RecInst(r, self._rec) if hasattr(r, "then_inc") else r

    def __getattr__(self, name):
        return getattr(self._real, name)


def collect_phase_sems(kind, opts):
    nc = bass.Bass("TRN2", target_bir_lowering=False, debug=False)
    stack = ExitStack()
    with stack:
        ctx = BuildCtx(nc, stack, npool=8)
        ctx.known = None
        ctx.collect_names = set()
        ctx.phase_map = None
        cv = Carver(nc)
        m = {}
        carve_persistent(cv, m)
        mark = cv.off
        carve_phase(kind, cv, m, opts.get("quantize", True))
        declare_psum(ctx, nc, m)
        declare_dram(nc, m, [kind], opts, fused=False)
        ctx.t = m
        xi, xo = m["d_xin"], m["d_xout"]
        for engname in ENGS:
            eng = getattr(nc, "sync" if engname == "sync" else engname)
            setattr(ctx.e, engname, eng)
            try:
                emit_kind(ctx, kind, engname, xi, xo, opts)
            except KeyError:
                raise
        return set(ctx.collect_names)


def declare_psum(ctx, nc, m):
    before = set(m)
    pools = {}
    for nm in ("poolA", "poolB", "poolC", "poolD"):
        h = nc.alloc_psum_tensor(nm, [128, 1024], F32)
        pools[nm] = h
        m[nm] = h.ap()
    pA, pB, pC, pD = (m["poolA"], m["poolB"], m["poolC"], m["poolD"])
    m["p0ps"] = pA[:, 0:8]
    m["lnpsS"] = pA[0:1, :]
    m["lnpsQ"] = pB[0:1, :]
    m["lnpsB1"] = pC
    m["lnpsB2"] = pD
    m["psG0"], m["psG1"] = pA, pB
    m["psU0"], m["psU1"] = pC, pD
    for kg in range(8):
        pool = (pA, pB, pC, pD)[kg // 2]
        m[f"psD{kg}"] = pool[:, (kg % 2) * 512:(kg % 2 + 1) * 512]
    m["p2psA"], m["p2psB"] = pA, pB
    m["p2s_psS"] = pC[0:1, :]
    m["p2s_psQ"] = pD[0:1, :]
    bA = pools["poolA"].ap().bitcast(BF16)
    bC = pools["poolC"].ap().bitcast(BF16)
    bD = pools["poolD"].ap().bitcast(BF16)
    m["p4psL"] = pA[0:64, :]
    m["p4psT"] = bC[:, 0:512]
    m["p4psTb"] = bD[0:64, 0:128]
    m["p4psR"] = pB[0:1, :]
    m["p4psW"] = pC
    m["p4psM0"], m["p4psM1"] = pA, pB
    m["p5psTT"] = bA[:, 0:512]
    m["p5psBC"] = pB
    m["p5psS0"], m["p5psS1"] = pA, pB
    m["p5psO0"], m["p5psO1"] = pC, pD
    return [k for k in m if k not in before]


NEEDS = {
    "p01": ["x", "wg", "wu", "wd", "wslgu", "wsld", "gsv", "usv", "dsv"],
    "p3": ["x", "gsv", "usv", "dsv"],
    "p2": ["x", "winT", "w1T", "w2T", "woutT", "bin2", "ln1w", "ln1b", "pzre",
           "pzim", "lcre", "lcim", "sclv", "ln2w", "ln2b", "b1v", "b2v",
           "lnlw", "lnlb", "boutv", "selv"],
    "p4": ["x", "rwT", "ebw", "rbv", "ident"],
    "p6": ["x", "rwT", "ebw", "rbv", "ident"],
    "p5": ["x", "bvq", "bvk", "bvv", "bvqp", "bvkp", "mask", "ident"],
}
SHAPES = {
    "wg": ([NFC, 128, NK * 128], F32), "wu": ([NFC, 128, NK * 128], F32),
    "wd": ([4, NFC, 128, 1024], F32),
    "wslgu": ([32, 128, 4096], F32), "wsld": ([64, 128, 1024], F32),
    "gsv": ([128, 128], F32), "usv": ([128, 128], F32), "dsv": ([128, 32], F32),
    "winT": ([4096, 512], F32), "w1T": ([4, 128, 8192], F32),
    "w2T": ([4, 128, 8192], F32), "woutT": ([128, 16384], F32),
    "bin2": ([128, 4], F32), "ln1w": ([128, 16], F32), "ln1b": ([128, 16], F32),
    "pzre": ([128, 16], F32), "pzim": ([128, 16], F32), "lcre": ([128, 16], F32),
    "lcim": ([128, 16], F32), "sclv": ([128, 16], F32), "ln2w": ([128, 16], F32),
    "ln2b": ([128, 16], F32), "b1v": ([128, 64], F32), "b2v": ([128, 16], F32),
    "lnlw": ([128, 4], F32), "lnlb": ([128, 4], F32), "boutv": ([128, 32], F32),
    "selv": ([128, 3], F32),
    "rwT": ([4096, 64], F32), "ebw": ([64, 4096], F32), "rbv": ([64, 1], F32),
    "bvq": ([1, 4096], F32), "bvk": ([1, 4096], F32), "bvv": ([1, 4096], F32),
    "bvqp": ([128, 32], F32), "bvkp": ([128, 32], F32),
    "mask": ([2048, 1024], BF16), "ident": ([128, 128], BF16),
}
PARAM_LOADS = [
    ("gsv", "gsv_t"), ("usv", "usv_t"), ("dsv", "dsv_t"),
    ("bin2", "bin2_t"), ("ln1w", "ln1w_t"), ("ln1b", "ln1b_t"),
    ("pzre", "pzre_t"), ("pzim", "pzim_t"), ("lcre", "lcre_t"),
    ("lcim", "lcim_t"), ("sclv", "sclv_t"), ("ln2w", "ln2w_t"),
    ("ln2b", "ln2b_t"), ("b1v", "b1v_t"), ("b2v", "b2v_t"), ("lnlw", "lnlw_t"),
    ("lnlb", "lnlb_t"), ("boutv", "boutv_t"), ("selv", "selv_t"),
    ("ident", "idt"),
]


def declare_dram(nc, m, kinds, opts, fused):
    need = set()
    for k in kinds:
        need.update(NEEDS[k])
    for nm in sorted(need):
        if nm == "x":
            continue
        shp, dt = SHAPES[nm]
        m["d_" + nm] = nc.dram_tensor(nm, shp, dt, kind="ExternalInput").ap()
    m["d_xext"] = nc.dram_tensor("x", [D, T], F32, kind="ExternalInput").ap()
    m["d_y"] = nc.dram_tensor("y", [D, T], F32, kind="ExternalOutput").ap()
    if fused:
        m["d_xstate"] = nc.dram_tensor("xstate", [D, T], F32).ap()
    m["d_xin"] = m["d_xext"]
    m["d_xout"] = m["d_y"]
    dbg = "ExternalOutput" if DEBUG_OUTS else "Internal"
    if any(k in ("p01", "p3") for k in kinds):
        m["d_h"] = nc.dram_tensor("hstate", [NFC, 128, T], BF16, kind=dbg).ap()
        m["d_wqg"] = nc.dram_tensor("wqg", [NFC, 128, NK * 128], BF16, kind=dbg).ap()
        m["d_wqu"] = nc.dram_tensor("wqu", [NFC, 128, NK * 128], BF16).ap()
        m["d_wqd"] = nc.dram_tensor("wqd", [4, NFC, 128, 1024], BF16).ap()
        m["d_thr_in"] = nc.dram_tensor("thr_in", [1, 8], F32).ap()
        m["d_thr_out"] = nc.dram_tensor("thr_out", [1, 8], F32).ap()
    if "p2" in kinds:
        m["d_sc_in"] = nc.dram_tensor("sc_in", [128, 8], F32).ap()
        m["d_sc_out"] = nc.dram_tensor("sc_out", [256, 8], F32).ap()
    if "p5" in kinds:
        m["d_gin_fm"] = nc.dram_tensor("gin_fm", [D, T], BF16).ap()
        m["d_g_fm"] = nc.dram_tensor("g_fm", [2 * D, T], BF16).ap()
        m["d_gin_tok"] = nc.dram_tensor("gin_tok", [T, D], BF16).ap()
        m["d_g_tok"] = nc.dram_tensor("g_tok", [2 * T, D], BF16).ap()


def emit_kind(ctx, kind, eng, xi, xo, opts):
    if kind == "p01":
        emit_p0(ctx, eng)
        emit_ln(ctx, "ln", eng, xi, ctx.t["xn"])
        emit_ffn_s1(ctx, "", eng, True, opts.get("save", True))
        emit_ffn_s2(ctx, "", eng, True, opts.get("save", True), xi, xo)
    elif kind == "p3":
        emit_ln(ctx, "ln", eng, xi, ctx.t["xn"])
        emit_ffn_s1(ctx, "", eng, False, False)
        emit_ffn_s2(ctx, "", eng, False, False, xi, xo)
    elif kind == "p2":
        emit_p2(ctx, eng, xi, xo)
    elif kind in ("p4", "p6"):
        emit_p4(ctx, eng, xi, xo, opts.get("load_weights", True))
    elif kind == "p5":
        emit_p5(ctx, eng, xi, xo)


_SEM_CACHE = {}
_LAST_MAPS = None
DEBUG_OUTS = False


def build_program(kinds_opts, fused):
    for kind, opts in kinds_opts:
        key = (kind, tuple(sorted(opts.items())))
        if key not in _SEM_CACHE:
            _SEM_CACHE[key] = collect_phase_sems(kind, dict(opts))
    nc = bass.Bass("TRN2", target_bir_lowering=False, debug=False)
    stack = ExitStack()
    ctx = BuildCtx(nc, stack)
    cv = Carver(nc)
    pm = {}
    carve_persistent(cv, pm)
    mark = cv.off
    maps = []
    for kind, opts in kinds_opts:
        m = dict(pm)
        cv.off = mark
        carve_phase(kind, cv, m, opts.get("quantize", kind == "p01"))
        maps.append(m)
    global _LAST_MAPS
    _LAST_MAPS = maps
    pskeys = declare_psum(ctx, nc, pm)
    for m in maps:
        for k2 in pskeys:
            m[k2] = pm[k2]
    dm = {}
    declare_dram(nc, dm, [k for k, _ in kinds_opts], {}, fused)
    for m in maps:
        m.update(dm)
    input_names = sorted(set(nm for k, _ in kinds_opts for nm in NEEDS[k]))
    prm_total = 16 * len([1 for nm, _t in PARAM_LOADS if "d_" + nm in dm])
    ini_total = 4
    nph = len(kinds_opts)
    chain = []
    for i in range(nph):
        xi = dm["d_xext"] if i == 0 else dm["d_xstate"] if fused else dm["d_xext"]
        xo = dm["d_y"] if i == nph - 1 else dm["d_xstate"] if fused else dm["d_y"]
        chain.append((xi, xo))
    finals = [dict() for _ in range(nph)]
    pbar = ctx.gsem["pbar"]
    prm = ctx.gsem["prm"]
    ini = ctx.gsem["ini"]
    bnd = ctx.gsem["bnd"]
    pool_nums = sorted(h.num for h in ctx.pool)
    pool_range = range(pool_nums[0], pool_nums[-1] + 1)
    assert pool_nums == list(pool_range)

    def emit_engine(engname, eng):
        for pi, (kind, opts) in enumerate(kinds_opts):
            key = (kind, tuple(sorted(opts.items())))
            ctx.begin_phase(_SEM_CACHE[key])
            ctx.t = maps[pi]
            ctx.rec = finals[pi]
            reng = RecEng(eng, finals[pi])
            setattr(ctx.e, engname, reng)
            if pi == 0:
                eng.wait_ge(prm, prm_total)
                eng.wait_ge(ini, ini_total)
            else:
                if engname == "sync":
                    handles = {h.name: h for h in ctx.pool}
                    for snm, val in sorted(finals[pi - 1].items()):
                        if snm in handles:
                            eng.wait_ge(handles[snm], val)
                    eng.sem_inc(bnd, 1)
                    eng.wait_ge(pbar, pi)
                elif engname == "gpsimd":
                    eng.wait_ge(bnd, pi)
                    eng.dma_reset(pool_range)
                    eng.sem_clear(pool_range)
                    eng.sem_inc(pbar, 1)
                else:
                    eng.wait_ge(pbar, pi)
            xi, xo = chain[pi]
            emit_kind(ctx, kind, engname, xi, xo, dict(opts))
        if engname == "sync":
            for snm, val in sorted(finals[nph - 1].items()):
                for h in ctx.pool:
                    if h.name == snm:
                        eng.wait_ge(h, val)

    with nc.Block() as block:
        @block.tensor
        def _(te):
            emit_engine("tensor", te)

        @block.vector
        def _(v):
            v.memset(pm["onesb"][:, 0:1], 1.0).then_inc(ini, 1)
            v.memset(pm["epsv"][:], EPS).then_inc(ini, 1)
            v.memset(pm["onesf"][:], 1.0).then_inc(ini, 1)
            v.wait_ge(prm, prm_total)
            if "d_dsv" in dm:
                v.tensor_scalar_mul(pm["dsh"][:, 0:32], pm["dsv_t"][:, 0:32],
                                    0.5).then_inc(ini, 1)
            else:
                v.memset(pm["dsh"][:, 0:32], 0.0).then_inc(ini, 1)
            emit_engine("vector", v)

        @block.scalar
        def _(sc):
            emit_engine("scalar", sc)

        @block.gpsimd
        def _(g):
            for nm, tgt in PARAM_LOADS:
                if "d_" + nm in dm:
                    shp = SHAPES[nm][0]
                    src = dm["d_" + nm]
                    dst = pm[tgt][0:shp[0], 0:shp[1]]
                    g.dma_start(out=dst, in_=src).then_inc(prm, 16)
            emit_engine("gpsimd", g)

        @block.sync
        def _(sy):
            emit_engine("sync", sy)

    return nc, input_names, pm



UNFUSED = False
PERCORE_NAMES = {"x", "wslgu", "wsld", "mask", "selv"}
_PROG_CACHE = {}


def _get_prog(spec, fused):
    key = (tuple((k, o) for k, o in spec), fused)
    if key not in _PROG_CACHE:
        _PROG_CACHE[key] = build_program([(k, dict(o)) for k, o in spec], fused)
    return _PROG_CACHE[key]


UNFUSED_LAUNCHES = [
    [("p01", (("save", True),))],
    [("p2", ())],
    [("p01", (("save", True),))],
    [("p4", (("load_weights", True),))],
    [("p5", ())],
    [("p4", (("load_weights", True),))],
]
FUSED_SPEC = [
    ("p01", (("save", True),)),
    ("p2", ()),
    ("p3", (("quantize", False),)),
    ("p4", (("load_weights", True),)),
    ("p5", ()),
    ("p6", (("load_weights", False),)),
]


def kernel(**inputs):
    sh, per_core = prep_host(inputs)
    xs = [pc["x"] for pc in per_core]
    if UNFUSED:
        for spec in UNFUSED_LAUNCHES:
            nc, names, _pm = _get_prog(tuple(spec), False)
            in_maps = []
            for c in range(8):
                im = {}
                for nm in names:
                    if nm == "x":
                        im["x"] = xs[c]
                    elif nm in PERCORE_NAMES:
                        im[nm] = per_core[c][nm]
                    else:
                        im[nm] = sh[nm]
                in_maps.append(im)
            res = run_bass_kernel_spmd(nc, in_maps, core_ids=list(range(8)))
            xs = [np.asarray(res.results[c]["y"]) for c in range(8)]
    else:
        nc, names, _pm = _get_prog(tuple(FUSED_SPEC), True)
        in_maps = []
        for c in range(8):
            im = {}
            for nm in names:
                if nm == "x":
                    im["x"] = xs[c]
                elif nm in PERCORE_NAMES:
                    im[nm] = per_core[c][nm]
                else:
                    im[nm] = sh[nm]
            in_maps.append(im)
        res = run_bass_kernel_spmd(nc, in_maps, core_ids=list(range(8)))
        xs = [np.asarray(res.results[c]["y"]) for c in range(8)]
    out = np.empty((4, 2048, 4096), np.float32)
    for c in range(8):
        b, j = c // 2, c % 2
        out[b, j * T:(j + 1) * T, :] = xs[c].T
    return out


# version 05 (final)
# speedup vs baseline: 1.0431x; 1.0431x over previous
from contextlib import ExitStack

import numpy as np
import ml_dtypes
import concourse.bass as bass
import concourse.mybir as mybir
from concourse.bass_utils import run_bass_kernel_spmd

F32 = mybir.dt.float32
BF16 = mybir.dt.bfloat16
AF = mybir.ActivationFunctionType
ALU = mybir.AluOpType
AX = mybir.AxisListType.X

D = 4096
T = 1024
NK = D // 128
FF = 16384
NFC = FF // 128
EPS = 1e-5
P0_NGU = 2 * NFC // 8
P0_ND = 4 * NFC // 8
BF = ml_dtypes.bfloat16
ENGS = ("sync", "tensor", "vector", "scalar", "gpsimd")



def _tile_big(w):
    wT = np.ascontiguousarray(w.T)
    t = wT.reshape(NK, 128, NFC, 128)
    return np.ascontiguousarray(t.transpose(2, 1, 0, 3).reshape(NFC, 128, NK * 128))


def _tile_down(w):
    wT = np.ascontiguousarray(w.T)
    t = wT.reshape(NFC, 128, 4, 8, 128)
    return np.ascontiguousarray(t.transpose(2, 0, 1, 3, 4).reshape(4, NFC, 128, 1024))


def _pcol(v, ncol):
    return np.ascontiguousarray(np.asarray(v, np.float32).reshape(ncol, 128).T)


def _pcol2(v):
    v = np.asarray(v, np.float32)
    L = v.shape[0]
    return np.ascontiguousarray(v.reshape(L, 4, 128).transpose(2, 0, 1).reshape(128, L * 4))


def prep_host(inputs):
    f = np.float32
    sh = {}
    sh["wg"] = _tile_big(np.asarray(inputs["L0_ffn_gate_w"], f))
    sh["wu"] = _tile_big(np.asarray(inputs["L0_ffn_up_w"], f))
    sh["wd"] = _tile_down(np.asarray(inputs["L0_ffn_down_w"], f))
    sh["gsv"] = _pcol(inputs["L0_ffn_gate_scale"].ravel(), NFC)
    sh["usv"] = _pcol(inputs["L0_ffn_up_scale"].ravel(), NFC)
    sh["dsv"] = _pcol(inputs["L0_ffn_down_scale"].ravel(), NK)
    W1 = np.asarray(inputs["L0_attn_W1"], f)
    W2 = np.asarray(inputs["L0_attn_W2"], f)
    sh["winT"] = np.ascontiguousarray(np.asarray(inputs["L0_attn_Win"], f).T)
    sh["w1T"] = np.ascontiguousarray(
        W1.transpose(0, 2, 1).reshape(4, 4, 128, 2048).transpose(0, 2, 1, 3).reshape(4, 128, 8192))
    sh["w2T"] = np.ascontiguousarray(
        W2.transpose(0, 2, 1).reshape(4, 16, 128, 512).transpose(0, 2, 1, 3).reshape(4, 128, 8192))
    sh["woutT"] = np.ascontiguousarray(
        np.asarray(inputs["L0_attn_Wout"], f).T.reshape(4, 128, 4096).transpose(1, 0, 2).reshape(128, 16384))
    sh["bin2"] = _pcol(inputs["L0_attn_bin"], 4)
    for nm, key in [("ln1w", "L0_attn_ln1_w"), ("ln1b", "L0_attn_ln1_b"),
                    ("pzre", "L0_attn_phz_re"), ("pzim", "L0_attn_phz_im"),
                    ("lcre", "L0_attn_lc_re"), ("lcim", "L0_attn_lc_im"),
                    ("sclv", "L0_attn_sc_lin"), ("ln2w", "L0_attn_ln2_w"),
                    ("ln2b", "L0_attn_ln2_b"), ("b2v", "L0_attn_b2")]:
        sh[nm] = _pcol2(inputs[key])
    sh["b1v"] = np.ascontiguousarray(
        np.asarray(inputs["L0_attn_b1"], f).reshape(4, 16, 128).transpose(2, 0, 1).reshape(128, 64))
    sh["lnlw"] = _pcol(inputs["L0_attn_lnl_w"], 4)
    sh["lnlb"] = _pcol(inputs["L0_attn_lnl_b"], 4)
    sh["boutv"] = _pcol(inputs["L0_attn_bout"], NK)
    sh["rwT"] = np.ascontiguousarray(np.asarray(inputs["L1_ffn_router_weight"], f).T)
    sh["ebw"] = np.asarray(inputs["L1_ffn_expert_bias"], f)
    sh["rbv"] = np.ascontiguousarray(np.asarray(inputs["L1_ffn_router_bias"], f).reshape(64, 1))
    sh["bvq"] = np.asarray(inputs["L1_attn_bv_q"], f).reshape(1, 4096).copy()
    sh["bvqp"] = np.ascontiguousarray(
        np.asarray(inputs["L1_attn_bv_q"], f).reshape(4, 8, 128).transpose(2, 0, 1).reshape(128, 32))
    sh["bvkp"] = np.ascontiguousarray(
        np.asarray(inputs["L1_attn_bv_k"], f).reshape(4, 8, 128).transpose(2, 0, 1).reshape(128, 32))
    sh["bvk"] = np.asarray(inputs["L1_attn_bv_k"], f).reshape(1, 4096).copy()
    sh["bvv"] = np.asarray(inputs["L1_attn_bv_v"], f).reshape(1, 4096).copy()
    sh["ident"] = np.eye(128, dtype=BF)
    wgu = np.concatenate([sh["wg"], sh["wu"]], axis=0)
    wdf = sh["wd"].reshape(512, 128, 1024)
    x_full = np.asarray(inputs["L0_attn_x"], f)
    per_core = []
    for c in range(8):
        b, j = c // 2, c % 2
        d = {"x": np.ascontiguousarray(x_full[b, j * T:(j + 1) * T, :].T)}
        hgu = P0_NGU // 2
        d["wslgu"] = np.ascontiguousarray(np.concatenate(
            [sh["wg"][c * hgu:(c + 1) * hgu], sh["wu"][c * hgu:(c + 1) * hgu]], axis=0))
        d["wsld"] = np.ascontiguousarray(wdf[c * P0_ND:(c + 1) * P0_ND])
        s_idx = np.arange(2048)[:, None]
        t_idx = np.arange(T)[None, :] + j * T
        d["mask"] = (s_idx <= t_idx).astype(BF)
        sel = np.zeros((128, 3), f)
        sel[:, 0] = 1.0 if j == 0 else 0.0
        sel[:, 1] = 0.0 if j == 0 else 1.0
        d["selv"] = sel
        per_core.append(d)
    return sh, per_core



class Eng:
    pass


class Ctx:
    def __init__(self, nc, stack):
        self.nc = nc
        self.stack = stack
        self.t = {}
        self.sems = {}
        self.fence = []
        self.e = Eng()
        self.ecnt = {}

    def sem(self, name):
        if name not in self.sems:
            self.sems[name] = self.stack.enter_context(self.nc.semaphore(name))
        return self.sems[name]

    def sb(self, stack, name, shape, dtype):
        h = stack.enter_context(self.nc.sbuf_tensor(name, shape, dtype))
        self.t[name] = h
        return h

    def ps(self, stack, name, shape, dtype=F32):
        h = stack.enter_context(self.nc.psum_tensor(name, shape, dtype))
        self.t[name] = h
        return h

    def emit_fence(self, eng_obj):
        pass

    def edge(self, eng_obj, inst, name="veg"):
        sem = self.sem(name)
        key = (name, id(eng_obj.__class__))
        self.ecnt[name] = self.ecnt.get(name, 0) + 1
        inst.then_inc(sem, 1)
        eng_obj.wait_ge(sem, self.ecnt[name])



def emit_ln(ctx, q, eng, x_src, xn):
    e = ctx.e
    lx = [ctx.sem(q + f"lx{i}") for i in range(2)]
    lx2 = [ctx.sem(q + f"ly{i}") for i in range(2)]
    cv, cs, sm = ctx.sem(q + "cv"), ctx.sem(q + "cs"), ctx.sem(q + "sm")
    vf, sf, bm, bc = ctx.sem(q + "vf"), ctx.sem(q + "sf"), ctx.sem(q + "bm"), ctx.sem(q + "bc")
    xnv = ctx.sem(q + "xnv")
    xld = ctx.t["lnxld"]; xb = ctx.t["lnxb"]; sq = ctx.t["lnsqb"]
    tmpf = ctx.t["lntmpf"]; rows = ctx.t["lnrows"]
    rbc = ctx.t["lnrbc"]; mrbc = ctx.t["lnmrbc"]
    psS, psQ = ctx.t["lnpsS"], ctx.t["lnpsQ"]
    psB1, psB2 = ctx.t["lnpsB1"], ctx.t["lnpsB2"]
    onesb, onesf = ctx.t["onesb"], ctx.t["onesf"]
    inv_n = 1.0 / D
    R = lambda i: rows[0:1, i * T:(i + 1) * T]

    if eng == "sync":
        sy = e.sync
        for k in range(NK):
            sl = k % 2
            if k >= 2:
                sy.wait_ge(sm, k - 1)
            sy.dma_start(out=xld[:, sl * T:(sl + 1) * T],
                         in_=x_src[k * 128:(k + 1) * 128, :]).then_inc(lx[sl], 16)
        for k in range(NK):
            sl = k % 2
            if k >= 2:
                sy.wait_ge(xnv, k - 1)
            else:
                sy.wait_ge(bc, 2)
            sy.dma_start(out=xld[:, sl * T:(sl + 1) * T],
                         in_=x_src[k * 128:(k + 1) * 128, :]).then_inc(lx2[sl], 16)
    elif eng == "vector":
        v = e.vector
        for k in range(NK):
            sl = k % 2
            v.wait_ge(lx[sl], 16 * (k // 2 + 1))
            if k >= 2:
                v.wait_ge(sm, k - 1)
            v.tensor_copy(xb[:, sl * T:(sl + 1) * T],
                          xld[:, sl * T:(sl + 1) * T]).then_inc(cv, 1)
        v.wait_ge(sm, NK)
        v.tensor_scalar_mul(R(0), psS[0:1, :], inv_n)
        v.tensor_scalar_mul(R(1), psQ[0:1, :], inv_n)
        ctx.edge(v, v.tensor_tensor(R(2), R(0), R(0), ALU.mult), q + "ve")
        v.tensor_tensor(R(1), R(1), R(2), ALU.subtract).then_inc(vf, 1)
        v.wait_ge(sf, 1)
        ctx.edge(v, v.reciprocal(R(2), R(1)), q + "ve")
        v.tensor_tensor(R(3), R(0), R(2), ALU.mult).then_inc(vf, 1)
        v.wait_ge(bm, 1)
        v.tensor_copy(rbc[:], psB1[:])
        v.tensor_copy(mrbc[:], psB2[:]).then_inc(bc, 2)
        for k in range(NK):
            sl = k % 2
            v.wait_ge(lx2[sl], 16 * (k // 2 + 1))
            v.tensor_tensor(tmpf[:, sl * T:(sl + 1) * T],
                            xld[:, sl * T:(sl + 1) * T], rbc[:], ALU.mult)
            v.tensor_tensor(xn[:, k * T:(k + 1) * T],
                            tmpf[:, sl * T:(sl + 1) * T], mrbc[:],
                            ALU.subtract).then_inc(xnv, 1)
    elif eng == "scalar":
        sc = e.scalar
        for k in range(NK):
            sl = k % 2
            sc.wait_ge(lx[sl], 16 * (k // 2 + 1))
            if k >= 2:
                sc.wait_ge(sm, k - 1)
            sc.activation(sq[:, sl * T:(sl + 1) * T],
                          xld[:, sl * T:(sl + 1) * T], AF.Square).then_inc(cs, 1)
        sc.wait_ge(vf, 1)
        sc.activation(R(1), R(1), AF.Sqrt, bias=ctx.t["epsv"][0:1, 0:1]).then_inc(sf, 1)
    elif eng == "tensor":
        te = e.tensor
        for k in range(NK):
            sl = k % 2
            te.wait_ge(cv, k + 1)
            te.wait_ge(cs, k + 1)
            for h in range(2):
                te.matmul(psS[0:1, h * 512:(h + 1) * 512], onesb[:, 0:1],
                          xb[:, sl * T + h * 512:sl * T + (h + 1) * 512],
                          start=(k == 0), stop=(k == NK - 1))
            for h in range(2):
                last = te.matmul(psQ[0:1, h * 512:(h + 1) * 512], onesb[:, 0:1],
                                 sq[:, sl * T + h * 512:sl * T + (h + 1) * 512],
                                 start=(k == 0), stop=(k == NK - 1))
            last.then_inc(sm, 1)
        te.wait_ge(vf, 2)
        for h in range(2):
            te.matmul(psB1[:, h * 512:(h + 1) * 512], onesf[0:1, :],
                      rows[0:1, 2 * T + h * 512:2 * T + (h + 1) * 512], start=True, stop=True)
        for h in range(2):
            last = te.matmul(psB2[:, h * 512:(h + 1) * 512], onesf[0:1, :],
                             rows[0:1, 3 * T + h * 512:3 * T + (h + 1) * 512], start=True, stop=True)
        last.then_inc(bm, 1)
    return None


def emit_p0(ctx, eng):
    e = ctx.e
    lw = [ctx.sem(f"p0lw{i}") for i in range(2)]
    rd = ctx.sem("p0rd")
    pr = ctx.sem("p0pr")
    mmd = ctx.sem("p0mm")
    ccd = ctx.sem("p0cc")
    thv = ctx.sem("p0thv")
    wslgu = ctx.t["d_wslgu"]; wsld = ctx.t["d_wsld"]
    wbuf = ctx.t["p0wbuf"]; acc = ctx.t["p0acc"]; accw = ctx.t["p0accw"]
    row = ctx.t["p0row"]
    ps = ctx.t["p0ps"]; thr = ctx.t["thr"]
    onesf = ctx.t["onesf"]
    cin, cout = ctx.t["d_thr_in"], ctx.t["d_thr_out"]
    NW = float(FF * D)

    chunks = [(0 if i < P0_NGU // 2 else 1, wslgu[i][:, hf * 2048:(hf + 1) * 2048], 2048)
              for i in range(P0_NGU) for hf in range(2)] + \
             [(2, wsld[i], 1024) for i in range(P0_ND)]

    if eng == "sync":
        sy = e.sync
        for i, (_, ap, w) in enumerate(chunks):
            s = i % 2
            if i >= 2:
                sy.wait_ge(rd, i - 1)
            sy.dma_start(out=wbuf[:, s * 2048:s * 2048 + w], in_=ap).then_inc(lw[s], 16)
    elif eng == "vector":
        v = e.vector
        rz = ctx.sem("p0rz")
        v.memset(row[:], 0.0).then_inc(rz, 1)
        bounds = [0]
        last = None
        for i, (col, _, w) in enumerate(chunks):
            s = i % 2
            if col >= len(bounds):
                bounds.append(i)
            v.wait_ge(lw[s], 16 * (i // 2 + 1))
            last = v.reduce_sum(accw[:, i:i + 1], wbuf[:, s * 2048:s * 2048 + w],
                                axis=AX, apply_absolute_value=True)
            if i < len(chunks) - 1:
                last.then_inc(rd, 1)
        bounds.append(len(chunks))
        ctx.edge(v, last)
        fin = ctx.sem("p0fin")
        for col in range(3):
            v.reduce_sum(acc[:, col:col + 1],
                         accw[:, bounds[col]:bounds[col + 1]], axis=AX).then_inc(fin, 1)
        v.wait_ge(mmd, 1)
        v.wait_ge(rz, 1)
        v.tensor_scalar_mul(row[0:1, 0:3], ps[0:1, 0:3], 0.7 / NW).then_inc(pr, 1)
        v.wait_ge(mmd, 2)
        v.tensor_copy(thr[:, 0:4], ps[:, 0:4])
        v.tensor_scalar_mul(thr[:, 4:8], ps[:, 0:4], -1.0).then_inc(thv, 1)
    elif eng == "tensor":
        te = e.tensor
        te.wait_ge(ctx.sem("p0fin"), 3)
        te.matmul(ps[0:1, 0:3], onesf[:, 0:1], acc[:, 0:3], start=True, stop=True).then_inc(mmd, 1)
        te.wait_ge(ccd, 32)
        te.matmul(ps[:, 0:8], onesf[0:1, :], row[0:1, 0:8],
                  start=True, stop=True).then_inc(mmd, 1)
    elif eng == "gpsimd":
        g = e.gpsimd
        ccv = ctx.sem("p0ccv")
        g.wait_ge(pr, 1)
        g.dma_start(out=cin, in_=row[0:1, :]).then_inc(ccd, 16)
        g.wait_ge(ccd, 16)
        g.collective_compute("AllReduce", ALU.add,
                             replica_groups=[list(range(8))],
                             ins=[cin], outs=[cout]).then_inc(ccv, 1)
        g.wait_ge(ccv, 1)
        g.dma_start(out=row[0:1, :], in_=cout).then_inc(ccd, 16)
    return {"done": [(thv, 1)]}



def emit_ffn_s1(ctx, ph, eng, quantize, save):
    e = ctx.e
    q = "s1"
    lg = [ctx.sem(q + f"lg{i}") for i in range(4)]
    lu = [ctx.sem(q + f"lu{i}") for i in range(4)]
    qd = ctx.sem(q + "qd")
    sq1 = ctx.sem(q + "sq1")
    mm = ctx.sem(q + "mm")
    ev = ctx.sem(q + "ev")
    hv = ctx.sem(q + "hv")
    sgq = [ctx.sem(q + f"sgq{i}") for i in range(2)]
    squ = [ctx.sem(q + f"squ{i}") for i in range(2)]
    sh = [ctx.sem(q + f"sh{i}") for i in range(2)]
    xnv = ctx.sem("lnxnv")
    thv = ctx.sem("p0thv")

    xn = ctx.t["xn"]
    wg, wu = ctx.t.get("d_wg"), ctx.t.get("d_wu")
    wqg, wqu = ctx.t["d_wqg"], ctx.t["d_wqu"]
    h_state = ctx.t["d_h"]
    gq, uq = ctx.t["gq"], ctx.t["uq"]
    sgs, ups, hbf = ctx.t["sgs"], ctx.t["ups"], ctx.t["hbf"]
    psG = [ctx.t["psG0"], ctx.t["psG1"]]
    psU = [ctx.t["psU0"], ctx.t["psU1"]]
    gsv, usv = ctx.t["gsv_t"], ctx.t["usv_t"]
    thr = ctx.t["thr"]
    if quantize:
        wst = ctx.t["wst"]
        mbb = ctx.t["mbb"]; mb2 = ctx.t["mb2"]

    def unit(fc, u):
        return 4 * fc + u

    if eng == "sync":
        sy = e.sync
        sy.wait_ge(xnv, NK)
        if quantize:
            sy.wait_ge(thv, 1)
        for fc in range(NFC):
            for u in range(4):
                mat = 0 if u < 2 else 1
                hf = u % 2
                if quantize:
                    slot = unit(fc, u) % 4
                    if fc >= 1:
                        sy.wait_ge(qd, unit(fc, u) - 3)
                    src = (wg if mat == 0 else wu)[fc][:, hf * 2048:(hf + 1) * 2048]
                    sem = lg[slot]
                    sy.dma_start(out=wst[:, slot * 2048:(slot + 1) * 2048],
                                 in_=src).then_inc(sem, 16)
                else:
                    if u % 2 == 1:
                        continue
                    s2 = fc % 2
                    if fc >= 2:
                        sy.wait_ge(mm, fc - 1)
                    if mat == 0:
                        sy.dma_start(out=gq[:, s2 * 4096:(s2 + 1) * 4096],
                                     in_=wqg[fc]).then_inc(lg[s2], 16)
                    else:
                        sy.dma_start(out=uq[:, s2 * 4096:(s2 + 1) * 4096],
                                     in_=wqu[fc]).then_inc(lu[s2], 16)
    elif eng == "scalar":
        sc = e.scalar
        ctx.emit_fence(sc)
        for fc in range(NFC):
            sc.wait_ge(mm, fc + 1)
            if fc >= 2:
                sc.wait_ge(hv, fc - 1)
            s = fc % 2
            sc.activation(sgs[:, s * T:(s + 1) * T], psG[s][:], AF.Silu,
                          scale=gsv[:, fc:fc + 1])
            sc.activation(ups[:, s * T:(s + 1) * T], psU[s][:], AF.Identity,
                          scale=usv[:, fc:fc + 1]).then_inc(ev, 2)
    elif eng == "vector":
        v = e.vector
        ctx.emit_fence(v)
        if quantize:
            v.wait_ge(thv, 1)
        for fc in range(NFC):
            s = fc % 2
            if quantize:
                for u in range(4):
                    mat = 0 if u < 2 else 1
                    hf = u % 2
                    slot = unit(fc, u) % 4
                    v.wait_ge(lg[slot], 16 * (unit(fc, u) // 4 + 1))
                    if fc >= 2:
                        v.wait_ge(mm, fc - 1)
                        v.wait_ge((sgq if mat == 0 else squ)[s], 16 * (fc // 2) if save else 0)
                    src = wst[:, slot * 2048:(slot + 1) * 2048]
                    v.tensor_single_scalar(mbb[:], src, thr[:, mat:mat + 1], ALU.is_gt)
                    v.tensor_single_scalar(mb2[:], src, thr[:, 4 + mat:5 + mat], ALU.is_lt)
                    dst = (gq if mat == 0 else uq)
                    v.tensor_tensor(dst[:, s * 4096 + hf * 2048:s * 4096 + (hf + 1) * 2048],
                                    mbb[:], mb2[:], ALU.subtract).then_inc(qd, 1)
            v.wait_ge(ev, 2 * fc + 2)
            if fc >= 2:
                v.wait_ge(sh[s], 16 * (fc // 2))
            v.tensor_tensor(hbf[:, s * T:(s + 1) * T], sgs[:, s * T:(s + 1) * T],
                            ups[:, s * T:(s + 1) * T], ALU.mult).then_inc(hv, 1)
    elif eng == "tensor":
        te = e.tensor
        ctx.emit_fence(te)
        te.wait_ge(xnv, NK)
        for fc in range(NFC):
            s = fc % 2
            if quantize:
                te.wait_ge(qd, unit(fc, 2))
            else:
                te.wait_ge(lg[s], 16 * (fc // 2 + 1))
            if fc >= 2:
                te.wait_ge(ev, 2 * (fc - 2) + 2)
            for k in range(NK):
                lhs = gq[:, s * 4096 + k * 128:s * 4096 + (k + 1) * 128]
                for h in range(2):
                    te.matmul(psG[s][:, h * 512:(h + 1) * 512], lhs,
                              xn[:, k * T + h * 512:k * T + (h + 1) * 512],
                              start=(k == 0), stop=(k == NK - 1))
            if quantize:
                te.wait_ge(qd, unit(fc, 4 - 1) + 1)
            else:
                te.wait_ge(lu[s], 16 * (fc // 2 + 1))
            for k in range(NK):
                lhs = uq[:, s * 4096 + k * 128:s * 4096 + (k + 1) * 128]
                for h in range(2):
                    last = te.matmul(psU[s][:, h * 512:(h + 1) * 512], lhs,
                                     xn[:, k * T + h * 512:k * T + (h + 1) * 512],
                                     start=(k == 0), stop=(k == NK - 1))
            last.then_inc(mm, 1)
    elif eng == "gpsimd":
        g = e.gpsimd
        ctx.emit_fence(g)
        for fc in range(NFC):
            s = fc % 2
            if quantize and save:
                g.wait_ge(qd, unit(fc, 1) + 1)
                g.dma_start(out=wqg[fc], in_=gq[:, s * 4096:(s + 1) * 4096]).then_inc(sgq[s], 16)
                g.wait_ge(qd, unit(fc, 3) + 1)
                g.dma_start(out=wqu[fc], in_=uq[:, s * 4096:(s + 1) * 4096]).then_inc(squ[s], 16)
            g.wait_ge(hv, fc + 1)
            g.dma_start(out=h_state[fc], in_=hbf[:, s * T:(s + 1) * T]).then_inc(sh[s], 16)
    return {}


def emit_ffn_s2(ctx, ph, eng, quantize, save, x_in, x_out):
    e = ctx.e
    q = "s2"
    NS = 2
    lh = [ctx.sem(q + f"lh{i}") for i in range(NS)]
    ldw = [ctx.sem(q + f"ld{i}") for i in range(NS)]
    qd2 = ctx.sem(q + "qd")
    sq2 = ctx.sem(q + "sq2")
    m2 = ctx.sem(q + "m2")
    e2 = ctx.sem(q + "e2")
    xov = ctx.sem(q + "xov")
    lx2 = ctx.sem(q + "lx2")
    so = [ctx.sem(q + f"so{i}") for i in range(2)]
    sdq = [ctx.sem(q + f"sdq{i}") for i in range(2)]
    sh1 = [ctx.sem(f"s1sh{i}") for i in range(2)]

    h_state = ctx.t["d_h"]
    wd, wqd = ctx.t.get("d_wd"), ctx.t["d_wqd"]
    hb2 = ctx.t["hb2"]; dq = ctx.t["dq"]
    xin2 = ctx.t["xin2"]; tmp2 = ctx.t["tmp2"]; xo = ctx.t["xo"]
    psD = [ctx.t[f"psD{kg}"] for kg in range(8)]
    dsh = ctx.t["dsh"]
    if quantize:
        dwf = ctx.t["dwf"]
        mbd = ctx.t["mbd"]; mb2d = ctx.t["mb2d"]
    thr = ctx.t["thr"]

    def quant_here(qi):
        return quantize and qi % 2 == 0

    if eng == "sync":
        sy = e.sync
        sy.wait_ge(ctx.sem("s1mm"), NFC)
        sy.wait_ge(ctx.sem("s1ev"), 2 * NFC)
        sy.wait_ge(ctx.sem("s1hv"), NFC)
        for i_ in range(2):
            sy.wait_ge(sh1[i_], 16 * (NFC // 2))
            if quantize and save:
                sy.wait_ge(ctx.sem(f"s1sgq{i_}"), 16 * (NFC // 2))
                sy.wait_ge(ctx.sem(f"s1squ{i_}"), 16 * (NFC // 2))
        for qi in range(8):
            g_, th = qi // 2, qi % 2
            sy.wait_ge(xov, 8 * qi)
            for kg in range(8):
                dk = g_ * 8 + kg
                sy.dma_start(out=xin2[:, kg * 512:(kg + 1) * 512],
                             in_=x_in[dk * 128:(dk + 1) * 128, th * 512:(th + 1) * 512]
                             ).then_inc(lx2, 16)
            for fc in range(NFC):
                it = qi * NFC + fc
                s = fc % NS
                if it >= NS:
                    sy.wait_ge(m2, it - (NS - 1))
                if qi == 0:
                    sy.wait_ge(sh1[fc % 2], 16 * (fc // 2 + 1))
                sy.dma_start(out=hb2[:, s * 512:(s + 1) * 512],
                             in_=h_state[fc][:, th * 512:(th + 1) * 512]).then_inc(lh[s], 16)
                if quant_here(qi):
                    nq = g_ * NFC + fc
                    if nq >= 2:
                        sy.wait_ge(qd2, nq - 1)
                    sy.dma_start(out=dwf[:, s * 1024:(s + 1) * 1024],
                                 in_=wd[g_][fc]).then_inc(ldw[s], 16)
                else:
                    if quantize:
                        sy.wait_ge(sdq[s], 16 * ((g_ * NFC + fc) // 2 + 1))
                    sy.dma_start(out=dq[:, s * 1024:(s + 1) * 1024],
                                 in_=wqd[g_][fc]).then_inc(ldw[s], 16)
    elif eng == "scalar":
        sc = e.scalar
        for qi in range(8):
            g_, th = qi // 2, qi % 2
            for kg in range(8):
                u = qi * 8 + kg
                sc.wait_ge(m2, NFC * (qi + 1))
                if u >= 2:
                    sc.wait_ge(xov, u - 1)
                sc.activation(tmp2[:, (u % 2) * 512:(u % 2 + 1) * 512], psD[kg][:],
                              AF.Identity, scale=dsh[:, g_ * 8 + kg:g_ * 8 + kg + 1]).then_inc(e2, 1)
    elif eng == "vector":
        v = e.vector
        for qi in range(8):
            g_, th = qi // 2, qi % 2
            if quant_here(qi):
                for fc in range(NFC):
                    nq = g_ * NFC + fc
                    s = fc % 2
                    it = qi * NFC + fc
                    v.wait_ge(ldw[s], 16 * (it // 2 + 1))
                    if it >= 2:
                        v.wait_ge(m2, it - 1)
                    if save and nq >= 2:
                        v.wait_ge(sdq[s], 16 * (nq // 2))
                    src = dwf[:, s * 1024:(s + 1) * 1024]
                    v.tensor_single_scalar(mbd[:], src, thr[:, 2:3], ALU.is_gt)
                    v.tensor_single_scalar(mb2d[:], src, thr[:, 6:7], ALU.is_lt)
                    v.tensor_tensor(dq[:, s * 1024:(s + 1) * 1024], mbd[:], mb2d[:],
                                    ALU.subtract).then_inc(qd2, 1)
            for kg in range(8):
                u = qi * 8 + kg
                v.wait_ge(e2, u + 1)
                v.wait_ge(lx2, 16 * 8 * (qi + 1))
                if u >= 2:
                    v.wait_ge(so[u % 2], 16 * (u // 2))
                v.tensor_tensor(xo[:, (u % 2) * 512:(u % 2 + 1) * 512],
                                xin2[:, kg * 512:(kg + 1) * 512],
                                tmp2[:, (u % 2) * 512:(u % 2 + 1) * 512],
                                ALU.add).then_inc(xov, 1)
    elif eng == "tensor":
        te = e.tensor
        for qi in range(8):
            g_, th = qi // 2, qi % 2
            for fc in range(NFC):
                it = qi * NFC + fc
                s = fc % NS
                if quant_here(qi):
                    te.wait_ge(qd2, g_ * NFC + fc + 1)
                else:
                    te.wait_ge(ldw[s], 16 * (it // NS + 1))
                te.wait_ge(lh[s], 16 * (it // NS + 1))
                if fc == 0:
                    te.wait_ge(e2, 8 * qi)
                for kg in range(8):
                    lhs = dq[:, s * 1024 + kg * 128:s * 1024 + (kg + 1) * 128]
                    last = te.matmul(psD[kg][:], lhs, hb2[:, s * 512:(s + 1) * 512],
                                     start=(fc == 0), stop=(fc == NFC - 1))
                last.then_inc(m2, 1)
    elif eng == "gpsimd":
        g = e.gpsimd
        for qi in range(8):
            g_, th = qi // 2, qi % 2
            if quant_here(qi) and save:
                for fc in range(NFC):
                    nq = g_ * NFC + fc
                    s = fc % 2
                    g.wait_ge(qd2, nq + 1)
                    g.dma_start(out=wqd[g_][fc],
                                in_=dq[:, s * 1024:(s + 1) * 1024]).then_inc(sdq[s], 16)
            for kg in range(8):
                u = qi * 8 + kg
                dk = g_ * 8 + kg
                g.wait_ge(xov, u + 1)
                g.dma_start(out=x_out[dk * 128:(dk + 1) * 128, th * 512:(th + 1) * 512],
                            in_=xo[:, (u % 2) * 512:(u % 2 + 1) * 512]).then_inc(so[u % 2], 16)
    return {"done": [(so[0], 16 * 32), (so[1], 16 * 32), (m2, 8 * NFC),
                     (ctx.sem(q + "e2"), 64), (ctx.sem(q + "xov"), 64)]}



def emit_p2(ctx, eng, x_in, x_out):
    e = ctx.e
    E = eng
    xn = ctx.t["xn"]
    win = ctx.t["p2win"]; hI = ctx.t["p2h"]
    psA, psB = ctx.t["p2psA"], ctx.t["p2psB"]
    PS = [psA, psB]
    d_winT = ctx.t["d_winT"]; d_w1T = ctx.t["d_w1T"]; d_w2T = ctx.t["d_w2T"]
    d_woutT = ctx.t["d_woutT"]

    emit_ln(ctx, "ln", eng, x_in, xn)

    wlwin = ctx.sem("p2wlwin")
    winm = ctx.sem("p2winm")
    wine = ctx.sem("p2wine")
    xnv = ctx.sem("lnxnv")

    PS4 = [ctx.t["poolA"], ctx.t["poolB"], ctx.t["poolC"], ctx.t["poolD"]]
    if E == "gpsimd":
        g = e.gpsimd
        for kvh in range(2):
            if kvh == 1:
                g.wait_ge(winm, 1)
            g.dma_start(out=win.rearrange("p (kv di) -> p kv di", kv=16),
                        in_=d_winT.rearrange("(kv p) di -> p kv di", p=128)
                        [:, kvh * 16:(kvh + 1) * 16, :]).then_inc(wlwin, 16)
    if E == "tensor":
        te = e.tensor
        te.wait_ge(xnv, NK)
        for kvh in range(2):
            te.wait_ge(wlwin, 16 * (kvh + 1))
            for kv in range(16):
                kvg = kvh * 16 + kv
                for mdi in range(4):
                    lhs = win[:, kv * 512 + mdi * 128:kv * 512 + (mdi + 1) * 128]
                    for h in range(2):
                        last = te.matmul(PS4[mdi][:, h * 512:(h + 1) * 512], lhs,
                                         xn[:, kvg * T + h * 512:kvg * T + (h + 1) * 512],
                                         start=(kvg == 0), stop=(kvg == NK - 1))
            last.then_inc(winm, 1)
    if E == "scalar":
        sc = e.scalar
        for mdi in range(4):
            sc.wait_ge(winm, 2)
            sc.activation(hI[:, mdi * T:(mdi + 1) * T], PS4[mdi][:], AF.Identity,
                          bias=ctx.t["bin2_t"][:, mdi:mdi + 1]).then_inc(wine, 1)

    hb = ctx.t["p2hb"]; sqh = ctx.t["p2sqh"]; rows = ctx.t["p2rows"]
    rbc = ctx.t["p2rbc"]; mrbc = ctx.t["p2mrbc"]
    yB = ctx.t["p2y"]; cA = ctx.t["p2cA"]; iA = ctx.t["p2iA"]; iB = ctx.t["p2iB"]
    povr = ctx.t["p2povr"]; povi = ctx.t["p2povi"]
    w1 = ctx.t["p2w1"]; w2 = ctx.t["p2w2"]; wout = ctx.t["p2wout"]
    zb = ctx.t["p2z"]; y2 = ctx.t["p2y2"]; tm2 = ctx.t["p2tm"]
    xin3 = ctx.t["p2xin"]; xo3 = ctx.t["p2xo"]
    pw = ctx.t["p2pw"]; scb = ctx.t["p2sc"]
    onesb, onesf = ctx.t["onesb"], ctx.t["onesf"]
    psS2 = ctx.t["p2s_psS"]; psQ2 = ctx.t["p2s_psQ"]
    psB1, psB2 = psA, psB
    d_sc_in, d_sc_out = ctx.t["d_sc_in"], ctx.t["d_sc_out"]
    ln1w, ln1b = ctx.t["ln1w_t"], ctx.t["ln1b_t"]
    ln2w, ln2b = ctx.t["ln2w_t"], ctx.t["ln2b_t"]
    pzre, pzim = ctx.t["pzre_t"], ctx.t["pzim_t"]
    lcre, lcim = ctx.t["lcre_t"], ctx.t["lcim_t"]
    sclv = ctx.t["sclv_t"]; b1v = ctx.t["b1v_t"]; b2v = ctx.t["b2v_t"]
    lnlw, lnlb = ctx.t["lnlw_t"], ctx.t["lnlb_t"]
    boutv = ctx.t["boutv_t"]; selv = ctx.t["selv_t"]

    hbv = ctx.sem("p2hbv"); hbs = ctx.sem("p2hbs"); ism = ctx.sem("p2ism")
    ivf = ctx.sem("p2ivf"); isf = ctx.sem("p2isf"); ibm = ctx.sem("p2ibm")
    itv = ctx.sem("p2itv"); ys = ctx.sem("p2ys")
    rq = ctx.sem("p2rq"); rs = ctx.sem("p2rs")
    scs = ctx.sem("p2scs"); gcc = ctx.sem("p2gcc"); gcv = ctx.sem("p2gcv")
    wl1 = ctx.sem("p2wl1"); wl2 = ctx.sem("p2wl2"); wlo = ctx.sem("p2wlo")
    ifm = ctx.sem("p2ifm"); ife = ctx.sem("p2ife"); ihv = ctx.sem("p2ihv")
    xl3 = [ctx.sem(f"p2xl3{i}") for i in range(2)]
    xo3v = ctx.sem("p2xo3v")
    so3 = [ctx.sem(f"p2so3{i}") for i in range(2)]

    n_ln = [0]
    n_ys = [0]
    n_ifm = [0]
    n_ife = [0]
    n_ihv = [0]

    def inner_ln(wvec, bvec, wcol, out_t, silu, tmb=None):
        li = n_ln[0]
        tm_ = cA if tmb is None else tmb
        R = lambda i_: rows[0:1, i_ * T:(i_ + 1) * T]
        if E == "vector":
            v = e.vector
            if li == 0:
                v.wait_ge(wine, 4)
            v.tensor_copy(hb[:], hI[:]).then_inc(hbv, 1)
            v.wait_ge(ism, li + 1)
            v.tensor_scalar_mul(R(0), psS2[0:1, :], 1.0 / 512)
            v.tensor_scalar_mul(R(1), psQ2[0:1, :], 1.0 / 512)
            ctx.edge(v, v.tensor_tensor(R(2), R(0), R(0), ALU.mult), "p2ve")
            v.tensor_tensor(R(1), R(1), R(2), ALU.subtract).then_inc(ivf, 1)
            v.wait_ge(isf, li + 1)
            ctx.edge(v, v.reciprocal(R(2), R(1)), "p2ve")
            v.tensor_tensor(R(3), R(0), R(2), ALU.mult).then_inc(ivf, 1)
            v.wait_ge(ibm, li + 1)
            v.tensor_copy(rbc[:], psB1[:])
            v.tensor_copy(mrbc[:], psB2[:])
            for g_ in range(4):
                v.tensor_tensor(tm_[:, g_ * T:(g_ + 1) * T], hI[:, g_ * T:(g_ + 1) * T],
                                rbc[:], ALU.mult)
                v.tensor_tensor(tm_[:, g_ * T:(g_ + 1) * T], tm_[:, g_ * T:(g_ + 1) * T],
                                mrbc[:], ALU.subtract).then_inc(itv, 1)
        elif E == "scalar":
            sc = e.scalar
            sc.wait_ge(hbv, li + 1)
            sc.activation(sqh[:], hI[:], AF.Square).then_inc(hbs, 1)
            sc.wait_ge(ivf, 2 * li + 1)
            sc.activation(rows[0:1, 1 * T:2 * T], rows[0:1, 1 * T:2 * T],
                          AF.Sqrt, bias=ctx.t["epsv"][0:1, 0:1]).then_inc(isf, 1)
            for g_ in range(4):
                sc.wait_ge(itv, 4 * li + g_ + 1)
                fn = AF.Silu if silu else AF.Identity
                sc.activation(out_t[:, g_ * T:(g_ + 1) * T], tm_[:, g_ * T:(g_ + 1) * T],
                              fn, scale=wvec[:, wcol + g_:wcol + g_ + 1],
                              bias=bvec[:, wcol + g_:wcol + g_ + 1]).then_inc(ys, 1)
        elif E == "tensor":
            te = e.tensor
            te.wait_ge(hbv, li + 1)
            te.wait_ge(hbs, li + 1)
            for g_ in range(4):
                for h in range(2):
                    te.matmul(psS2[0:1, h * 512:(h + 1) * 512], onesb[:, 0:1],
                              hb[:, g_ * T + h * 512:g_ * T + (h + 1) * 512],
                              start=(g_ == 0), stop=(g_ == 3))
            for g_ in range(4):
                for h in range(2):
                    last = te.matmul(psQ2[0:1, h * 512:(h + 1) * 512], onesb[:, 0:1],
                                     sqh[:, g_ * T + h * 512:g_ * T + (h + 1) * 512],
                                     start=(g_ == 0), stop=(g_ == 3))
            last.then_inc(ism, 1)
            te.wait_ge(ivf, 2 * li + 2)
            for h in range(2):
                te.matmul(psB1[:, h * 512:(h + 1) * 512], onesf[0:1, :],
                          rows[0:1, 2 * T + h * 512:2 * T + (h + 1) * 512],
                          start=True, stop=True)
            for h in range(2):
                last = te.matmul(psB2[:, h * 512:(h + 1) * 512], onesf[0:1, :],
                                 rows[0:1, 3 * T + h * 512:3 * T + (h + 1) * 512],
                                 start=True, stop=True)
            last.then_inc(ibm, 1)
        n_ln[0] += 1
        n_ys[0] += 4

    def cmul_sc(v, dre, dim_, sre, sim, scre, scim, tmp):
        v.tensor_single_scalar(dre, sre, scre, ALU.mult)
        ctx.edge(v, v.tensor_single_scalar(tmp, sim, scim, ALU.mult), "p2ve")
        v.tensor_tensor(dre, dre, tmp, ALU.subtract)
        v.tensor_single_scalar(dim_, sim, scre, ALU.mult)
        ctx.edge(v, v.tensor_single_scalar(tmp, sre, scim, ALU.mult), "p2ve")
        ctx.edge(v, v.tensor_tensor(dim_, dim_, tmp, ALU.add), "p2ve")

    V3 = lambda t_: t_.rearrange("p (g t) -> p g t", g=4)

    for i in range(4):
        ycol = 4 * i
        inner_ln(ln1w, ln1b, 4 * i, yB, True)
        ys_after_ln1 = n_ys[0]

        if E == "vector":
            v = e.vector
            EG = lambda inst: ctx.edge(v, inst, "p2ve")
            zre = pzre[:, 4 * i:4 * i + 4]; zim = pzim[:, 4 * i:4 * i + 4]
            P_RE = scb[:, 48:52]; P_IM = scb[:, 52:56]
            TM = scb[:, 44:48]
            v.tensor_tensor(scb[:, 36:40], zre, zre, ALU.mult)
            EG(v.tensor_tensor(TM, zim, zim, ALU.mult))
            v.tensor_tensor(scb[:, 36:40], scb[:, 36:40], TM, ALU.add).then_inc(rq, 1)
            v.wait_ge(rs, i + 1)
            EG(v.reciprocal(TM, scb[:, 36:40]))
            EG(v.tensor_tensor(TM, TM, scb[:, 40:44], ALU.mult))
            v.tensor_tensor(P_RE, zre, TM, ALU.mult)
            EG(v.tensor_tensor(P_IM, zim, TM, ALU.mult))
            v.tensor_copy(pw[:, 0:4], P_RE)
            EG(v.tensor_copy(pw[:, 48:52], P_IM))
            for k in range(1, 11):
                pr0 = pw[:, (k - 1) * 4:k * 4]; pi0 = pw[:, 48 + (k - 1) * 4:48 + k * 4]
                prk = pw[:, k * 4:(k + 1) * 4]; pik = pw[:, 48 + k * 4:48 + (k + 1) * 4]
                v.tensor_tensor(prk, pr0, pr0, ALU.mult)
                EG(v.tensor_tensor(TM, pi0, pi0, ALU.mult))
                v.tensor_tensor(prk, prk, TM, ALU.subtract)
                EG(v.tensor_tensor(pik, pr0, pi0, ALU.mult))
                EG(v.tensor_scalar_mul(pik, pik, 2.0))
            EG(v.tensor_scalar_mul(pw[:, 96:140], pw[:, 48:92], -1.0))
            pvr = V3(povr); pvi = V3(povi)
            for g_ in range(4):
                v.tensor_copy(pvr[:, g_, 0:1], P_RE[:, g_:g_ + 1])
                v.tensor_copy(pvi[:, g_, 0:1], P_IM[:, g_:g_ + 1])
                n = 1
                k = 0
                while n < T:
                    take = min(n, T - n)
                    cmul_sc(v, pvr[:, g_, n:n + take], pvi[:, g_, n:n + take],
                            pvr[:, g_, 0:take], pvi[:, g_, 0:take],
                            pw[:, k * 4 + g_:k * 4 + g_ + 1],
                            pw[:, 48 + k * 4 + g_:48 + k * 4 + g_ + 1],
                            iB[:, 0:take])
                    n *= 2
                    k += 1
        if E == "scalar":
            sc = e.scalar
            sc.wait_ge(rq, i + 1)
            sc.activation(scb[:, 36:40], scb[:, 36:40], AF.Sqrt)
            sc.activation(scb[:, 40:44], scb[:, 36:40], AF.Tanh).then_inc(rs, 1)

        if E == "vector":
            v = e.vector
            v.wait_ge(ys, ys_after_ln1)
            yv = V3(yB); cAv = V3(cA); iAv = V3(iA); iBv = V3(iB)
            ms2 = ctx.sem("p2ms")
            last = None
            for g_ in range(4):
                last = v.memset(iAv[:, g_, 0:1], 0.0)
            last.then_inc(ms2, 1)
            for g_ in range(4):
                pr = pw[:, g_:g_ + 1]; pi = pw[:, 48 + g_:48 + g_ + 1]
                v.tensor_copy(cAv[:, g_, 0:1], yv[:, g_, 0:1])
                v.scalar_tensor_tensor(cAv[:, g_, 1:T], yv[:, g_, 0:T - 1], pr,
                                       yv[:, g_, 1:T], ALU.mult, ALU.add)
                v.tensor_single_scalar(iAv[:, g_, 1:T], yv[:, g_, 0:T - 1], pi, ALU.mult)
            v.wait_ge(ms2, i + 1)
            src_re, src_im, dst_re, dst_im = cA, iA, yB, iB
            for k in range(1, 10):
                s_ = 1 << k
                sr, si_ = V3(src_re), V3(src_im)
                dr, di_ = V3(dst_re), V3(dst_im)
                for g_ in range(4):
                    pr = pw[:, k * 4 + g_:k * 4 + g_ + 1]
                    pi = pw[:, 48 + k * 4 + g_:48 + k * 4 + g_ + 1]
                    pin = pw[:, 96 + k * 4 + g_:96 + k * 4 + g_ + 1]
                    v.tensor_copy(dr[:, g_, 0:s_], sr[:, g_, 0:s_])
                    v.tensor_copy(di_[:, g_, 0:s_], si_[:, g_, 0:s_])
                    v.scalar_tensor_tensor(dr[:, g_, s_:T], sr[:, g_, 0:T - s_], pr,
                                           sr[:, g_, s_:T], ALU.mult, ALU.add)
                    v.scalar_tensor_tensor(dr[:, g_, s_:T], si_[:, g_, 0:T - s_],
                                           pin,
                                           dr[:, g_, s_:T], ALU.mult, ALU.add)
                    v.scalar_tensor_tensor(di_[:, g_, s_:T], si_[:, g_, 0:T - s_], pr,
                                           si_[:, g_, s_:T], ALU.mult, ALU.add)
                    v.scalar_tensor_tensor(di_[:, g_, s_:T], sr[:, g_, 0:T - s_], pi,
                                           di_[:, g_, s_:T], ALU.mult, ALU.add)
                src_re, src_im, dst_re, dst_im = dst_re, dst_im, src_re, src_im
            fre, fim = V3(src_re), V3(src_im)
            lr = lcre[:, 4 * i:4 * i + 4]; li_ = lcim[:, 4 * i:4 * i + 4]
            for g_ in range(4):
                cmul_sc(v, scb[:, 0 + g_:1 + g_], scb[:, 4 + g_:5 + g_],
                        lr[:, g_:g_ + 1], li_[:, g_:g_ + 1],
                        pw[:, 40 + g_:41 + g_], pw[:, 88 + g_:89 + g_],
                        scb[:, 44:45])
                ctx.edge(v, v.tensor_tensor(scb[:, 0 + g_:1 + g_], scb[:, 0 + g_:1 + g_],
                                fre[:, g_, T - 1:T], ALU.add), "p2ve")
                last = v.tensor_tensor(scb[:, 4 + g_:5 + g_], scb[:, 4 + g_:5 + g_],
                                       fim[:, g_, T - 1:T], ALU.add)
                if g_ < 3:
                    ctx.edge(v, last, "p2ve")
            last.then_inc(scs, 1)
            v.wait_ge(gcc, 48 * (i + 1))
            for g_ in range(4):
                v.tensor_single_scalar(scb[:, 24 + g_:25 + g_], lr[:, g_:g_ + 1],
                                       selv[:, 0:1], ALU.mult)
                ctx.edge(v, v.tensor_single_scalar(scb[:, 44:45], scb[:, 8 + g_:9 + g_],
                                       selv[:, 1:2], ALU.mult), "p2ve")
                v.tensor_tensor(scb[:, 24 + g_:25 + g_], scb[:, 24 + g_:25 + g_],
                                scb[:, 44:45], ALU.add)
                v.tensor_single_scalar(scb[:, 28 + g_:29 + g_], li_[:, g_:g_ + 1],
                                       selv[:, 0:1], ALU.mult)
                ctx.edge(v, v.tensor_single_scalar(scb[:, 44:45], scb[:, 12 + g_:13 + g_],
                                       selv[:, 1:2], ALU.mult), "p2ve")
                ctx.edge(v, v.tensor_tensor(scb[:, 28 + g_:29 + g_], scb[:, 28 + g_:29 + g_],
                                scb[:, 44:45], ALU.add), "p2ve")
                ctx.edge(v, v.tensor_scalar_mul(scb[:, 32 + g_:33 + g_], scb[:, 28 + g_:29 + g_], -1.0), "p2ve")
            for g_ in range(4):
                v.scalar_tensor_tensor(fre[:, g_, :], V3(povr)[:, g_, :],
                                       scb[:, 24 + g_:25 + g_], fre[:, g_, :],
                                       ALU.mult, ALU.add)
                v.scalar_tensor_tensor(fre[:, g_, :], V3(povi)[:, g_, :],
                                       scb[:, 32 + g_:33 + g_], fre[:, g_, :],
                                       ALU.mult, ALU.add)
                v.scalar_tensor_tensor(hI[:, g_ * T:(g_ + 1) * T], src_re[:, g_ * T:(g_ + 1) * T],
                                       sclv[:, 4 * i + g_:4 * i + g_ + 1],
                                       hI[:, g_ * T:(g_ + 1) * T], ALU.mult, ALU.add)
        if E == "gpsimd":
            g = e.gpsimd
            g.wait_ge(scs, i + 1)
            g.dma_start(out=d_sc_in, in_=scb[:, 0:8]).then_inc(gcc, 16)
            g.wait_ge(gcc, 48 * i + 16)
            g.collective_compute("AllGather", ALU.bypass,
                                 replica_groups=[[0, 1], [2, 3], [4, 5], [6, 7]],
                                 ins=[d_sc_in], outs=[d_sc_out]).then_inc(gcv, 1)
            g.wait_ge(gcv, i + 1)
            g.dma_start(out=scb[:, 8:16], in_=d_sc_out[0:128, :]).then_inc(gcc, 16)
            g.dma_start(out=scb[:, 16:24], in_=d_sc_out[128:256, :]).then_inc(gcc, 16)

        inner_ln(ln2w, ln2b, 4 * i, y2, False)
        ys_after_ln2 = n_ys[0]

        if E == "gpsimd":
            g = e.gpsimd
            if True:
                g.wait_ge(ifm, n_ifm[0])
                if i == 0:
                    g.wait_ge(winm, 2)
                g.dma_start(out=w1.rearrange("p (kd f) -> p kd f", kd=4),
                            in_=d_w1T[i].rearrange("p (kd f) -> p kd f", kd=4)).then_inc(wl1, 16)
                g.dma_start(out=w2.rearrange("p (fk dd) -> p fk dd", fk=16),
                            in_=d_w2T[i].rearrange("p (fk dd) -> p fk dd", fk=16)).then_inc(wl2, 16)
        if E == "tensor":
            te = e.tensor
            te.wait_ge(wl1, 16 * (i + 1))
            te.wait_ge(ys, ys_after_ln2)
            for fk in range(16):
                ps = PS[fk % 2]
                if fk >= 2:
                    te.wait_ge(ife, n_ife[0] + fk - 1)
                else:
                    te.wait_ge(ife, n_ife[0])
                    if i == 0:
                        te.wait_ge(wine, 4)
                for kd in range(4):
                    lhs = w1[:, kd * 2048 + fk * 128:kd * 2048 + (fk + 1) * 128]
                    for h in range(2):
                        last = te.matmul(ps[:, h * 512:(h + 1) * 512], lhs,
                                         y2[:, kd * T + h * 512:kd * T + (h + 1) * 512],
                                         start=(kd == 0), stop=(kd == 3))
                last.then_inc(ifm, 1)
            te.wait_ge(wl2, 16 * (i + 1))
            for g_ in range(4):
                ps = PS[g_ % 2]
                te.wait_ge(ife, n_ife[0] + 16)
                if g_ >= 2:
                    te.wait_ge(ife, n_ife[0] + 16 + g_ - 1)
                for fk in range(16):
                    lhs = w2[:, fk * 512 + g_ * 128:fk * 512 + (g_ + 1) * 128]
                    for h in range(2):
                        last = te.matmul(ps[:, h * 512:(h + 1) * 512], lhs,
                                         zb[:, fk * T + h * 512:fk * T + (h + 1) * 512],
                                         start=(fk == 0), stop=(fk == 15))
                last.then_inc(ifm, 1)
        if E == "scalar":
            sc = e.scalar
            for fk in range(16):
                sc.wait_ge(ifm, n_ifm[0] + fk + 1)
                sc.activation(zb[:, fk * T:(fk + 1) * T], PS[fk % 2][:], AF.Silu,
                              bias=b1v[:, 16 * i + fk:16 * i + fk + 1]).then_inc(ife, 1)
            for g_ in range(4):
                sc.wait_ge(ifm, n_ifm[0] + 16 + g_ + 1)
                if g_ >= 2:
                    sc.wait_ge(ihv, n_ihv[0] + g_ - 1)
                sc.activation(tm2[:, (g_ % 2) * T:(g_ % 2 + 1) * T], PS[g_ % 2][:],
                              AF.Identity, bias=b2v[:, 4 * i + g_:4 * i + g_ + 1]).then_inc(ife, 1)
        if E == "vector":
            v = e.vector
            for g_ in range(4):
                v.wait_ge(ife, n_ife[0] + 16 + g_ + 1)
                v.tensor_tensor(hI[:, g_ * T:(g_ + 1) * T], hI[:, g_ * T:(g_ + 1) * T],
                                tm2[:, (g_ % 2) * T:(g_ % 2 + 1) * T], ALU.add).then_inc(ihv, 1)
        n_ifm[0] += 20
        n_ife[0] += 20
        n_ihv[0] += 4

    inner_ln(lnlw, lnlb, 0, y2, False, tmb=yB)
    ys_final = n_ys[0]

    if E == "gpsimd":
        g = e.gpsimd
        g.wait_ge(ifm, n_ifm[0])
        g.dma_start(out=wout.rearrange("p (kd v) -> p kd v", kd=4),
                    in_=d_woutT.rearrange("p (kd v) -> p kd v", kd=4)).then_inc(wlo, 16)
        for vk in range(NK):
            g.wait_ge(xo3v, vk + 1)
            g.dma_start(out=x_out[vk * 128:(vk + 1) * 128, :],
                        in_=xo3[:, (vk % 2) * T:(vk % 2 + 1) * T]).then_inc(so3[vk % 2], 16)
    if E == "sync":
        sy = e.sync
        for vk in range(NK):
            if vk < 2:
                sy.wait_ge(ys, ys_final)
            if vk >= 2:
                sy.wait_ge(xo3v, vk - 1)
            sy.dma_start(out=xin3[:, (vk % 2) * T:(vk % 2 + 1) * T],
                         in_=x_in[vk * 128:(vk + 1) * 128, :]).then_inc(xl3[vk % 2], 16)
    if E == "tensor":
        te = e.tensor
        te.wait_ge(wlo, 16)
        te.wait_ge(ys, ys_final)
        for vk in range(NK):
            ps = PS[vk % 2]
            if vk >= 2:
                te.wait_ge(ife, n_ife[0] + vk - 1)
            else:
                te.wait_ge(ife, n_ife[0])
            for kd in range(4):
                lhs = wout[:, kd * 4096 + vk * 128:kd * 4096 + (vk + 1) * 128]
                for h in range(2):
                    last = te.matmul(ps[:, h * 512:(h + 1) * 512], lhs,
                                     y2[:, kd * T + h * 512:kd * T + (h + 1) * 512],
                                     start=(kd == 0), stop=(kd == 3))
            last.then_inc(ifm, 1)
    if E == "scalar":
        sc = e.scalar
        for vk in range(NK):
            sc.wait_ge(ifm, n_ifm[0] + vk + 1)
            if vk < 2:
                sc.wait_ge(ihv, n_ihv[0])
            if vk >= 2:
                sc.wait_ge(xo3v, vk - 1)
            sc.activation(tm2[:, (vk % 2) * T:(vk % 2 + 1) * T], PS[vk % 2][:],
                          AF.Identity, bias=boutv[:, vk:vk + 1]).then_inc(ife, 1)
    if E == "vector":
        v = e.vector
        for vk in range(NK):
            v.wait_ge(ife, n_ife[0] + vk + 1)
            v.wait_ge(xl3[vk % 2], 16 * (vk // 2 + 1))
            if vk >= 2:
                v.wait_ge(so3[vk % 2], 16 * (vk // 2))
            v.tensor_tensor(xo3[:, (vk % 2) * T:(vk % 2 + 1) * T],
                            xin3[:, (vk % 2) * T:(vk % 2 + 1) * T],
                            tm2[:, (vk % 2) * T:(vk % 2 + 1) * T], ALU.add).then_inc(xo3v, 1)
    return {"done": [(so3[0], 16 * 16), (so3[1], 16 * 16), (ctx.sem("p2ifm"), n_ifm[0] + NK),
                     (ctx.sem("p2xo3v"), NK), (ctx.sem("p2gcc"), 48 * 4)]}



def emit_p4(ctx, eng, x_in, x_out, load_weights):
    e = ctx.e
    E = eng
    xn = ctx.t["xn"]
    emit_ln(ctx, "ln", eng, x_in, xn)
    xnv = ctx.sem("lnxnv")
    rwsb = ctx.t["rw_sb"]; ebsb = ctx.t["eb_sb"]; rbsb = ctx.t["rb_sb"]
    idt = ctx.t["idt"]
    lgb = ctx.t["p4lgb"]; smf = ctx.t["p4smf"]; wkf = ctx.t["p4wkf"]
    wdf4 = ctx.t["p4wdf4"]; wdT = ctx.t["p4wdT"]; cols = ctx.t["p4cols"]
    wrow = ctx.t["p4wrow"]; hws = ctx.t["p4hws"]
    tmp4 = ctx.t["p4tmp4"]; xin4 = ctx.t["p4xin4"]; xo4 = ctx.t["p4xo4"]
    psL = ctx.t["p4psL"]; psT = ctx.t["p4psT"]; psTb = ctx.t["p4psTb"]
    psR = ctx.t["p4psR"]; psW = ctx.t["p4psW"]
    psM = [ctx.t["p4psM0"], ctx.t["p4psM1"]]
    onesb, onesf = ctx.t["onesb"], ctx.t["onesf"]
    d_rwT, d_ebw, d_rbv = ctx.t["d_rwT"], ctx.t["d_ebw"], ctx.t["d_rbv"]

    rl = ctx.sem("p46rl")
    lgm = ctx.sem("p4lgm"); lge = ctx.sem("p4lge")
    trm = ctx.sem("p4trm"); trc = ctx.sem("p4trc")
    vnm = ctx.sem("p4vnm"); sexp = ctx.sem("p4sexp")
    wdv = ctx.sem("p4wdv")
    tbm = ctx.sem("p4tbm")
    tbc = ctx.sem("p4tbc")
    wsr = ctx.sem("p4wsr"); wbm = ctx.sem("p4wbm"); wbc = ctx.sem("p4wbc")
    mxm = ctx.sem("p4mxm"); mxe = ctx.sem("p4mxe"); mxv = ctx.sem("p4mxv")
    xl4 = [ctx.sem(f"p4xl4{i}") for i in range(2)]
    so4 = [ctx.sem(f"p4so4{i}") for i in range(2)]
    wdb = ctx.t["p4lgb"]

    if E == "gpsimd":
        g = e.gpsimd
        ctx.emit_fence(g)
        if load_weights:
            g.dma_start(out=rwsb.rearrange("p (kv e) -> p kv e", kv=NK),
                        in_=d_rwT.rearrange("(kv p) e -> p kv e", p=128)).then_inc(rl, 16)
            g.dma_start(out=ebsb[:], in_=d_ebw).then_inc(rl, 16)
            g.dma_start(out=rbsb[0:64, 0:1], in_=d_rbv).then_inc(rl, 16)
        for dk in range(NK):
            g.wait_ge(mxv, dk + 1)
            g.dma_start(out=x_out[dk * 128:(dk + 1) * 128, :],
                        in_=xo4[:, (dk % 2) * T:(dk % 2 + 1) * T]).then_inc(so4[dk % 2], 16)
    if E == "sync":
        sy = e.sync
        for dk in range(NK):
            if dk < 2:
                sy.wait_ge(xnv, NK)
            if dk >= 2:
                sy.wait_ge(mxv, dk - 1)
            sy.dma_start(out=xin4[:, (dk % 2) * T:(dk % 2 + 1) * T],
                         in_=x_in[dk * 128:(dk + 1) * 128, :]).then_inc(xl4[dk % 2], 16)
    if E == "tensor":
        te = e.tensor
        if load_weights:
            te.wait_ge(rl, 48)
        te.wait_ge(xnv, NK)
        for kv in range(NK):
            for h in range(2):
                last = te.matmul(psL[:, h * 512:(h + 1) * 512],
                                 rwsb[:, kv * 64:(kv + 1) * 64],
                                 xn[:, kv * T + h * 512:kv * T + (h + 1) * 512],
                                 start=(kv == 0), stop=(kv == NK - 1))
        last.then_inc(lgm, 1)
        te.wait_ge(lge, 1)
        for ti in range(8):
            te.transpose(psT[:, ti * 64:(ti + 1) * 64], lgb[0:64, ti * 128:(ti + 1) * 128],
                         idt[0:64, 0:64]).then_inc(trm, 1)
        for ti in range(8):
            te.wait_ge(wdv, ti + 1)
            if ti >= 1:
                te.wait_ge(tbc, ti)
            te.transpose(psTb[:], ctx.t["p4wdb"][:, ti * 64:(ti + 1) * 64],
                         idt[:]).then_inc(tbm, 1)
        te.wait_ge(tbc, 8)
        for h in range(2):
            last = te.matmul(psR[0:1, h * 512:(h + 1) * 512], onesb[0:64, 0:1],
                             wdT[:, h * 512:(h + 1) * 512], start=True, stop=True)
        last.then_inc(lgm, 1)
        te.wait_ge(wsr, 1)
        for h in range(2):
            last = te.matmul(psW[:, h * 512:(h + 1) * 512], onesf[0:1, :],
                             wrow[0:1, h * 512:(h + 1) * 512], start=True, stop=True)
        last.then_inc(wbm, 1)
        for dk in range(NK):
            if dk >= 2:
                te.wait_ge(mxe, dk - 1)
            for h in range(2):
                last = te.matmul(psM[dk % 2][:, h * 512:(h + 1) * 512],
                                 ebsb[:, dk * 128:(dk + 1) * 128],
                                 wdT[:, h * 512:(h + 1) * 512], start=True, stop=True)
            last.then_inc(mxm, 1)
    if E == "scalar":
        sc = e.scalar
        ctx.emit_fence(sc)
        sc.wait_ge(lgm, 1)
        sc.activation(lgb[:], psL[:], AF.Identity, bias=rbsb[:, 0:1]).then_inc(lge, 1)
        for ti in range(8):
            sc.wait_ge(vnm, ti + 1)
            sc.activation(smf[:, ti * 64:(ti + 1) * 64], smf[:, ti * 64:(ti + 1) * 64],
                          AF.Exp, bias=cols[:, 0:1]).then_inc(sexp, 1)
        for dk in range(NK):
            sc.wait_ge(mxm, dk + 1)
            if dk >= 2:
                sc.wait_ge(mxv, dk - 1)
            sc.activation(tmp4[:, (dk % 2) * T:(dk % 2 + 1) * T], psM[dk % 2][:],
                          AF.Identity, scale=0.5).then_inc(mxe, 1)
    if E == "vector":
        v = e.vector
        ctx.emit_fence(v)
        EG = lambda inst: ctx.edge(v, inst, "p4ve")
        for ti in range(8):
            v.wait_ge(trm, ti + 1)
            S = smf[:, ti * 64:(ti + 1) * 64]
            EG(v.tensor_copy(S, psT[:, ti * 64:(ti + 1) * 64]))
            v.reduce_max(cols[:, 0:1], S, axis=AX, negate=True).then_inc(vnm, 1)
            v.wait_ge(sexp, ti + 1)
            EG(v.reduce_sum(cols[:, 1:2], S, axis=AX))
            EG(v.reciprocal(cols[:, 2:3], cols[:, 1:2]))
            EG(v.tensor_scalar_mul(S, S, cols[:, 2:3]))
            W = wkf[:, ti * 64:(ti + 1) * 64]
            EG(v.tensor_copy(W, S))
            for r in range(8):
                EG(v.reduce_max(cols[:, 3:4], W, axis=AX))
                if r < 7:
                    EG(v.tensor_single_scalar(wdf4[:, ti * 64:(ti + 1) * 64], W,
                                              cols[:, 3:4], ALU.is_ge))
                    EG(v.scalar_tensor_tensor(W, wdf4[:, ti * 64:(ti + 1) * 64], -1e30,
                                              W, ALU.mult, ALU.add))
            EG(v.tensor_single_scalar(wdf4[:, ti * 64:(ti + 1) * 64], S, cols[:, 3:4], ALU.is_ge))
            EG(v.tensor_tensor(wdf4[:, ti * 64:(ti + 1) * 64], S,
                               wdf4[:, ti * 64:(ti + 1) * 64], ALU.mult))
            v.tensor_copy(ctx.t["p4wdb"][:, ti * 64:(ti + 1) * 64],
                          wdf4[:, ti * 64:(ti + 1) * 64]).then_inc(wdv, 1)
        for ti in range(8):
            v.wait_ge(tbm, ti + 1)
            v.tensor_copy(wdT[:, ti * 128:(ti + 1) * 128], psTb[:]).then_inc(tbc, 1)
        v.wait_ge(lgm, 2)
        v.tensor_scalar_mul(wrow[0:1, :], psR[0:1, :], 0.5).then_inc(wsr, 1)
        v.wait_ge(wbm, 1)
        v.tensor_copy(hws[:], psW[:]).then_inc(wbc, 1)
        for dk in range(NK):
            v.wait_ge(mxe, dk + 1)
            v.wait_ge(xl4[dk % 2], 16 * (dk // 2 + 1))
            if dk >= 2:
                v.wait_ge(so4[dk % 2], 16 * (dk // 2))
            o = xo4[:, (dk % 2) * T:(dk % 2 + 1) * T]
            v.tensor_tensor(o, xn[:, dk * T:(dk + 1) * T], hws[:], ALU.mult)
            v.tensor_tensor(o, o, xin4[:, (dk % 2) * T:(dk % 2 + 1) * T], ALU.add)
            v.tensor_tensor(o, o, tmp4[:, (dk % 2) * T:(dk % 2 + 1) * T],
                            ALU.add).then_inc(mxv, 1)
    return {"done": [(so4[0], 16 * 16), (so4[1], 16 * 16), (ctx.sem("p4mxv"), NK),
                     (ctx.sem("p4mxe"), NK), (ctx.sem("p4mxm"), NK)]}



def emit_p5(ctx, eng, x_in, x_out):
    e = ctx.e
    E = eng
    xn = ctx.t["xn"]
    emit_ln(ctx, "ln", eng, x_in, xn)
    xnv = ctx.sem("lnxnv")
    idt = ctx.t["idt"]
    onesb, onesf = ctx.t["onesb"], ctx.t["onesf"]
    stg = ctx.t["p5stg"]; row5 = ctx.t["p5row"]; sgr = ctx.t["p5sgr"]
    al5 = ctx.t["p5al"]; qkp = ctx.t["p5qkp"]; sgp = ctx.t["p5sgp"]
    abc = ctx.t["p5abc"]; qbp = ctx.t["p5qb"]; absq = ctx.t["p5absq"]
    Ksb = ctx.t["p5K"]; Vsb = ctx.t["p5V"]; att = ctx.t["p5att"]
    Qsb = ctx.t["p5Q"]
    vbc = ctx.t["p5vbc"]; msk = ctx.t["p5msk"]; tmp5 = ctx.t["p5tmp"]
    xin5 = ctx.t["p5xin"]; xo5 = ctx.t["p5xo"]
    psTT = ctx.t["p5psTT"]; psBC = ctx.t["p5psBC"]
    psS5 = [ctx.t["p5psS0"], ctx.t["p5psS1"]]
    psO5 = [ctx.t["p5psO0"], ctx.t["p5psO1"]]
    d_bvq, d_bvk, d_bvv = ctx.t["d_bvq"], ctx.t["d_bvk"], ctx.t["d_bvv"]
    d_bvqp, d_bvkp = ctx.t["d_bvqp"], ctx.t["d_bvkp"]
    d_mask = ctx.t["d_mask"]
    gin_fm, g_fm = ctx.t["d_gin_fm"], ctx.t["d_g_fm"]
    gin_tok, g_tok = ctx.t["d_gin_tok"], ctx.t["d_g_tok"]

    bvl = ctx.sem("p5bvl")
    bvs = ctx.sem("p5bvs")
    bvv_ = ctx.sem("p5bvv")
    bvm = ctx.sem("p5bvm")
    bvc = ctx.sem("p5bvc")
    trm5 = ctx.sem("p5trm")
    trc5 = ctx.sem("p5trc")
    cfm = ctx.sem("p5cfm")
    ctk = ctx.sem("p5ctk")
    gcf = ctx.sem("p5gcf")
    gct = ctx.sem("p5gct")
    qmv = ctx.sem("p5qmv")
    klc = ctx.sem("p5klc")
    kmv = ctx.sem("p5kmv")
    vlc = ctx.sem("p5vlc")
    vmv = ctx.sem("p5vmv")
    mls = [ctx.sem(f"p5mls{i}") for i in range(2)]
    smm = ctx.sem("p5smm")
    sev = ctx.sem("p5sev")
    smk = ctx.sem("p5smk")
    omm = ctx.sem("p5omm")
    oev = ctx.sem("p5oev")
    oxv = ctx.sem("p5oxv")
    xl5 = [ctx.sem(f"p5xl5{i}") for i in range(2)]
    so5 = [ctx.sem(f"p5so5{i}") for i in range(2)]

    HD2 = 1 / 32.0

    if E == "gpsimd":
        g = e.gpsimd
        g.wait_ge(xnv, NK)
        g.dma_start(out=row5[0:1, 0:4096], in_=d_bvv).then_inc(bvl, 16)
        g.dma_start(out=qkp[:, 0:32], in_=d_bvqp).then_inc(bvl, 16)
        g.dma_start(out=qkp[:, 32:64], in_=d_bvkp).then_inc(bvl, 16)
        g.wait_ge(xnv, NK)
        g.dma_start(out=gin_fm.rearrange("(k p) t -> p k t", p=128),
                    in_=xn.rearrange("p (k t) -> p k t", k=NK)).then_inc(cfm, 16)
        g.wait_ge(cfm, 16)
        g.collective_compute("AllGather", ALU.bypass,
                             replica_groups=[[0, 1], [2, 3], [4, 5], [6, 7]],
                             ins=[gin_fm], outs=[g_fm]).then_inc(gcf, 1)
        trcs = ctx.sem("p5trcs")
        for tt in range(8):
            g.wait_ge(trcs, 8 * (tt + 1))
            g.dma_start(out=gin_tok[tt * 128:(tt + 1) * 128, :], in_=stg[:]).then_inc(ctk, 16)
        g.wait_ge(ctk, 128)
        g.collective_compute("AllGather", ALU.bypass,
                             replica_groups=[[0, 1], [2, 3], [4, 5], [6, 7]],
                             ins=[gin_tok], outs=[g_tok]).then_inc(gct, 1)
        for hd in range(4):
            for hk in range(8):
                u = hd * 8 + hk
                g.wait_ge(oxv, u + 1)
                dk = hd * 8 + hk
                g.dma_start(out=x_out[dk * 128:(dk + 1) * 128, :],
                            in_=xo5[:, (u % 2) * T:(u % 2 + 1) * T]).then_inc(so5[u % 2], 16)
    if E == "sync":
        sy = e.sync
        qlc = ctx.sem("p5qlc")
        for hd in range(4):
            sy.wait_ge(cfm, 16)
            if hd > 0:
                sy.wait_ge(smm, 16 * hd)
            sy.dma_start(out=Qsb.rearrange("p (hk t) -> p hk t", hk=8),
                         in_=gin_fm[hd * 1024:(hd + 1) * 1024, :]
                         .rearrange("(hk p) t -> p hk t", p=128)).then_inc(qlc, 16)
            sy.wait_ge(gcf, 1)
            sy.wait_ge(ctk, 128)
            for blk in range(2):
                dst = Ksb.rearrange("p (hk b t) -> p hk b t", hk=8, b=2)[:, :, blk, :]
                src = g_fm[blk * 4096 + hd * 1024:blk * 4096 + (hd + 1) * 1024, :]
                sy.dma_start(out=dst,
                             in_=src.rearrange("(hk p) t -> p hk t", p=128)).then_inc(klc, 16)
            sy.wait_ge(gct, 1)
            if hd > 0:
                sy.wait_ge(omm, 8 * hd)
            for si in range(16):
                sy.dma_start(out=Vsb[:, si * T:(si + 1) * T],
                             in_=g_tok[si * 128:(si + 1) * 128,
                                       hd * 1024:(hd + 1) * 1024]).then_inc(vlc, 16)
            for si in range(16):
                u = hd * 16 + si
                if u == 0:
                    sy.wait_ge(ctx.sem("p5bvm"), 6)
                    sy.wait_ge(ctk, 128)
                if u >= 2:
                    sy.wait_ge(smk, u - 1)
                sy.dma_start(out=msk[:, (u % 2) * T:(u % 2 + 1) * T],
                             in_=d_mask[si * 128:(si + 1) * 128, :]).then_inc(mls[u % 2], 16)
            for hk in range(8):
                u = hd * 8 + hk
                if u < 2:
                    sy.wait_ge(ctx.sem("p5bvm"), 6)
                    sy.wait_ge(ctk, 128)
                if u >= 2:
                    sy.wait_ge(oxv, u - 1)
                dk = hd * 8 + hk
                sy.dma_start(out=xin5[:, (u % 2) * T:(u % 2 + 1) * T],
                             in_=x_in[dk * 128:(dk + 1) * 128, :]).then_inc(xl5[u % 2], 16)
    if E == "scalar":
        sc = e.scalar
        ctx.emit_fence(sc)
        sc.wait_ge(bvl, 48)
        sc.activation(absq[:], qkp[:, 0:64], AF.Abs).then_inc(bvs, 1)
        sc.activation(sgp[:, 0:64], qkp[:, 0:64], AF.Sign).then_inc(bvs, 1)
        sc.activation(sgr[0:1, :], row5[0:1, :], AF.Sign).then_inc(bvs, 1)
        for hd in range(4):
            for si in range(16):
                u = hd * 16 + si
                sc.wait_ge(smm, u + 1)
                if hd > 0:
                    sc.wait_ge(omm, 8 * hd)
                sc.activation(att[:, si * T:(si + 1) * T], psS5[si % 2][:],
                              AF.Sigmoid, scale=4.0 * HD2).then_inc(sev, 1)
            for hk in range(8):
                u = hd * 8 + hk
                sc.wait_ge(omm, u + 1)
                if u >= 2:
                    sc.wait_ge(oxv, u - 1)
                sc.activation(tmp5[:, (u % 2) * T:(u % 2 + 1) * T], psO5[hk % 2][:],
                              AF.Identity).then_inc(oev, 1)
    if E == "tensor":
        te = e.tensor
        te.wait_ge(xnv, NK)
        trcs = ctx.sem("p5trcs")
        for tt in range(8):
            for k in range(NK):
                if k % 4 == 0 and (k > 0 or tt > 0):
                    te.wait_ge(trcs, tt * 8 + k // 4)
                te.transpose(psTT[:, (k % 4) * 128:(k % 4 + 1) * 128],
                             xn[:, k * T + tt * 128:k * T + tt * 128 + 128],
                             idt[:]).then_inc(trm5, 1)
        te.wait_ge(bvs, 1)
        te.matmul(psBC[0:1, 0:64], onesb[:, 0:1], absq[:],
                  start=True, stop=True).then_inc(bvm, 1)
        te.wait_ge(bvv_, 1)
        te.matmul(psBC[:, 0:64], onesf[0:1, :], al5[0:1, 0:64],
                  start=True, stop=True).then_inc(bvm, 1)
        te.wait_ge(bvv_, 3)
        for hd in range(4):
            for h in range(2):
                last = te.matmul(psBC[:, h * 512:(h + 1) * 512],
                                 onesf[0:1, :],
                                 sgr[0:1, hd * 1024 + h * 512:hd * 1024 + (h + 1) * 512],
                                 start=True, stop=True)
            last.then_inc(bvm, 1)
            te.wait_ge(bvc, hd + 1)
        for hd in range(4):
            te.wait_ge(kmv, 8 * (hd + 1))
            te.wait_ge(qmv, 8 * (hd + 1))
            for si in range(16):
                u = hd * 16 + si
                blk, sil = si // 8, si % 8
                if u >= 2:
                    te.wait_ge(sev, u - 1)
                for hk in range(8):
                    lhs = Ksb[:, hk * 2048 + blk * 1024 + sil * 128:
                              hk * 2048 + blk * 1024 + (sil + 1) * 128]
                    for h in range(2):
                        last = te.matmul(psS5[si % 2][:, h * 512:(h + 1) * 512], lhs,
                                         Qsb[:, hk * T + h * 512:hk * T + (h + 1) * 512],
                                         start=(hk == 0), stop=(hk == 7))
                last.then_inc(smm, 1)
            te.wait_ge(vmv, 16 * (hd + 1))
            for hk in range(8):
                u = hd * 8 + hk
                if u >= 2:
                    te.wait_ge(oev, u - 1)
                for si in range(16):
                    lhs = Vsb[:, si * T + hk * 128:si * T + (hk + 1) * 128]
                    te.wait_ge(smk, hd * 16 + si + 1)
                    for h in range(2):
                        last = te.matmul(psO5[hk % 2][:, h * 512:(h + 1) * 512], lhs,
                                         att[:, si * T + h * 512:si * T + (h + 1) * 512],
                                         start=(si == 0), stop=(si == 15))
                last.then_inc(omm, 1)
    if E == "vector":
        v = e.vector
        ctx.emit_fence(v)
        trcs = ctx.sem("p5trcs")
        for tt in range(8):
            if tt >= 1:
                v.wait_ge(ctk, 16 * tt)
            for kq in range(8):
                v.wait_ge(trm5, tt * 32 + kq * 4 + 4)
                v.tensor_copy(stg[:, kq * 512:(kq + 1) * 512],
                              psTT[:]).then_inc(trcs, 1)
        EG5 = lambda inst: ctx.edge(v, inst, "p5ve")
        v.wait_ge(bvl, 16)
        for hd in range(4):
            last = v.reduce_sum(al5[0:1, 72 + hd:73 + hd],
                                row5[0:1, hd * 1024:(hd + 1) * 1024],
                                axis=AX, apply_absolute_value=True)
        EG5(last)
        EG5(v.tensor_scalar_mul(al5[0:1, 72:76], al5[0:1, 72:76], 1.0 / 1024))
        v.wait_ge(bvm, 1)
        for c_ in range(8):
            last = v.reduce_sum(al5[0:1, 64 + c_:65 + c_], psBC[0:1, c_ * 8:(c_ + 1) * 8], axis=AX)
        EG5(last)
        EG5(v.tensor_scalar_mul(al5[0:1, 64:72], al5[0:1, 64:72], 1.0 / 1024))
        last = None
        for c_ in range(8):
            last = v.tensor_single_scalar(al5[0:1, c_ * 8:(c_ + 1) * 8],
                                          ctx.t["onesf"][0:1, 0:8],
                                          al5[0:1, 64 + c_:65 + c_], ALU.mult)
        last.then_inc(bvv_, 1)
        v.wait_ge(bvs, 2)
        v.wait_ge(bvm, 2)
        EG5(v.tensor_copy(abc[:, 0:64], psBC[:, 0:64]))
        v.tensor_tensor(qbp[:, 0:64], sgp[:, 0:64], abc[:, 0:64], ALU.mult).then_inc(bvv_, 1)
        v.wait_ge(bvs, 3)
        for hd in range(4):
            seg = sgr[0:1, hd * 1024:(hd + 1) * 1024]
            last = v.tensor_single_scalar(seg, seg, al5[0:1, 72 + hd:73 + hd], ALU.mult)
        last.then_inc(bvv_, 1)
        for hd in range(4):
            v.wait_ge(bvm, hd + 3)
            v.tensor_copy(vbc[:, hd * 1024:(hd + 1) * 1024], psBC[:, 0:1024]).then_inc(bvc, 1)
        for hd in range(4):
            v.wait_ge(ctx.sem("p5qlc"), 16 * (hd + 1))
            for hk in range(8):
                v.tensor_single_scalar(Qsb[:, hk * T:(hk + 1) * T],
                                       Qsb[:, hk * T:(hk + 1) * T],
                                       qbp[:, hd * 8 + hk:hd * 8 + hk + 1],
                                       ALU.mult).then_inc(qmv, 1)
            v.wait_ge(klc, 32 * (hd + 1))
            for hk in range(8):
                v.tensor_single_scalar(Ksb[:, hk * 2048:(hk + 1) * 2048],
                                       Ksb[:, hk * 2048:(hk + 1) * 2048],
                                       qbp[:, 32 + hd * 8 + hk:33 + hd * 8 + hk],
                                       ALU.mult).then_inc(kmv, 1)
            v.wait_ge(vlc, 16 * 16 * (hd + 1))
            v.wait_ge(bvc, hd + 1)
            for si in range(16):
                v.tensor_tensor(Vsb[:, si * T:(si + 1) * T], Vsb[:, si * T:(si + 1) * T],
                                vbc[:, hd * 1024:(hd + 1) * 1024], ALU.mult).then_inc(vmv, 1)
            for si in range(16):
                u = hd * 16 + si
                v.wait_ge(sev, u + 1)
                v.wait_ge(mls[u % 2], 16 * (u // 2 + 1))
                v.tensor_tensor(att[:, si * T:(si + 1) * T], att[:, si * T:(si + 1) * T],
                                msk[:, (u % 2) * T:(u % 2 + 1) * T], ALU.mult).then_inc(smk, 1)
            for hk in range(8):
                u = hd * 8 + hk
                v.wait_ge(oev, u + 1)
                v.wait_ge(xl5[u % 2], 16 * (u // 2 + 1))
                if u >= 2:
                    v.wait_ge(so5[u % 2], 16 * (u // 2))
                v.tensor_tensor(xo5[:, (u % 2) * T:(u % 2 + 1) * T],
                                xin5[:, (u % 2) * T:(u % 2 + 1) * T],
                                tmp5[:, (u % 2) * T:(u % 2 + 1) * T],
                                ALU.add).then_inc(oxv, 1)
    return {"done": [(so5[0], 16 * 16), (so5[1], 16 * 16), (ctx.sem("p5oxv"), 32),
                     (ctx.sem("p5omm"), 32), (ctx.sem("p5gct"), 1)]}



class Carver:
    def __init__(self, nc):
        total = nc.sbuf_bytes_remaining - 2048
        self.nbytes = (total // 512) * 512
        self.h = nc.alloc_sbuf_tensor("arena", [128, self.nbytes], mybir.dt.uint8)
        self.f32 = self.h.ap().bitcast(F32)
        self.bf = self.h.ap().bitcast(BF16)
        self.off = 0

    def take(self, m, name, cols, dtype, rows=128):
        sz = 4 if dtype == F32 else 2
        self.off = (self.off + 63) // 64 * 64
        e0 = self.off // sz
        m[name] = (self.f32 if dtype == F32 else self.bf)[0:rows, e0:e0 + cols]
        self.off += cols * sz
        assert self.off <= self.nbytes, (name, self.off, self.nbytes)

    def row(self, m, name, cols, dtype, part):
        sz = 4 if dtype == F32 else 2
        self.off = (self.off + 63) // 64 * 64
        e0 = self.off // sz
        m[name] = (self.f32 if dtype == F32 else self.bf)[part:part + 1, e0:e0 + cols]
        self.off += cols * sz


def carve_persistent(cv, m):
    cv.take(m, "onesb", 16, BF16)
    cv.take(m, "onesf", 128, F32)
    cv.take(m, "thr", 16, F32)
    cv.take(m, "epsv", 16, F32)
    cv.take(m, "gsv_t", 128, F32)
    cv.take(m, "usv_t", 128, F32)
    cv.take(m, "dsv_t", 32, F32)
    cv.take(m, "dsh", 32, F32)
    cv.take(m, "bin2_t", 16, F32)
    for nm in ("ln1w_t", "ln1b_t", "pzre_t", "pzim_t", "lcre_t", "lcim_t",
               "sclv_t", "ln2w_t", "ln2b_t", "b2v_t"):
        cv.take(m, nm, 16, F32)
    cv.take(m, "b1v_t", 64, F32)
    cv.take(m, "lnlw_t", 16, F32)
    cv.take(m, "lnlb_t", 16, F32)
    cv.take(m, "boutv_t", 32, F32)
    cv.take(m, "selv_t", 16, F32)
    cv.take(m, "rw_sb", 2048, BF16)
    cv.take(m, "eb_sb", 4096, BF16, rows=64)
    cv.take(m, "rb_sb", 16, F32, rows=64)
    cv.take(m, "idt", 128, BF16)
    cv.take(m, "p5qkp", 64, F32)
    cv.take(m, "p5sgp", 64, F32)
    cv.take(m, "p5abc", 64, F32)
    cv.take(m, "p5qb", 64, F32)
    cv.take(m, "p5absq", 64, BF16)
    cv.row(m, "p5al", 80, F32, 0)
    cv.take(m, "p0acc", 4, F32)
    cv.take(m, "p0accw", 192, F32)
    cv.row(m, "p0row", 8, F32, 0)


def carve_ln(cv, m):
    cv.take(m, "lnxld", 2 * T, F32)
    cv.take(m, "lnxb", 2 * T, BF16)
    cv.take(m, "lnsqb", 2 * T, BF16)
    cv.take(m, "lntmpf", 2 * T, F32)
    cv.take(m, "lnrbc", T, F32)
    cv.take(m, "lnmrbc", T, F32)
    cv.take(m, "lnrows", 4 * T, F32)


def carve_phase(kind, cv, m, quantize):
    base = cv.off
    cv.take(m, "xn", NK * T, BF16)
    ln_start = cv.off
    carve_ln(cv, m)
    ln_end = cv.off
    if kind in ("p01", "p3"):
        cv.off = ln_start
        if quantize:
            cv.take(m, "wst", 4 * 2048, F32)
            cv.take(m, "mbb", 2048, BF16)
            cv.take(m, "mb2", 2048, BF16)
        cv.take(m, "gq", 2 * 4096, BF16)
        cv.take(m, "uq", 2 * 4096, BF16)
        cv.take(m, "sgs", 2 * T, BF16)
        cv.take(m, "ups", 2 * T, BF16)
        cv.take(m, "hbf", 2 * T, BF16)
        s1_end = cv.off
        cv.off = ln_start
        ns2 = 2
        cv.take(m, "hb2", ns2 * 512, BF16)
        if quantize:
            cv.take(m, "dwf", 2 * 1024, F32)
            cv.take(m, "mbd", 1024, BF16)
            cv.take(m, "mb2d", 1024, BF16)
        cv.take(m, "dq", ns2 * 1024, BF16)
        cv.take(m, "xin2", 8 * 512, F32)
        cv.take(m, "tmp2", 2 * 512, F32)
        cv.take(m, "xo", 2 * 512, F32)
        cv.off = max(cv.off, s1_end, ln_end)
        if quantize:
            cv.take(m, "p0wbuf", 2 * 2048, F32)
    elif kind == "p2":
        cv.off = base
        cv.take(m, "p2y", 4 * T, F32)
        cv.take(m, "p2cA", 4 * T, F32)
        cv.take(m, "p2iA", 4 * T, F32)
        cv.take(m, "p2iB", 4 * T, F32)
        scan_end = cv.off
        yb, ca, ia, ib = m["p2y"], m["p2cA"], m["p2iA"], m["p2iB"]
        m["p2tm"] = yb.bitcast(F32)[:, 0:2 * T] if False else m["p2y"][:, 0:2 * T]
        m["p2xin"] = m["p2y"][:, 2 * T:4 * T]
        bb = m["p2cA"].bitcast(BF16)
        m["p2z"] = None
        e0 = m["p2cA"].offset * 2
        m["p2z"] = cv.bf[:, e0:e0 + 16 * T]
        m["p2wout"] = cv.bf[:, e0:e0 + 16384]
        e0b = m["p2iB"].offset * 2
        m["p2hb"] = cv.bf[:, e0b:e0b + 4 * T]
        m["p2sqh"] = cv.bf[:, e0b + 4 * T:e0b + 8 * T]
        cv.off = ln_start
        cv.take(m, "p2povr", 4 * T, BF16)
        cv.take(m, "p2povi", 4 * T, BF16)
        assert cv.off <= ln_start + 16 * 1024 + 4096
        cv.off = ln_end
        cv.take(m, "p2y2", 4 * T, BF16)
        cv.take(m, "p2xo", 2 * T, F32)
        cv.take(m, "p2win", 16 * 512, BF16)
        m["p2w1"] = m["p2win"]
        cv.take(m, "p2w2", 16 * 512, BF16)
        cv.take(m, "p2h", 4 * T, F32)
        cv.take(m, "p2pw", 144, F32)
        cv.take(m, "p2sc", 64, F32)
        m["p2rbc"] = m["lnrbc"]
        m["p2mrbc"] = m["lnmrbc"]
        m["p2rows"] = m["lnrows"]
    elif kind in ("p4", "p6"):
        cv.off = ln_start
        cv.take(m, "p4lgb", T, BF16, rows=64)
        cv.take(m, "p4smf", 8 * 64, F32)
        cv.take(m, "p4wkf", 8 * 64, F32)
        cv.take(m, "p4wdf4", 8 * 64, F32)
        cv.take(m, "p4wdb", 8 * 64, BF16)
        cv.take(m, "p4wdT", T, BF16, rows=64)
        cv.take(m, "p4cols", 8, F32)
        cv.row(m, "p4wrow", T, F32, 0)
        cv.take(m, "p4hws", T, F32)
        cv.take(m, "p4tmp4", 2 * T, F32)
        cv.take(m, "p4xin4", 2 * T, F32)
        cv.take(m, "p4xo4", 2 * T, F32)
        cv.off = max(cv.off, ln_end)
    elif kind == "p5":
        cv.off = ln_start
        cv.take(m, "p5stg", 4096, BF16)
        cv.row(m, "p5row", 4096, F32, 0)
        cv.row(m, "p5sgr", 4096, F32, 0)
        pre_end = cv.off
        cv.off = ln_start
        cv.take(m, "p5msk", 2 * T, BF16)
        cv.take(m, "p5tmp", 2 * T, F32)
        cv.take(m, "p5xin", 2 * T, F32)
        cv.take(m, "p5xo", 2 * T, F32)
        assert cv.off <= ln_end and pre_end <= ln_end
        cv.off = base
        cv.take(m, "p5K", 8 * 2048, BF16)
        cv.take(m, "p5V", 16 * T, BF16)
        assert cv.off <= ln_start + 2048
        cv.off = ln_end
        cv.take(m, "p5att", 16 * T, BF16)
        cv.take(m, "p5Q", 8 * T, BF16)
        cv.take(m, "p5vbc", 4 * T, BF16)
    print(f"  carve {kind}: peak {cv.off} / {cv.nbytes}")


GLOBAL_SEMS = ("prm", "ini", "pbar", "bnd")


class BuildCtx(Ctx):
    def __init__(self, nc, stack, npool=64):
        super().__init__(nc, stack)
        self.pool = [stack.enter_context(nc.semaphore(f"sp{i}")) for i in range(npool)]
        self.gsem = {g: stack.enter_context(nc.semaphore(g)) for g in GLOBAL_SEMS}
        self.phase_map = {}
        self.rec = None
        self.known = None

    def begin_phase(self, known_names):
        self.ecnt = {}
        self.known = sorted(set(known_names))
        assert len(self.known) <= len(self.pool), len(self.known)
        self.phase_map = {nm: self.pool[i] for i, nm in enumerate(self.known)}

    def sem(self, name):
        if name in self.gsem:
            return self.gsem[name]
        if self.phase_map is None or name not in self.phase_map:
            if self.known is None:
                self.collect_names.add(name)
                return self.pool[len(self.collect_names) % len(self.pool)]
            raise KeyError(f"sem {name} not registered for this phase")
        return self.phase_map[name]


class RecEng:

    def __init__(self, real, rec):
        self._real = real
        self._rec = rec

    def __getattr__(self, name):
        attr = getattr(self._real, name)
        if not callable(attr):
            return attr
        rec = self._rec

        def call(*a, **k):
            r = attr(*a, **k)
            if hasattr(r, "then_inc"):
                return RecInst(r, rec)
            return r
        return call


class RecInst:
    def __init__(self, real, rec):
        self._real = real
        self._rec = rec

    def then_inc(self, sem, n=1):
        self._rec[sem.name] = self._rec.get(sem.name, 0) + n
        r = self._real.then_inc(sem, n)
        return RecInst(r, self._rec) if hasattr(r, "then_inc") else r

    def __getattr__(self, name):
        return getattr(self._real, name)


def collect_phase_sems(kind, opts):
    nc = bass.Bass("TRN2", target_bir_lowering=False, debug=False)
    stack = ExitStack()
    with stack:
        ctx = BuildCtx(nc, stack, npool=8)
        ctx.known = None
        ctx.collect_names = set()
        ctx.phase_map = None
        cv = Carver(nc)
        m = {}
        carve_persistent(cv, m)
        mark = cv.off
        carve_phase(kind, cv, m, opts.get("quantize", True))
        declare_psum(ctx, nc, m)
        declare_dram(nc, m, [kind], opts, fused=False)
        ctx.t = m
        xi, xo = m["d_xin"], m["d_xout"]
        for engname in ENGS:
            eng = getattr(nc, "sync" if engname == "sync" else engname)
            setattr(ctx.e, engname, eng)
            try:
                emit_kind(ctx, kind, engname, xi, xo, opts)
            except KeyError:
                raise
        return set(ctx.collect_names)


def declare_psum(ctx, nc, m):
    before = set(m)
    pools = {}
    for nm in ("poolA", "poolB", "poolC", "poolD"):
        h = nc.alloc_psum_tensor(nm, [128, 1024], F32)
        pools[nm] = h
        m[nm] = h.ap()
    pA, pB, pC, pD = (m["poolA"], m["poolB"], m["poolC"], m["poolD"])
    m["p0ps"] = pA[:, 0:8]
    m["lnpsS"] = pA[0:1, :]
    m["lnpsQ"] = pB[0:1, :]
    m["lnpsB1"] = pC
    m["lnpsB2"] = pD
    m["psG0"], m["psG1"] = pA, pB
    m["psU0"], m["psU1"] = pC, pD
    for kg in range(8):
        pool = (pA, pB, pC, pD)[kg // 2]
        m[f"psD{kg}"] = pool[:, (kg % 2) * 512:(kg % 2 + 1) * 512]
    m["p2psA"], m["p2psB"] = pA, pB
    m["p2s_psS"] = pC[0:1, :]
    m["p2s_psQ"] = pD[0:1, :]
    bA = pools["poolA"].ap().bitcast(BF16)
    bC = pools["poolC"].ap().bitcast(BF16)
    bD = pools["poolD"].ap().bitcast(BF16)
    m["p4psL"] = pA[0:64, :]
    m["p4psT"] = bC[:, 0:512]
    m["p4psTb"] = bD[0:64, 0:128]
    m["p4psR"] = pB[0:1, :]
    m["p4psW"] = pC
    m["p4psM0"], m["p4psM1"] = pA, pB
    m["p5psTT"] = bA[:, 0:512]
    m["p5psBC"] = pB
    m["p5psS0"], m["p5psS1"] = pA, pB
    m["p5psO0"], m["p5psO1"] = pC, pD
    return [k for k in m if k not in before]


NEEDS = {
    "p01": ["x", "wg", "wu", "wd", "wslgu", "wsld", "gsv", "usv", "dsv"],
    "p3": ["x", "gsv", "usv", "dsv"],
    "p2": ["x", "winT", "w1T", "w2T", "woutT", "bin2", "ln1w", "ln1b", "pzre",
           "pzim", "lcre", "lcim", "sclv", "ln2w", "ln2b", "b1v", "b2v",
           "lnlw", "lnlb", "boutv", "selv"],
    "p4": ["x", "rwT", "ebw", "rbv", "ident"],
    "p6": ["x", "rwT", "ebw", "rbv", "ident"],
    "p5": ["x", "bvq", "bvk", "bvv", "bvqp", "bvkp", "mask", "ident"],
}
SHAPES = {
    "wg": ([NFC, 128, NK * 128], F32), "wu": ([NFC, 128, NK * 128], F32),
    "wd": ([4, NFC, 128, 1024], F32),
    "wslgu": ([32, 128, 4096], F32), "wsld": ([64, 128, 1024], F32),
    "gsv": ([128, 128], F32), "usv": ([128, 128], F32), "dsv": ([128, 32], F32),
    "winT": ([4096, 512], F32), "w1T": ([4, 128, 8192], F32),
    "w2T": ([4, 128, 8192], F32), "woutT": ([128, 16384], F32),
    "bin2": ([128, 4], F32), "ln1w": ([128, 16], F32), "ln1b": ([128, 16], F32),
    "pzre": ([128, 16], F32), "pzim": ([128, 16], F32), "lcre": ([128, 16], F32),
    "lcim": ([128, 16], F32), "sclv": ([128, 16], F32), "ln2w": ([128, 16], F32),
    "ln2b": ([128, 16], F32), "b1v": ([128, 64], F32), "b2v": ([128, 16], F32),
    "lnlw": ([128, 4], F32), "lnlb": ([128, 4], F32), "boutv": ([128, 32], F32),
    "selv": ([128, 3], F32),
    "rwT": ([4096, 64], F32), "ebw": ([64, 4096], F32), "rbv": ([64, 1], F32),
    "bvq": ([1, 4096], F32), "bvk": ([1, 4096], F32), "bvv": ([1, 4096], F32),
    "bvqp": ([128, 32], F32), "bvkp": ([128, 32], F32),
    "mask": ([2048, 1024], BF16), "ident": ([128, 128], BF16),
}
PARAM_LOADS = [
    ("gsv", "gsv_t"), ("usv", "usv_t"), ("dsv", "dsv_t"),
    ("bin2", "bin2_t"), ("ln1w", "ln1w_t"), ("ln1b", "ln1b_t"),
    ("pzre", "pzre_t"), ("pzim", "pzim_t"), ("lcre", "lcre_t"),
    ("lcim", "lcim_t"), ("sclv", "sclv_t"), ("ln2w", "ln2w_t"),
    ("ln2b", "ln2b_t"), ("b1v", "b1v_t"), ("b2v", "b2v_t"), ("lnlw", "lnlw_t"),
    ("lnlb", "lnlb_t"), ("boutv", "boutv_t"), ("selv", "selv_t"),
    ("ident", "idt"),
]


def declare_dram(nc, m, kinds, opts, fused):
    need = set()
    for k in kinds:
        need.update(NEEDS[k])
    for nm in sorted(need):
        if nm == "x":
            continue
        shp, dt = SHAPES[nm]
        m["d_" + nm] = nc.dram_tensor(nm, shp, dt, kind="ExternalInput").ap()
    m["d_xext"] = nc.dram_tensor("x", [D, T], F32, kind="ExternalInput").ap()
    m["d_y"] = nc.dram_tensor("y", [D, T], F32, kind="ExternalOutput").ap()
    if fused:
        m["d_xstate"] = nc.dram_tensor("xstate", [D, T], F32).ap()
    m["d_xin"] = m["d_xext"]
    m["d_xout"] = m["d_y"]
    dbg = "ExternalOutput" if DEBUG_OUTS else "Internal"
    if any(k in ("p01", "p3") for k in kinds):
        m["d_h"] = nc.dram_tensor("hstate", [NFC, 128, T], BF16, kind=dbg).ap()
        m["d_wqg"] = nc.dram_tensor("wqg", [NFC, 128, NK * 128], BF16, kind=dbg).ap()
        m["d_wqu"] = nc.dram_tensor("wqu", [NFC, 128, NK * 128], BF16).ap()
        m["d_wqd"] = nc.dram_tensor("wqd", [4, NFC, 128, 1024], BF16).ap()
        m["d_thr_in"] = nc.dram_tensor("thr_in", [1, 8], F32).ap()
        m["d_thr_out"] = nc.dram_tensor("thr_out", [1, 8], F32).ap()
    if "p2" in kinds:
        m["d_sc_in"] = nc.dram_tensor("sc_in", [128, 8], F32).ap()
        m["d_sc_out"] = nc.dram_tensor("sc_out", [256, 8], F32).ap()
    if "p5" in kinds:
        m["d_gin_fm"] = nc.dram_tensor("gin_fm", [D, T], BF16).ap()
        m["d_g_fm"] = nc.dram_tensor("g_fm", [2 * D, T], BF16).ap()
        m["d_gin_tok"] = nc.dram_tensor("gin_tok", [T, D], BF16).ap()
        m["d_g_tok"] = nc.dram_tensor("g_tok", [2 * T, D], BF16).ap()


def emit_kind(ctx, kind, eng, xi, xo, opts):
    if kind == "p01":
        emit_p0(ctx, eng)
        emit_ln(ctx, "ln", eng, xi, ctx.t["xn"])
        emit_ffn_s1(ctx, "", eng, True, opts.get("save", True))
        emit_ffn_s2(ctx, "", eng, True, opts.get("save", True), xi, xo)
    elif kind == "p3":
        emit_ln(ctx, "ln", eng, xi, ctx.t["xn"])
        emit_ffn_s1(ctx, "", eng, False, False)
        emit_ffn_s2(ctx, "", eng, False, False, xi, xo)
    elif kind == "p2":
        emit_p2(ctx, eng, xi, xo)
    elif kind in ("p4", "p6"):
        emit_p4(ctx, eng, xi, xo, opts.get("load_weights", True))
    elif kind == "p5":
        emit_p5(ctx, eng, xi, xo)


_SEM_CACHE = {}
_LAST_MAPS = None
DEBUG_OUTS = False


def build_program(kinds_opts, fused):
    for kind, opts in kinds_opts:
        key = (kind, tuple(sorted(opts.items())))
        if key not in _SEM_CACHE:
            _SEM_CACHE[key] = collect_phase_sems(kind, dict(opts))
    nc = bass.Bass("TRN2", target_bir_lowering=False, debug=False)
    stack = ExitStack()
    ctx = BuildCtx(nc, stack)
    cv = Carver(nc)
    pm = {}
    carve_persistent(cv, pm)
    mark = cv.off
    maps = []
    for kind, opts in kinds_opts:
        m = dict(pm)
        cv.off = mark
        carve_phase(kind, cv, m, opts.get("quantize", kind == "p01"))
        maps.append(m)
    global _LAST_MAPS
    _LAST_MAPS = maps
    pskeys = declare_psum(ctx, nc, pm)
    for m in maps:
        for k2 in pskeys:
            m[k2] = pm[k2]
    dm = {}
    declare_dram(nc, dm, [k for k, _ in kinds_opts], {}, fused)
    for m in maps:
        m.update(dm)
    input_names = sorted(set(nm for k, _ in kinds_opts for nm in NEEDS[k]))
    prm_total = 16 * len([1 for nm, _t in PARAM_LOADS if "d_" + nm in dm])
    ini_total = 4
    nph = len(kinds_opts)
    chain = []
    for i in range(nph):
        xi = dm["d_xext"] if i == 0 else dm["d_xstate"] if fused else dm["d_xext"]
        xo = dm["d_y"] if i == nph - 1 else dm["d_xstate"] if fused else dm["d_y"]
        chain.append((xi, xo))
    finals = [dict() for _ in range(nph)]
    pbar = ctx.gsem["pbar"]
    prm = ctx.gsem["prm"]
    ini = ctx.gsem["ini"]
    bnd = ctx.gsem["bnd"]
    pool_nums = sorted(h.num for h in ctx.pool)
    pool_range = range(pool_nums[0], pool_nums[-1] + 1)
    assert pool_nums == list(pool_range)

    def emit_engine(engname, eng):
        for pi, (kind, opts) in enumerate(kinds_opts):
            key = (kind, tuple(sorted(opts.items())))
            ctx.begin_phase(_SEM_CACHE[key])
            ctx.t = maps[pi]
            ctx.rec = finals[pi]
            reng = RecEng(eng, finals[pi])
            setattr(ctx.e, engname, reng)
            if pi == 0:
                eng.wait_ge(prm, prm_total)
                eng.wait_ge(ini, ini_total)
            else:
                if engname == "sync":
                    handles = {h.name: h for h in ctx.pool}
                    for snm, val in sorted(finals[pi - 1].items()):
                        if snm in handles:
                            eng.wait_ge(handles[snm], val)
                    eng.sem_inc(bnd, 1)
                    eng.wait_ge(pbar, pi)
                elif engname == "gpsimd":
                    eng.wait_ge(bnd, pi)
                    eng.dma_reset(pool_range)
                    eng.sem_clear(pool_range)
                    eng.sem_inc(pbar, 1)
                else:
                    eng.wait_ge(pbar, pi)
            xi, xo = chain[pi]
            emit_kind(ctx, kind, engname, xi, xo, dict(opts))
        if engname == "sync":
            for snm, val in sorted(finals[nph - 1].items()):
                for h in ctx.pool:
                    if h.name == snm:
                        eng.wait_ge(h, val)

    with nc.Block() as block:
        @block.tensor
        def _(te):
            emit_engine("tensor", te)

        @block.vector
        def _(v):
            v.memset(pm["onesb"][:, 0:1], 1.0).then_inc(ini, 1)
            v.memset(pm["epsv"][:], EPS).then_inc(ini, 1)
            v.memset(pm["onesf"][:], 1.0).then_inc(ini, 1)
            v.wait_ge(prm, prm_total)
            if "d_dsv" in dm:
                v.tensor_scalar_mul(pm["dsh"][:, 0:32], pm["dsv_t"][:, 0:32],
                                    0.5).then_inc(ini, 1)
            else:
                v.memset(pm["dsh"][:, 0:32], 0.0).then_inc(ini, 1)
            emit_engine("vector", v)

        @block.scalar
        def _(sc):
            emit_engine("scalar", sc)

        @block.gpsimd
        def _(g):
            for nm, tgt in PARAM_LOADS:
                if "d_" + nm in dm:
                    shp = SHAPES[nm][0]
                    src = dm["d_" + nm]
                    dst = pm[tgt][0:shp[0], 0:shp[1]]
                    g.dma_start(out=dst, in_=src).then_inc(prm, 16)
            emit_engine("gpsimd", g)

        @block.sync
        def _(sy):
            emit_engine("sync", sy)

    return nc, input_names, pm



UNFUSED = False
PERCORE_NAMES = {"x", "wslgu", "wsld", "mask", "selv"}
_PROG_CACHE = {}


def _get_prog(spec, fused):
    key = (tuple((k, o) for k, o in spec), fused)
    if key not in _PROG_CACHE:
        _PROG_CACHE[key] = build_program([(k, dict(o)) for k, o in spec], fused)
    return _PROG_CACHE[key]


UNFUSED_LAUNCHES = [
    [("p01", (("save", True),))],
    [("p2", ())],
    [("p01", (("save", True),))],
    [("p4", (("load_weights", True),))],
    [("p5", ())],
    [("p4", (("load_weights", True),))],
]
FUSED_SPEC = [
    ("p01", (("save", True),)),
    ("p2", ()),
    ("p3", (("quantize", False),)),
    ("p4", (("load_weights", True),)),
    ("p5", ()),
    ("p6", (("load_weights", False),)),
]


def kernel(**inputs):
    sh, per_core = prep_host(inputs)
    xs = [pc["x"] for pc in per_core]
    if UNFUSED:
        for spec in UNFUSED_LAUNCHES:
            nc, names, _pm = _get_prog(tuple(spec), False)
            in_maps = []
            for c in range(8):
                im = {}
                for nm in names:
                    if nm == "x":
                        im["x"] = xs[c]
                    elif nm in PERCORE_NAMES:
                        im[nm] = per_core[c][nm]
                    else:
                        im[nm] = sh[nm]
                in_maps.append(im)
            res = run_bass_kernel_spmd(nc, in_maps, core_ids=list(range(8)))
            xs = [np.asarray(res.results[c]["y"]) for c in range(8)]
    else:
        nc, names, _pm = _get_prog(tuple(FUSED_SPEC), True)
        in_maps = []
        for c in range(8):
            im = {}
            for nm in names:
                if nm == "x":
                    im["x"] = xs[c]
                elif nm in PERCORE_NAMES:
                    im[nm] = per_core[c][nm]
                else:
                    im[nm] = sh[nm]
            in_maps.append(im)
        res = run_bass_kernel_spmd(nc, in_maps, core_ids=list(range(8)))
        xs = [np.asarray(res.results[c]["y"]) for c in range(8)]
    out = np.empty((4, 2048, 4096), np.float32)
    for c in range(8):
        b, j = c // 2, c % 2
        out[b, j * T:(j + 1) * T, :] = xs[c].T
    return out
